# version 18
# baseline (speedup 1.0000x reference)
from contextlib import ExitStack
import numpy as np
import ml_dtypes

import concourse.bass as bass
import concourse.mybir as mybir
from concourse.bass_utils import run_bass_kernel_spmd

F32 = mybir.dt.float32
BF16 = mybir.dt.bfloat16
AF = mybir.ActivationFunctionType
OP = mybir.AluOpType
AX = mybir.AxisListType.X
BFD = ml_dtypes.bfloat16

D = 4096
S = 4096
TOWN = 2048
NH = 32
NKV = 8
HD = 128
NSLOT = 16
RSQ = float(np.float32(1.0) / np.float32(np.sqrt(np.float32(128.0))))
EPS = 1e-5
SWAP_MASK = [x for i in range(16) for x in (2 * i + 1, 2 * i)]


def build_nc():
    nc = bass.Bass("TRN2", target_bir_lowering=False)

    def inp(name, shape, dt=F32):
        return nc.dram_tensor(name, shape, dt, kind="ExternalInput").ap()

    x_own = inp("x_own", [TOWN, D])
    WQ = inp("WQ", [128, 32, 32, 128])
    WK = inp("WK", [128, 8, 32, 128])
    WV = inp("WV", [128, 8, 32, 128])
    WO = inp("WO", [128, 8, 32, 512])
    BW = inp("BW", [128, 8, 32, 512])
    BST = inp("BST", [128, 8, 8, 512])
    ART = inp("ART", [128, 32, 80])
    bias_rep = inp("bias_rep", [128, D])
    COWN = inp("COWN", [128, TOWN])
    SOWN = inp("SOWN", [128, TOWN])
    MSK = inp("MSK", [2, 128, 128], BF16)
    ones_col_bf = inp("ones_col_bf", [128, 1], BF16)
    rmask0 = inp("rmask0", [128, 1])
    rmask1 = inp("rmask1", [128, 1])
    ones_row_f32 = inp("ones_row_f32", [1, 128])
    epscol_in = inp("epscol", [128, 1])

    xn_own_h = nc.dram_tensor("xn_own_h", [TOWN, D], BF16).ap()
    KrTo_h = nc.dram_tensor("KrTo_h", [2 * NKV * 128, TOWN], BF16).ap()
    VTo_h = nc.dram_tensor("VTo_h", [2 * NKV * 128, TOWN], BF16).ap()
    KrTg_h = nc.dram_tensor("KrTg_h", [2 * NKV * 128, TOWN], BF16).ap()
    VTg_h = nc.dram_tensor("VTg_h", [2 * NKV * 128, TOWN], BF16).ap()
    kn_h = nc.dram_tensor("kn_h", [1, 8], F32).ap()
    knr_h = nc.dram_tensor("knr_h", [1, 8], F32).ap()
    knr2_h = nc.dram_tensor("knr2_h", [1, 8], F32).ap()
    Q_h = nc.dram_tensor("Q_h", [NH, 128, TOWN], BF16).ap()
    Wgt_h = nc.dram_tensor("Wgt_h", [TOWN, 1024], BF16).ap()
    G_h = nc.dram_tensor("G_h", [TOWN, D], F32).ap()
    AVT_h = nc.dram_tensor("AVT_h", [NH, 128, TOWN], BF16).ap()

    out_own = nc.dram_tensor("out_own", [TOWN, D], F32, kind="ExternalOutput").ap()

    CNT = dict(ld=0, st=0, g=0, t=0, a=0, v=0, cc=0)

    es = ExitStack()
    with es:
        SEM = {k: es.enter_context(nc.semaphore(k)) for k in CNT}

        def incd(instr, s):
            CNT[s] += 16
            instr.then_inc(SEM[s], 16)
            return CNT[s]

        def inc1(instr, s):
            CNT[s] += 1
            instr.then_inc(SEM[s], 1)
            return CNT[s]

        s_eps = es.enter_context(nc.sbuf_tensor("s_eps", [128, 1], F32))
        s_ones_bf = es.enter_context(nc.sbuf_tensor("s_ones_bf", [128, 1], BF16))
        s_ones_f = es.enter_context(nc.sbuf_tensor("s_ones_f", [1, 128], F32))
        s_msk = es.enter_context(nc.sbuf_tensor("s_msk", [128, 256], BF16))
        s_rm0 = es.enter_context(nc.sbuf_tensor("s_rm0", [128, 1], F32))
        s_rm1 = es.enter_context(nc.sbuf_tensor("s_rm1", [128, 1], F32))
        s_kn2 = es.enter_context(nc.sbuf_tensor("s_kn2", [1, 8], F32))
        s_qn2 = es.enter_context(nc.sbuf_tensor("s_qn2", [1, 32], F32))
        s_nmall = es.enter_context(nc.sbuf_tensor("s_nmall", [128, 8], F32))

        with nc.Block() as blk:
            @blk.sync
            def _(sync):
                incd(sync.dma_start(out=s_eps[:], in_=epscol_in), "ld")
                incd(sync.dma_start(out=s_ones_bf[:], in_=ones_col_bf), "ld")
                incd(sync.dma_start(out=s_ones_f[:], in_=ones_row_f32), "ld")
                incd(sync.dma_start(
                    out=s_msk[:].rearrange("p (m q) -> p m q", m=2),
                    in_=MSK.rearrange("m p q -> p m q")), "ld")
                incd(sync.dma_start(out=s_rm0[:], in_=rmask0), "ld")
                incd(sync.dma_start(out=s_rm1[:], in_=rmask1), "ld")
                sync.wait_ge(SEM["ld"], CNT["ld"])

            @blk.vector
            def _(vector):
                vector.memset(s_kn2[:], 0.0)
                inc1(vector.memset(s_qn2[:], 0.0), "v")

        NT0 = 16

        def p0_src(i):
            return (x_own[i * 128:(i + 1) * 128, :],
                    xn_own_h[i * 128:(i + 1) * 128, :])

        ld0 = CNT["ld"]; st0 = CNT["st"]; a0 = CNT["a"]; v0 = CNT["v"]

        def a_rstd(i): return a0 + 2 * i + 1
        def a_id(i): return a0 + 2 * i + 2
        def v_var(i): return v0 + 2 * i + 1
        def v_nmr(i): return v0 + 2 * i + 2
        def ld_x(i): return ld0 + 16 * (i + 1)
        def st_xn(i): return st0 + 16 * (i + 1)

        with ExitStack() as ps:
            xb = [ps.enter_context(nc.sbuf_tensor(f"p0xb{i}", [128, D], F32)) for i in range(2)]
            xnb = [ps.enter_context(nc.sbuf_tensor(f"p0xn{i}", [128, D], BF16)) for i in range(2)]
            sstat = ps.enter_context(nc.sbuf_tensor("p0st", [128, 48], F32))
            sbn = ps.enter_context(nc.sbuf_tensor("p0bn", [128, 2], F32))
            _cols = [ps.enter_context(nc.sbuf_tensor(f"p0c{i}", [128, 1], F32))
                     for i in range(4)]
            lnc, rstdc, nmrc, spc0 = [c[:] for c in _cols]

            with nc.Block() as blk:
                @blk.sync
                def _(sync):
                    for i in range(2):
                        src, _dst = p0_src(i)
                        incd(sync.dma_start(out=xb[i % 2][:], in_=src), "ld")
                    for i in range(NT0):
                        _src, dst = p0_src(i)
                        sync.wait_ge(SEM["a"], a_id(i))
                        incd(sync.dma_start(out=dst, in_=xnb[i % 2][:]), "st")
                        if i + 2 < NT0:
                            src2, _ = p0_src(i + 2)
                            incd(sync.dma_start(out=xb[i % 2][:], in_=src2), "ld")
                    sync.wait_ge(SEM["st"], CNT["st"])

                @blk.vector
                def _(vector):
                    for i in range(NT0):
                        vector.wait_ge(SEM["ld"], ld_x(i))
                        for c in range(8):
                            vector.bn_stats(sstat[:, c * 6:(c + 1) * 6],
                                            xb[i % 2][:, c * 512:(c + 1) * 512])
                        vector.tensor_copy(spc0, s_eps[:])
                        vector.bn_aggr(sbn[:], sstat[:])
                        inc1(vector.drain(), "v")
                        vector.wait_ge(SEM["a"], a_rstd(i))
                        vector.tensor_scalar(nmrc, sbn[:, 0:1], rstdc, -1.0,
                                             OP.mult, OP.mult)
                        inc1(vector.drain(), "v")

                @blk.scalar
                def _(scalar):
                    for i in range(NT0):
                        scalar.wait_ge(SEM["ld"], ld_x(i))
                        if i >= 2:
                            scalar.wait_ge(SEM["st"], st_xn(i - 2))
                        scalar.wait_ge(SEM["v"], v_var(i))
                        scalar.activation(lnc, sbn[:, 1:2], AF.Ln, bias=s_eps[:])
                        inc1(scalar.activation(rstdc, lnc, AF.Exp, scale=-0.5), "a")
                        scalar.wait_ge(SEM["v"], v_nmr(i))
                        inc1(scalar.activation(xnb[i % 2][:], xb[i % 2][:], AF.Identity,
                                               scale=rstdc, bias=nmrc), "a")

        with ExitStack() as ps:
            wres = ps.enter_context(nc.sbuf_tensor("p1kw", [128, 8 * 32 * 128], BF16))
            xbuf = [ps.enter_context(nc.sbuf_tensor(f"p1kx{i}", [128, 32 * 512], BF16))
                    for i in range(2)]
            s_c = ps.enter_context(nc.sbuf_tensor("p1kc", [128, TOWN], F32))
            s_s = ps.enter_context(nc.sbuf_tensor("p1ks", [128, TOWN], F32))
            qbf = [ps.enter_context(nc.sbuf_tensor(f"p1kq{i}", [128, 512], BF16))
                   for i in range(4)]
            rot = [ps.enter_context(nc.sbuf_tensor(f"p1kr{i}", [128, 512], BF16))
                   for i in range(4)]
            qsq = [ps.enter_context(nc.sbuf_tensor(f"p1kqs{i}", [128, 512], BF16))
                   for i in range(2)]
            rA = [ps.enter_context(nc.sbuf_tensor(f"p1kra{i}", [128, 512], BF16))
                  for i in range(2)]
            rB = [ps.enter_context(nc.sbuf_tensor(f"p1krb{i}", [128, 512], BF16))
                  for i in range(2)]
            nrow = [ps.enter_context(nc.sbuf_tensor(f"p1kn{i}", [1, 512], F32))
                    for i in range(2)]
            vncol = ps.enter_context(nc.sbuf_tensor("p1kvn", [1, 1], F32))
            tmp1 = ps.enter_context(nc.sbuf_tensor("p1kt1", [128, 512], F32))
            tmp2 = ps.enter_context(nc.sbuf_tensor("p1kt2", [128, 512], F32))
            swp = ps.enter_context(nc.sbuf_tensor("p1ksw", [128, 512], BF16))
            pk = [ps.enter_context(nc.psum_tensor(f"p1kp{i}", [128, 512], F32))
                  for i in range(4)]
            pN = [ps.enter_context(nc.psum_tensor(f"p1kpn{i}", [1, 512], F32))
                  for i in range(2)]

            t_b = CNT["t"]; a_b = CNT["a"]; v_b = CNT["v"]; st_b = CNT["st"]
            g_b = CNT["g"]; ld_b = CNT["ld"]
            NU = 32
            T_CHAIN = {}; T_N = {}; A_EVQ = {}; A_EVN = {}
            V_ROT = {}; V_SQ = {}; V_NM = {}; ST = {}
            tc = t_b
            for u in range(NU):
                tc += 1; T_CHAIN[u] = tc
                if u >= 1:
                    tc += 1; T_N[u - 1] = tc
            tc += 1; T_N[NU - 1] = tc
            ac = a_b
            for u in range(NU):
                ac += 1; A_EVQ[u] = ac
                if u >= 1:
                    ac += 1; A_EVN[u - 1] = ac
            ac += 1; A_EVN[NU - 1] = ac
            vc = v_b
            for u in range(NU):
                vc += 1; V_ROT[u] = vc
                if u >= 1:
                    vc += 1; V_NM[u - 1] = vc
                vc += 1; V_SQ[u] = vc
            vc += 1; V_NM[NU - 1] = vc
            stc = st_b
            for u in range(NU):
                stc += 32; ST[u] = stc
            def ld_grp(g_): return ld_b + 32 + 16 * 32 * (g_ + 1)

            with nc.Block() as blk:
                @blk.sync
                def _(sync):
                    incd(sync.dma_start(out=s_c[:], in_=COWN), "ld")
                    incd(sync.dma_start(out=s_s[:], in_=SOWN), "ld")
                    for g_ in range(4):
                        if g_ >= 2:
                            sync.wait_ge(SEM["t"], T_CHAIN[(g_ - 2) * 8 + 7])
                        for k in range(32):
                            incd(sync.dma_start_transpose(
                                out=xbuf[g_ % 2][:, k * 512:(k + 1) * 512],
                                in_=xn_own_h[g_ * 512:(g_ + 1) * 512,
                                             k * 128:(k + 1) * 128]), "ld")
                    sync.wait_ge(SEM["ld"], CNT["ld"])

                @blk.gpsimd
                def _(gpsimd):
                    incd(gpsimd.dma_start(
                        out=wres[:].rearrange("p (f k c) -> p f k c", f=8, k=32),
                        in_=WK), "g")
                    for u in range(NU):
                        g_, f = divmod(u, 8)
                        gpsimd.wait_ge(SEM["v"], V_ROT[u])
                        if u >= 2:
                            gpsimd.wait_ge(SEM["st"], ST[u - 2])
                        gpsimd.tensor_scalar(rA[u % 2][:], rot[u % 4][:],
                                             s_rm0[:], None, OP.mult)
                        gpsimd.tensor_scalar(rB[u % 2][:], rot[u % 4][:],
                                             s_rm1[:], None, OP.mult)
                        incd(gpsimd.dma_start(
                            out=KrTo_h[f * 128:(f + 1) * 128,
                                       g_ * 512:(g_ + 1) * 512],
                            in_=rA[u % 2][:]), "st")
                        assert incd(gpsimd.dma_start(
                            out=KrTo_h[(NKV + f) * 128:(NKV + f + 1) * 128,
                                       g_ * 512:(g_ + 1) * 512],
                            in_=rB[u % 2][:]), "st") == ST[u]
                    gpsimd.wait_ge(SEM["st"], CNT["st"])
                g_cast = g_b + 16

                @blk.tensor
                def _(tensor):
                    tensor.wait_ge(SEM["g"], g_cast)
                    for u in range(NU):
                        g_, f = divmod(u, 8)
                        if f == 0:
                            tensor.wait_ge(SEM["ld"], ld_grp(g_))
                        if u >= 4:
                            tensor.wait_ge(SEM["a"], A_EVQ[u - 4])
                        for k in range(32):
                            inst = tensor.matmul(
                                pk[u % 4][:],
                                wres[:, (f * 32 + k) * 128:(f * 32 + k + 1) * 128],
                                xbuf[g_ % 2][:, k * 512:(k + 1) * 512],
                                start=(k == 0), stop=(k == 31))
                        assert inc1(inst, "t") == T_CHAIN[u]
                        if u >= 1:
                            up = u - 1
                            tensor.wait_ge(SEM["v"], V_SQ[up])
                            if up >= 2:
                                tensor.wait_ge(SEM["a"], A_EVN[up - 2])
                            assert inc1(tensor.matmul(pN[up % 2][:], s_ones_bf[:],
                                                      qsq[up % 2][:], start=True,
                                                      stop=True), "t") == T_N[up]
                    up = NU - 1
                    tensor.wait_ge(SEM["v"], V_SQ[up])
                    tensor.wait_ge(SEM["a"], A_EVN[up - 2])
                    assert inc1(tensor.matmul(pN[up % 2][:], s_ones_bf[:],
                                              qsq[up % 2][:], start=True,
                                              stop=True), "t") == T_N[up]

                @blk.scalar
                def _(scalar):
                    def evq(u):
                        scalar.wait_ge(SEM["t"], T_CHAIN[u])
                        if u >= 4:
                            scalar.wait_ge(SEM["v"], V_ROT[u - 4])
                        assert inc1(scalar.copy(qbf[u % 4][:], pk[u % 4][:]),
                                    "a") == A_EVQ[u]

                    def evn(u):
                        scalar.wait_ge(SEM["t"], T_N[u])
                        if u >= 2:
                            scalar.wait_ge(SEM["v"], V_NM[u - 2])
                        assert inc1(scalar.copy(nrow[u % 2][:], pN[u % 2][:]),
                                    "a") == A_EVN[u]

                    for u in range(NU):
                        evq(u)
                        if u >= 1:
                            evn(u - 1)
                    evn(NU - 1)

                @blk.vector
                def _(vector):
                    for u in range(NU):
                        g_, f = divmod(u, 8)
                        csl = s_c[:, g_ * 512:(g_ + 1) * 512]
                        ssl = s_s[:, g_ * 512:(g_ + 1) * 512]
                        vector.wait_ge(SEM["a"], A_EVQ[u])
                        vector.stream_shuffle(swp[:], qbf[u % 4][:], SWAP_MASK)
                        vector.tensor_tensor(tmp1[:], qbf[u % 4][:], csl, OP.mult)
                        vector.tensor_tensor(tmp2[:], swp[:], ssl, OP.mult)
                        if u >= 1:
                            vector.wait_ge(SEM["a"], A_EVN[u - 1])
                            vector.reduce_max(vncol[:], nrow[(u - 1) % 2][:], axis=AX)
                        else:
                            vector.tensor_copy(swp[:, 0:1], s_eps[:])
                        if u >= 4:
                            vector.wait_ge(SEM["st"], ST[u - 4])
                        assert inc1(vector.tensor_tensor(rot[u % 4][:], tmp1[:],
                                                         tmp2[:], OP.add),
                                    "v") == V_ROT[u]
                        if u >= 1:
                            _, pf = divmod(u - 1, 8)
                            nt = s_kn2[:, pf:pf + 1]
                            assert inc1(vector.tensor_tensor(nt, nt, vncol[:],
                                                             OP.max), "v") == V_NM[u - 1]
                        else:
                            vector.tensor_copy(swp[:, 0:1], s_eps[:])
                        if u >= 2:
                            vector.wait_ge(SEM["t"], T_N[u - 2])
                        assert inc1(vector.tensor_tensor(qsq[u % 2][:], rot[u % 4][:],
                                                         rot[u % 4][:], OP.mult),
                                    "v") == V_SQ[u]
                    up = NU - 1
                    _, pf = divmod(up, 8)
                    vector.wait_ge(SEM["a"], A_EVN[up])
                    vector.reduce_max(vncol[:], nrow[up % 2][:], axis=AX)
                    vector.tensor_copy(swp[:, 0:1], s_eps[:])
                    nt = s_kn2[:, pf:pf + 1]
                    assert inc1(vector.tensor_tensor(nt, nt, vncol[:],
                                                     OP.max), "v") == V_NM[up]

        with ExitStack() as ps:
            wres = ps.enter_context(nc.sbuf_tensor("p1vw", [128, 8 * 32 * 128], BF16))
            xbuf = [ps.enter_context(nc.sbuf_tensor(f"p1vx{i}", [128, 32 * 512], BF16))
                    for i in range(2)]
            vbf = [ps.enter_context(nc.sbuf_tensor(f"p1vq{i}", [128, 512], BF16))
                   for i in range(4)]
            rA = [ps.enter_context(nc.sbuf_tensor(f"p1vra{i}", [128, 512], BF16))
                  for i in range(2)]
            rB = [ps.enter_context(nc.sbuf_tensor(f"p1vrb{i}", [128, 512], BF16))
                  for i in range(2)]
            pk = [ps.enter_context(nc.psum_tensor(f"p1vp{i}", [128, 512], F32))
                  for i in range(4)]

            t_b = CNT["t"]; a_b = CNT["a"]; st_b = CNT["st"]
            g_b = CNT["g"]; ld_b = CNT["ld"]
            NU = 32
            def tchain(u): return t_b + u + 1
            def aev(u): return a_b + u + 1
            def stv(u): return st_b + 32 * (u + 1)
            def ld_grp(g_): return ld_b + 16 * 32 * (g_ + 1)

            with nc.Block() as blk:
                @blk.sync
                def _(sync):
                    for g_ in range(4):
                        if g_ >= 2:
                            sync.wait_ge(SEM["t"], tchain((g_ - 2) * 8 + 7))
                        for k in range(32):
                            incd(sync.dma_start_transpose(
                                out=xbuf[g_ % 2][:, k * 512:(k + 1) * 512],
                                in_=xn_own_h[g_ * 512:(g_ + 1) * 512,
                                             k * 128:(k + 1) * 128]), "ld")
                    sync.wait_ge(SEM["ld"], CNT["ld"])

                @blk.gpsimd
                def _(gpsimd):
                    incd(gpsimd.dma_start(
                        out=wres[:].rearrange("p (f k c) -> p f k c", f=8, k=32),
                        in_=WV), "g")
                    for u in range(NU):
                        g_, f = divmod(u, 8)
                        gpsimd.wait_ge(SEM["a"], aev(u))
                        if u >= 2:
                            gpsimd.wait_ge(SEM["st"], stv(u - 2))
                        gpsimd.tensor_scalar(rA[u % 2][:], vbf[u % 4][:],
                                             s_rm0[:], None, OP.mult)
                        gpsimd.tensor_scalar(rB[u % 2][:], vbf[u % 4][:],
                                             s_rm1[:], None, OP.mult)
                        incd(gpsimd.dma_start(
                            out=VTo_h[f * 128:(f + 1) * 128,
                                      g_ * 512:(g_ + 1) * 512],
                            in_=rA[u % 2][:]), "st")
                        assert incd(gpsimd.dma_start(
                            out=VTo_h[(NKV + f) * 128:(NKV + f + 1) * 128,
                                      g_ * 512:(g_ + 1) * 512],
                            in_=rB[u % 2][:]), "st") == stv(u)
                    gpsimd.wait_ge(SEM["st"], CNT["st"])
                    incd(gpsimd.dma_start(out=kn_h, in_=s_kn2[:]), "st")
                    gpsimd.wait_ge(SEM["st"], CNT["st"])
                    grp = [[0, 1], [2, 3], [4, 5], [6, 7]]
                    gpsimd.collective_compute(
                        "AllReduce", OP.add, replica_groups=grp,
                        ins=[KrTo_h], outs=[KrTg_h]).then_inc(SEM["cc"], 1)
                    gpsimd.collective_compute(
                        "AllReduce", OP.add, replica_groups=grp,
                        ins=[VTo_h], outs=[VTg_h]).then_inc(SEM["cc"], 1)
                    gpsimd.collective_compute(
                        "AllReduce", OP.max, replica_groups=grp,
                        ins=[kn_h], outs=[knr_h]).then_inc(SEM["cc"], 1)
                    gpsimd.wait_ge(SEM["cc"], 3)
                g_cast = g_b + 16

                @blk.tensor
                def _(tensor):
                    tensor.wait_ge(SEM["g"], g_cast)
                    for u in range(NU):
                        g_, f = divmod(u, 8)
                        if f == 0:
                            tensor.wait_ge(SEM["ld"], ld_grp(g_))
                        if u >= 4:
                            tensor.wait_ge(SEM["a"], aev(u - 4))
                        for k in range(32):
                            inst = tensor.matmul(
                                pk[u % 4][:],
                                wres[:, (f * 32 + k) * 128:(f * 32 + k + 1) * 128],
                                xbuf[g_ % 2][:, k * 512:(k + 1) * 512],
                                start=(k == 0), stop=(k == 31))
                        assert inc1(inst, "t") == tchain(u)

                @blk.scalar
                def _(scalar):
                    for u in range(NU):
                        scalar.wait_ge(SEM["t"], tchain(u))
                        if u >= 4:
                            scalar.wait_ge(SEM["st"], stv(u - 4))
                        assert inc1(scalar.copy(vbf[u % 4][:], pk[u % 4][:]),
                                    "a") == aev(u)

        with ExitStack() as span:
            with ExitStack() as ps:
                xnT = ps.enter_context(nc.sbuf_tensor("xnT", [128, 32 * TOWN], BF16))
                wqb = [ps.enter_context(nc.sbuf_tensor(f"wqb{i}", [128, 4096], BF16))
                       for i in range(2)]
                s_co = ps.enter_context(nc.sbuf_tensor("s_co", [128, TOWN], F32))
                s_so = ps.enter_context(nc.sbuf_tensor("s_so", [128, TOWN], F32))
                qbf = [ps.enter_context(nc.sbuf_tensor(f"qbf{i}", [128, 512], BF16))
                       for i in range(4)]
                rot = [ps.enter_context(nc.sbuf_tensor(f"qrot{i}", [128, 512], BF16))
                       for i in range(4)]
                qsq = [ps.enter_context(nc.sbuf_tensor(f"qsq{i}", [128, 512], BF16))
                       for i in range(2)]
                nrow = [ps.enter_context(nc.sbuf_tensor(f"qnr{i}", [1, 512], F32))
                        for i in range(2)]
                vncol = ps.enter_context(nc.sbuf_tensor("qvn", [1, 1], F32))
                tmp1 = ps.enter_context(nc.sbuf_tensor("qt1", [128, 512], F32))
                tmp2 = ps.enter_context(nc.sbuf_tensor("qt2", [128, 512], F32))
                swp = ps.enter_context(nc.sbuf_tensor("qsw", [128, 512], BF16))
                pk = [ps.enter_context(nc.psum_tensor(f"p2ap{i}", [128, 512], F32))
                      for i in range(4)]
                pN = [ps.enter_context(nc.psum_tensor(f"p2apn{i}", [1, 512], F32))
                      for i in range(2)]

                t_b = CNT["t"]; a_b = CNT["a"]; v_b = CNT["v"]; st_b = CNT["st"]
                g_b = CNT["g"]; ld_b = CNT["ld"]
                NU = 128
                T_CHAIN = {}; T_N = {}; A_EVQ = {}; A_EVN = {}
                V_ROT = {}; V_SQ = {}; V_NM = {}; ST = {}
                tc = t_b
                for u in range(NU):
                    tc += 1; T_CHAIN[u] = tc
                    if u >= 1:
                        tc += 1; T_N[u - 1] = tc
                tc += 1; T_N[NU - 1] = tc
                ac = a_b
                for u in range(NU):
                    ac += 1; A_EVQ[u] = ac
                    if u >= 1:
                        ac += 1; A_EVN[u - 1] = ac
                ac += 1; A_EVN[NU - 1] = ac
                vc = v_b
                for u in range(NU):
                    vc += 1; V_ROT[u] = vc
                    if u >= 1:
                        vc += 1; V_NM[u - 1] = vc
                    vc += 1; V_SQ[u] = vc
                vc += 1; V_NM[NU - 1] = vc
                stc = st_b
                for u in range(NU):
                    stc += 16; ST[u] = stc
                ld_xnT = ld_b + 16 * 2 + 16 * 128
                def g_wq(f): return g_b + 16 * (f + 1)

                with nc.Block() as blk:
                    @blk.sync
                    def _(sync):
                        incd(sync.dma_start(out=s_co[:], in_=COWN), "ld")
                        incd(sync.dma_start(out=s_so[:], in_=SOWN), "ld")
                        for k in range(32):
                            for gg in range(4):
                                incd(sync.dma_start_transpose(
                                    out=xnT[:, k * TOWN + gg * 512: k * TOWN + (gg + 1) * 512],
                                    in_=xn_own_h[gg * 512:(gg + 1) * 512,
                                                 k * 128:(k + 1) * 128]), "ld")
                        sync.wait_ge(SEM["ld"], CNT["ld"])

                    @blk.gpsimd
                    def _(gpsimd):
                        for f in range(2):
                            incd(gpsimd.dma_start(
                                out=wqb[f % 2][:].rearrange("p (k c) -> p k c", k=32),
                                in_=WQ[:, f]), "g")
                        for u in range(NU):
                            f, gg = divmod(u, 4)
                            if gg == 0 and f + 2 < 32:
                                gpsimd.wait_ge(SEM["t"], T_CHAIN[f * 4 + 3])
                                incd(gpsimd.dma_start(
                                    out=wqb[f % 2][:].rearrange("p (k c) -> p k c", k=32),
                                    in_=WQ[:, f + 2]), "g")
                            gpsimd.wait_ge(SEM["v"], V_ROT[u])
                            assert incd(gpsimd.dma_start(
                                out=Q_h[f, :, gg * 512:(gg + 1) * 512],
                                in_=rot[u % 4][:]), "st") == ST[u]
                        gpsimd.wait_ge(SEM["st"], CNT["st"])

                    @blk.tensor
                    def _(tensor):
                        tensor.wait_ge(SEM["ld"], ld_xnT)
                        for u in range(NU):
                            f, gg = divmod(u, 4)
                            if gg == 0:
                                tensor.wait_ge(SEM["g"], g_wq(f))
                            if u >= 4:
                                tensor.wait_ge(SEM["a"], A_EVQ[u - 4])
                            for k in range(32):
                                inst = tensor.matmul(
                                    pk[u % 4][:],
                                    wqb[f % 2][:, k * 128:(k + 1) * 128],
                                    xnT[:, k * TOWN + gg * 512: k * TOWN + (gg + 1) * 512],
                                    start=(k == 0), stop=(k == 31))
                            assert inc1(inst, "t") == T_CHAIN[u]
                            if u >= 1:
                                up = u - 1
                                tensor.wait_ge(SEM["v"], V_SQ[up])
                                if up >= 2:
                                    tensor.wait_ge(SEM["a"], A_EVN[up - 2])
                                assert inc1(tensor.matmul(pN[up % 2][:], s_ones_bf[:],
                                                          qsq[up % 2][:], start=True,
                                                          stop=True), "t") == T_N[up]
                        up = NU - 1
                        tensor.wait_ge(SEM["v"], V_SQ[up])
                        tensor.wait_ge(SEM["a"], A_EVN[up - 2])
                        assert inc1(tensor.matmul(pN[up % 2][:], s_ones_bf[:],
                                                  qsq[up % 2][:], start=True,
                                                  stop=True), "t") == T_N[up]

                    @blk.scalar
                    def _(scalar):
                        def evq(u):
                            scalar.wait_ge(SEM["t"], T_CHAIN[u])
                            if u >= 4:
                                scalar.wait_ge(SEM["v"], V_ROT[u - 4])
                            assert inc1(scalar.copy(qbf[u % 4][:], pk[u % 4][:]),
                                        "a") == A_EVQ[u]

                        def evn(u):
                            scalar.wait_ge(SEM["t"], T_N[u])
                            if u >= 2:
                                scalar.wait_ge(SEM["v"], V_NM[u - 2])
                            assert inc1(scalar.copy(nrow[u % 2][:], pN[u % 2][:]),
                                        "a") == A_EVN[u]

                        for u in range(NU):
                            evq(u)
                            if u >= 1:
                                evn(u - 1)
                        evn(NU - 1)

                    @blk.vector
                    def _(vector):
                        for u in range(NU):
                            f, gg = divmod(u, 4)
                            csl = s_co[:, gg * 512:(gg + 1) * 512]
                            ssl = s_so[:, gg * 512:(gg + 1) * 512]
                            vector.wait_ge(SEM["a"], A_EVQ[u])
                            vector.stream_shuffle(swp[:], qbf[u % 4][:], SWAP_MASK)
                            vector.tensor_tensor(tmp1[:], qbf[u % 4][:], csl, OP.mult)
                            vector.tensor_tensor(tmp2[:], swp[:], ssl, OP.mult)
                            if u >= 1:
                                vector.wait_ge(SEM["a"], A_EVN[u - 1])
                                vector.reduce_max(vncol[:], nrow[(u - 1) % 2][:], axis=AX)
                            else:
                                vector.tensor_copy(swp[:, 0:1], s_eps[:])
                            if u >= 4:
                                vector.wait_ge(SEM["st"], ST[u - 4])
                            assert inc1(vector.tensor_tensor(rot[u % 4][:], tmp1[:],
                                                             tmp2[:], OP.add),
                                        "v") == V_ROT[u]
                            if u >= 1:
                                pf, _pgg = divmod(u - 1, 4)
                                nt = s_qn2[:, pf:pf + 1]
                                assert inc1(vector.tensor_tensor(nt, nt, vncol[:],
                                                                 OP.max), "v") == V_NM[u - 1]
                            else:
                                vector.tensor_copy(swp[:, 0:1], s_eps[:])
                            if u >= 2:
                                vector.wait_ge(SEM["t"], T_N[u - 2])
                            assert inc1(vector.tensor_tensor(qsq[u % 2][:], rot[u % 4][:],
                                                             rot[u % 4][:], OP.mult),
                                        "v") == V_SQ[u]
                        up = NU - 1
                        pf, _pgg = divmod(up, 4)
                        vector.wait_ge(SEM["a"], A_EVN[up])
                        vector.reduce_max(vncol[:], nrow[up % 2][:], axis=AX)
                        vector.tensor_copy(swp[:, 0:1], s_eps[:])
                        nt = s_qn2[:, pf:pf + 1]
                        assert inc1(vector.tensor_tensor(nt, nt, vncol[:],
                                                         OP.max), "v") == V_NM[up]

                t_b = CNT["t"]; a_b = CNT["a"]; v_b = CNT["v"]; st_b = CNT["st"]
                g_b = CNT["g"]
                def t_rl(tt): return t_b + tt + 1
                def a_ex(tt): return a_b + tt + 1
                def v_pre(tt): return v_b + tt * 3 + 1
                def v_cmb(tt): return v_b + tt * 3 + 2
                def v_wg(tt): return v_b + tt * 3 + 3
                def st_wg(tt): return st_b + 16 * (tt + 1)
                g_art = g_b + 16

                with ExitStack() as ps2:
                    artb = ps2.enter_context(nc.sbuf_tensor("artb", [128, 32 * 80], BF16))
                    rh_all = ps2.enter_context(nc.sbuf_tensor("rh_all", [128, 1024], F32))
                    unorm = [ps2.enter_context(nc.sbuf_tensor(f"unorm{i}", [128, 16], F32))
                             for i in range(2)]
                    selb = ps2.enter_context(nc.sbuf_tensor("selb", [128, 16], F32))
                    comb = ps2.enter_context(nc.sbuf_tensor("comb", [128, 16], F32))
                    m8 = ps2.enter_context(nc.sbuf_tensor("m8", [128, 8], F32))
                    _rc = [ps2.enter_context(nc.sbuf_tensor(f"p2brc{i}", [128, 1], F32))
                           for i in range(8)]
                    mxc, nmxc, Sc, dnc, rfc, s1c, spcc, _x = [c[:] for c in _rc]
                    wgt = [ps2.enter_context(nc.sbuf_tensor(f"wgt{i}", [128, 1024], BF16))
                           for i in range(2)]
                    prl = [ps2.enter_context(nc.psum_tensor(f"prl{i}", [128, 80], F32))
                           for i in range(2)]

                    with nc.Block() as blk:
                        @blk.gpsimd
                        def _(gpsimd):
                            incd(gpsimd.dma_start(
                                out=artb[:].rearrange("p (k c) -> p k c", k=32),
                                in_=ART), "g")

                        @blk.tensor
                        def _(tensor):
                            tensor.wait_ge(SEM["g"], g_art)
                            for tt in range(16):
                                if tt >= 2:
                                    tensor.wait_ge(SEM["v"], v_cmb(tt - 2))
                                for k in range(32):
                                    inst = tensor.matmul(
                                        prl[tt % 2][:],
                                        xnT[:, k * TOWN + tt * 128: k * TOWN + (tt + 1) * 128],
                                        artb[:, k * 80:(k + 1) * 80],
                                        start=(k == 0), stop=(k == 31))
                                assert inc1(inst, "t") == t_rl(tt)

                        @blk.vector
                        def _(vector):
                            for tt in range(16):
                                vector.wait_ge(SEM["t"], t_rl(tt))
                                vector.reduce_max(mxc, prl[tt % 2][:, 64:80], axis=AX)
                                vector.tensor_copy(rh_all[:, tt * 64:(tt + 1) * 64],
                                                   prl[tt % 2][:, 0:64])
                                vector.tensor_scalar(nmxc, mxc, -1.0, None, OP.mult)
                                assert inc1(vector.drain(), "v") == v_pre(tt)
                                vector.wait_ge(SEM["a"], a_ex(tt))
                                vector.max(m8[:], unorm[tt % 2][:])
                                vector.tensor_scalar(dnc, Sc, 1e-6, None, OP.mult)
                                vector.tensor_tensor(s1c, m8[:, 0:1], m8[:, 1:2], OP.add)
                                vector.tensor_scalar(selb[:], unorm[tt % 2][:],
                                                     m8[:, 1:2], None, OP.is_ge)
                                vector.tensor_tensor(dnc, dnc, s1c, OP.add)
                                vector.tensor_copy(spcc, s_eps[:])
                                vector.reciprocal(rfc, dnc)
                                vector.tensor_copy(spcc, s_eps[:])
                                assert inc1(vector.scalar_tensor_tensor(
                                    comb[:], unorm[tt % 2][:], rfc, selb[:],
                                    OP.mult, OP.mult), "v") == v_cmb(tt)
                                if tt >= 2:
                                    vector.wait_ge(SEM["st"], st_wg(tt - 2))
                                vector.tensor_copy(spcc, s_eps[:])
                                for e in range(16):
                                    inst = vector.tensor_scalar(
                                        wgt[tt % 2][:, e * 64:(e + 1) * 64],
                                        rh_all[:, tt * 64:(tt + 1) * 64],
                                        comb[:, e:e + 1], 0.5, OP.mult, OP.mult)
                                assert inc1(inst, "v") == v_wg(tt)

                        @blk.scalar
                        def _(scalar):
                            for tt in range(16):
                                scalar.wait_ge(SEM["v"], v_pre(tt))
                                assert inc1(scalar.activation(
                                    unorm[tt % 2][:], prl[tt % 2][:, 64:80], AF.Exp,
                                    bias=nmxc, accum_out=Sc), "a") == a_ex(tt)

                        @blk.sync
                        def _(sync):
                            for tt in range(16):
                                sync.wait_ge(SEM["v"], v_wg(tt))
                                assert incd(sync.dma_start(
                                    out=Wgt_h[tt * 128:(tt + 1) * 128, :],
                                    in_=wgt[tt % 2][:]), "st") == st_wg(tt)
                            sync.wait_ge(SEM["st"], CNT["st"])

            t_b = CNT["t"]; a_b = CNT["a"]; v_b = CNT["v"]; st_b = CNT["st"]
            g_b = CNT["g"]; ld_b = CNT["ld"]

            def t_g(lin): return t_b + lin + 1
            def a_silu(lin): return a_b + lin + 1
            def v_gadd(lin): return v_b + lin + 1
            def st_g(lin): return st_b + 16 * (lin + 1)
            def ld_wave(wave): return ld_b + 16 * 8 + 16 * 64 * (wave + 1)
            g_bias = g_b + 16
            def g_bw(idx, half): return g_bias + 16 * (idx * 3 + half + 1)
            def g_bst(idx): return g_bias + 16 * (idx * 3 + 3)

            with ExitStack() as ps:
                wgtT = ps.enter_context(nc.sbuf_tensor("wgtT", [128, 8 * TOWN], BF16))
                xnw = ps.enter_context(nc.sbuf_tensor("xnw", [128, 32 * 1024], BF16))
                bwb = [ps.enter_context(nc.sbuf_tensor(f"bwb{i}", [128, 16 * 512], BF16))
                       for i in range(4)]
                bstb = [ps.enter_context(nc.sbuf_tensor(f"bstb{i}", [128, 8 * 512], BF16))
                        for i in range(2)]
                biasB = ps.enter_context(nc.sbuf_tensor("biasB", [128, D], BF16))
                gtmp = [ps.enter_context(nc.sbuf_tensor(f"gtmp{i}", [128, 512], F32))
                        for i in range(4)]
                gout = [ps.enter_context(nc.sbuf_tensor(f"gout{i}", [128, 512], F32))
                        for i in range(4)]

                with nc.Block() as blk:
                    @blk.sync
                    def _(sync):
                        for ek in range(8):
                            incd(sync.dma_start_transpose(
                                out=wgtT[:, ek * TOWN:(ek + 1) * TOWN],
                                in_=Wgt_h[:, ek * 128:(ek + 1) * 128]), "ld")

                    @blk.gpsimd
                    def _(gpsimd):
                        assert incd(gpsimd.dma_start(out=biasB[:], in_=bias_rep),
                                    "g") == g_bias

                with ExitStack() as psm:
                    ptt = [psm.enter_context(nc.psum_tensor(f"pttA{i}", [128, 512], F32))
                           for i in range(8)]
                    with nc.Block() as blk:
                        @blk.sync
                        def _(sync):
                            for wave in range(2):
                                if wave > 0:
                                    sync.wait_ge(SEM["t"], t_g(63))
                                for k in range(32):
                                    for hh in range(2):
                                        tok0 = wave * 1024 + hh * 512
                                        incd(sync.dma_start_transpose(
                                            out=xnw[:, k * 1024 + hh * 512:
                                                    k * 1024 + (hh + 1) * 512],
                                            in_=xn_own_h[tok0:tok0 + 512,
                                                         k * 128:(k + 1) * 128]), "ld")
                                for fc in range(8):
                                    for tt in range(8):
                                        lin = (wave * 8 + fc) * 8 + tt
                                        tg = wave * 8 + tt
                                        sync.wait_ge(SEM["a"], a_silu(lin))
                                        assert incd(sync.dma_start(
                                            out=G_h[tg * 128:(tg + 1) * 128,
                                                    fc * 512:(fc + 1) * 512],
                                            in_=gout[lin % 4][:]), "st") == st_g(lin)
                            sync.wait_ge(SEM["st"], CNT["st"])

                        @blk.gpsimd
                        def _(gpsimd):
                            for idx in range(16):
                                wave, fc = divmod(idx, 8)
                                if idx >= 2:
                                    gpsimd.wait_ge(SEM["t"], t_g((idx - 2) * 8 + 7))
                                for half in range(2):
                                    incd(gpsimd.dma_start(
                                        out=bwb[(idx % 2) * 2 + half][:]
                                        .rearrange("p (k c) -> p k c", k=16),
                                        in_=BW[:, fc, half * 16:(half + 1) * 16]), "g")
                                incd(gpsimd.dma_start(
                                    out=bstb[idx % 2][:].rearrange("p (k c) -> p k c", k=8),
                                    in_=BST[:, fc]), "g")

                        @blk.tensor
                        def _(tensor):
                            for idx in range(16):
                                wave, fc = divmod(idx, 8)
                                if fc == 0:
                                    tensor.wait_ge(SEM["ld"], ld_wave(wave))
                                if idx > 0:
                                    for tt in range(8):
                                        tensor.wait_ge(SEM["v"], v_gadd((idx - 1) * 8 + tt))
                                for half in range(2):
                                    tensor.wait_ge(SEM["g"], g_bw(idx, half))
                                    for k16 in range(16):
                                        k = half * 16 + k16
                                        for tt in range(8):
                                            tensor.matmul(
                                                ptt[tt][:],
                                                xnw[:, k * 1024 + tt * 128:
                                                    k * 1024 + (tt + 1) * 128],
                                                bwb[(idx % 2) * 2 + half][:,
                                                    k16 * 512:(k16 + 1) * 512],
                                                start=(k == 0), stop=False,
                                                skip_group_check=True)
                                tensor.wait_ge(SEM["g"], g_bst(idx))
                                for ek in range(8):
                                    for tt in range(8):
                                        tg = wave * 8 + tt
                                        inst = tensor.matmul(
                                            ptt[tt][:],
                                            wgtT[:, ek * TOWN + tg * 128:
                                                 ek * TOWN + (tg + 1) * 128],
                                            bstb[idx % 2][:, ek * 512:(ek + 1) * 512],
                                            start=False, stop=(ek == 7),
                                            skip_group_check=True)
                                        if ek == 7:
                                            assert inc1(inst, "t") == t_g(idx * 8 + tt)

                        @blk.vector
                        def _(vector):
                            vector.wait_ge(SEM["g"], g_bias)
                            for idx in range(16):
                                wave, fc = divmod(idx, 8)
                                for tt in range(8):
                                    lin = idx * 8 + tt
                                    vector.wait_ge(SEM["t"], t_g(lin))
                                    if lin >= 4:
                                        vector.wait_ge(SEM["a"], a_silu(lin - 4))
                                    assert inc1(vector.tensor_tensor(
                                        gtmp[lin % 4][:], ptt[tt][:],
                                        biasB[:, fc * 512:(fc + 1) * 512],
                                        OP.add), "v") == v_gadd(lin)

                        @blk.scalar
                        def _(scalar):
                            for idx in range(16):
                                for tt in range(8):
                                    lin = idx * 8 + tt
                                    scalar.wait_ge(SEM["v"], v_gadd(lin))
                                    if lin >= 4:
                                        scalar.wait_ge(SEM["st"], st_g(lin - 4))
                                    assert inc1(scalar.activation(
                                        gout[lin % 4][:], gtmp[lin % 4][:],
                                        AF.Silu), "a") == a_silu(lin)

        ld_b = CNT["ld"]; t_b = CNT["t"]; a_b = CNT["a"]; v_b = CNT["v"]
        st_b = CNT["st"]

        def e_of(j): return 2 * j + 2

        def stream_segs():
            segs = []
            for j in range(NSLOT):
                segs.append(("INIT", j))
                for c0 in range(0, e_of(j), 2):
                    segs.append(("CHUNK", j, c0))
            segs.append(("RFIN",))
            return segs

        SEGS1 = stream_segs()
        MERGED = []
        for sa, sb in zip(SEGS1, SEGS1):
            MERGED.append((0,) + sa)
            MERGED.append((1,) + sb)

        TPOSr = {}; APOSr = {}; VPOSr = {}; STr = {}
        tc = ac = vc = stc = 0
        for seg in MERGED:
            s = seg[0]; kind = seg[1]
            if kind == "INIT":
                j = seg[2]
                tc += 1; TPOSr[(s, "S", j, 1)] = tc
                if j >= 1:
                    tc += 1; TPOSr[(s, "R", j - 1)] = tc
                    ac += 1; APOSr[(s, "RC", j - 1)] = ac
                    vc += 1; VPOSr[(s, "REC", j - 1)] = vc
            elif kind == "CHUNK":
                j = seg[2]; c0 = seg[3]; e = e_of(j)
                if c0 + 3 < e:
                    tc += 1; TPOSr[(s, "S", j, c0 + 3)] = tc
                if c0 == e - 2:
                    vc += 2; VPOSr[(s, "M", j)] = vc
                ac += 1; APOSr[(s, "E", j, c0)] = ac
                if c0 + 1 == e - 1:
                    tc += 1; TPOSr[(s, "L", j, e - 1)] = tc
                if c0 == 0 and j >= 1:
                    vc += 1; VPOSr[(s, "EV", j - 1)] = vc
                    stc += 16; STr[(s, j - 1)] = stc
            else:
                tc += 1; TPOSr[(s, "R", NSLOT - 1)] = tc
                ac += 1; APOSr[(s, "RC", NSLOT - 1)] = ac
                vc += 1; VPOSr[(s, "REC", NSLOT - 1)] = vc
                vc += 1; VPOSr[(s, "EV", NSLOT - 1)] = vc
                stc += 16; STr[(s, NSLOT - 1)] = stc
        TPK, APK, VPK, STK = tc, ac, vc, stc

        t_nm0 = t_b; a_nm0 = a_b; v_nm0 = v_b
        t_m = t_b + NKV
        a_m = a_b + 3 * NKV
        v_m = v_b + 4 * NKV

        def pair_of(kvh): return kvh // 2, kvh % 2
        def t_op(kvh, kind, j, t_=None):
            p, s = pair_of(kvh)
            key = (s, kind, j) if kind == "R" else (s, kind, j, t_)
            return t_m + p * TPK + TPOSr[key]
        def a_exp(kvh, j, c0):
            p, s = pair_of(kvh)
            return a_m + p * APK + APOSr[(s, "E", j, c0)]
        def a_rc(kvh, j):
            p, s = pair_of(kvh)
            return a_m + p * APK + APOSr[(s, "RC", j)]
        def v_of(kvh, kind, j):
            p, s = pair_of(kvh)
            return v_m + p * VPK + VPOSr[(s, kind, j)]
        def st_avt(kvh, j):
            p, s = pair_of(kvh)
            return st_b + p * STK + STr[(s, j)]
        def ld_kvh(kvh): return ld_b + 16 + 16 * 37 * (kvh + 1)
        ld_kn = ld_b + 16
        def t_pair_end(p): return t_m + (p + 1) * TPK

        with ExitStack() as ps:
            krb = [ps.enter_context(nc.sbuf_tensor(f"krb{i}", [128, S], BF16)) for i in range(3)]
            vtb = [ps.enter_context(nc.sbuf_tensor(f"vtb{i}", [128, S], BF16)) for i in range(3)]
            qkb = [ps.enter_context(nc.sbuf_tensor(f"qkb{i}", [128, 8192], BF16)) for i in range(3)]
            prb = [ps.enter_context(nc.sbuf_tensor(f"prb{i}", [128, 32 * 512], BF16)) for i in range(2)]
            rsb = [ps.enter_context(nc.sbuf_tensor(f"rsb{i}", [128, 512], F32)) for i in range(2)]
            rrow = [ps.enter_context(nc.sbuf_tensor(f"rrow{i}", [1, 512], F32)) for i in range(2)]
            nmsc = ps.enter_context(nc.sbuf_tensor("nmsc", [1, 8], F32))
            avst = [ps.enter_context(nc.sbuf_tensor(f"avst{i}", [128, 512], BF16)) for i in range(2)]
            pS4 = ps.enter_context(nc.psum_tensor("pS4", [128, 2048], F32))
            pLb = [ps.enter_context(nc.psum_tensor(f"pLb{i}", [128, 512], F32)) for i in range(2)]
            pAV = [ps.enter_context(nc.psum_tensor(f"pAV{i}", [128, 512], F32)) for i in range(2)]

            def psl(s, t_):
                b = s * 1024 + (t_ % 2) * 512
                return pS4[:, b:b + 512]

            with nc.Block() as blk:
                @blk.vector
                def _(vector):
                    for kvh in range(NKV):
                        assert inc1(vector.reduce_max(
                            nmsc[:, 0:1], s_qn2[:, 4 * kvh:4 * kvh + 4], axis=AX),
                            "v") == v_nm0 + 4 * kvh + 1
                        vector.wait_ge(SEM["a"], a_nm0 + 3 * kvh + 2)
                        assert inc1(vector.tensor_tensor(
                            nmsc[:, 2:3], nmsc[:, 3:4], nmsc[:, 4:5], OP.add),
                            "v") == v_nm0 + 4 * kvh + 2
                        vector.wait_ge(SEM["a"], a_nm0 + 3 * kvh + 3)
                        vector.tensor_scalar(
                            nmsc[:, 7:8], nmsc[:, 6:7], -1.0001 * RSQ, -0.01,
                            OP.mult, OP.add)
                        assert inc1(vector.drain(), "v") == v_nm0 + 4 * kvh + 3
                        vector.wait_ge(SEM["t"], t_nm0 + kvh + 1)
                        vector.tensor_copy(s_nmall[:, kvh:kvh + 1], pLb[0][:, 0:1])
                        assert inc1(vector.drain(), "v") == v_nm0 + 4 * kvh + 4
                    for p in range(4):
                        for seg in MERGED:
                            s = seg[0]; kvh = 2 * p + s; kind = seg[1]
                            if kind == "INIT":
                                j = seg[2]
                                if j >= 1:
                                    vector.wait_ge(SEM["t"], t_op(kvh, "L", j - 1, e_of(j - 1) - 1))
                                    vector.reciprocal(rrow[s][:], pLb[s][0:1, :])
                                    assert inc1(vector.drain(), "v") == v_of(kvh, "REC", j - 1)
                            elif kind == "CHUNK":
                                j = seg[2]; c0 = seg[3]; e = e_of(j)
                                if c0 == 0 and j >= 1:
                                    vector.wait_ge(SEM["t"], t_op(kvh, "R", j - 1))
                                    vector.wait_ge(SEM["a"], a_rc(kvh, j - 1))
                                    if j >= 2:
                                        vector.wait_ge(SEM["st"], st_avt(kvh, j - 2))
                                    assert inc1(vector.tensor_tensor(
                                        avst[s][:], pAV[s][:], rsb[s][:], OP.mult),
                                        "v") == v_of(kvh, "EV", j - 1)
                                if c0 == e - 2:
                                    vector.wait_ge(SEM["t"], t_op(kvh, "S", j, e - 1))
                                    for mi, t_ in enumerate((e - 2, e - 1)):
                                        sl = psl(s, t_).rearrange("p (h q) -> p h q", h=4)
                                        mk = s_msk[:, mi * 128:(mi + 1) * 128].unsqueeze(1) \
                                            .broadcast_to([128, 4, 128])
                                        vector.tensor_tensor(sl, sl, mk, OP.add)
                                    CNT["v"] += 2
                                    vector.drain().then_inc(SEM["v"], 2)
                                    assert CNT["v"] == v_of(kvh, "M", j)
                            else:
                                jl = NSLOT - 1
                                vector.wait_ge(SEM["t"], t_op(kvh, "L", jl, e_of(jl) - 1))
                                vector.reciprocal(rrow[s][:], pLb[s][0:1, :])
                                assert inc1(vector.drain(), "v") == v_of(kvh, "REC", jl)
                                vector.wait_ge(SEM["t"], t_op(kvh, "R", jl))
                                vector.wait_ge(SEM["a"], a_rc(kvh, jl))
                                vector.wait_ge(SEM["st"], st_avt(kvh, jl - 1))
                                assert inc1(vector.tensor_tensor(
                                    avst[s][:], pAV[s][:], rsb[s][:], OP.mult),
                                    "v") == v_of(kvh, "EV", jl)

                @blk.scalar
                def _(scalar):
                    scalar.wait_ge(SEM["ld"], ld_kn)
                    for kvh in range(NKV):
                        scalar.wait_ge(SEM["v"], v_nm0 + 4 * kvh + 1)
                        assert inc1(scalar.activation(nmsc[:, 3:4], nmsc[:, 0:1],
                                                      AF.Ln), "a") == a_nm0 + 3 * kvh + 1
                        assert inc1(scalar.activation(nmsc[:, 4:5], s_kn2[:, kvh:kvh + 1],
                                                      AF.Ln), "a") == a_nm0 + 3 * kvh + 2
                        scalar.wait_ge(SEM["v"], v_nm0 + 4 * kvh + 2)
                        assert inc1(scalar.activation(nmsc[:, 6:7], nmsc[:, 2:3],
                                                      AF.Exp, scale=0.5),
                                    "a") == a_nm0 + 3 * kvh + 3
                    for p in range(4):
                        for seg in MERGED:
                            s = seg[0]; kvh = 2 * p + s; kind = seg[1]
                            if kind == "INIT":
                                j = seg[2]
                                if j >= 1:
                                    scalar.wait_ge(SEM["t"], t_op(kvh, "R", j - 1))
                                    if j >= 2:
                                        scalar.wait_ge(SEM["v"], v_of(kvh, "EV", j - 2))
                                    assert inc1(scalar.copy(rsb[s][:], pLb[s][:]),
                                                "a") == a_rc(kvh, j - 1)
                            elif kind == "CHUNK":
                                j = seg[2]; c0 = seg[3]; e = e_of(j)
                                scalar.wait_ge(SEM["t"], t_op(kvh, "S", j, c0 + 1))
                                if c0 == e - 2:
                                    scalar.wait_ge(SEM["v"], v_of(kvh, "M", j))
                                b0 = s * 1024
                                assert inc1(scalar.activation(
                                    prb[s][:, c0 * 512:(c0 + 2) * 512],
                                    pS4[:, b0:b0 + 1024], AF.Exp, scale=RSQ,
                                    bias=s_nmall[:, kvh:kvh + 1]),
                                    "a") == a_exp(kvh, j, c0)
                            else:
                                jl = NSLOT - 1
                                scalar.wait_ge(SEM["t"], t_op(kvh, "R", jl))
                                scalar.wait_ge(SEM["v"], v_of(kvh, "EV", jl - 1))
                                assert inc1(scalar.copy(rsb[s][:], pLb[s][:]),
                                            "a") == a_rc(kvh, jl)

                @blk.tensor
                def _(tensor):
                    for kvh in range(NKV):
                        tensor.wait_ge(SEM["v"], v_nm0 + 4 * kvh + 3)
                        assert inc1(tensor.matmul(pLb[0][:, 0:1], s_ones_f[:],
                                                  nmsc[:, 7:8], start=True, stop=True),
                                    "t") == t_nm0 + kvh + 1

                    def mmS(kvh, s, j, t_):
                        inst = tensor.matmul(
                            psl(s, t_),
                            krb[kvh % 3][:, t_ * 128:(t_ + 1) * 128],
                            qkb[kvh % 3][:, j * 512:(j + 1) * 512],
                            start=True, stop=True, skip_group_check=True)
                        if t_ % 2 == 1:
                            assert inc1(inst, "t") == t_op(kvh, "S", j, t_)

                    tensor.wait_ge(SEM["v"], v_nm0 + 4 * NKV)
                    for p in range(4):
                        for s0 in range(2):
                            kvh = 2 * p + s0
                            tensor.wait_ge(SEM["ld"], ld_kvh(kvh))
                        for seg in MERGED:
                            s = seg[0]; kvh = 2 * p + s; kind = seg[1]
                            if kind == "INIT":
                                j = seg[2]
                                if j == 0:
                                    tensor.wait_ge(SEM["a"], a_m + p * APK if p == 0 or s == 1 else a_m + p * APK)
                                    tensor.wait_ge(SEM["a"], a_m + p * APK)
                                else:
                                    tensor.wait_ge(SEM["a"],
                                                   a_exp(kvh, j - 1, e_of(j - 1) - 2))
                                for t_ in range(2):
                                    mmS(kvh, s, j, t_)
                                if j >= 1:
                                    tensor.wait_ge(SEM["v"], v_of(kvh, "REC", j - 1))
                                    inst = tensor.matmul(pLb[s][:], s_ones_f[:],
                                                         rrow[s][:], start=True,
                                                         stop=True, skip_group_check=True)
                                    assert inc1(inst, "t") == t_op(kvh, "R", j - 1)
                            elif kind == "CHUNK":
                                j = seg[2]; c0 = seg[3]; e = e_of(j)
                                for t2 in range(c0 + 2, min(c0 + 4, e)):
                                    if t2 % 2 == 0:
                                        tensor.wait_ge(SEM["a"], a_exp(kvh, j, t2 - 2))
                                    mmS(kvh, s, j, t2)
                                if c0 + 2 >= e:
                                    tensor.wait_ge(SEM["a"], a_exp(kvh, j, c0))
                                for t_ in (c0, c0 + 1):
                                    if t_ == c0:
                                        if c0 == 0 and j >= 1:
                                            tensor.wait_ge(SEM["a"], a_rc(kvh, j - 1))
                                    inst = tensor.matmul(
                                        pLb[s][0:1, :], s_ones_bf[:],
                                        prb[s][:, t_ * 512:(t_ + 1) * 512],
                                        start=(t_ == 0), stop=(t_ == e - 1),
                                        skip_group_check=True)
                                    if t_ == e - 1:
                                        assert inc1(inst, "t") == t_op(kvh, "L", j, t_)
                                    if t_ == 0 and j >= 1:
                                        tensor.wait_ge(SEM["v"], v_of(kvh, "EV", j - 1))
                                    tensor.matmul(
                                        pAV[s][:],
                                        vtb[kvh % 3][:, t_ * 128:(t_ + 1) * 128],
                                        prb[s][:, t_ * 512:(t_ + 1) * 512],
                                        start=(t_ == 0), stop=(t_ == e - 1),
                                        skip_group_check=True)
                            else:
                                jl = NSLOT - 1
                                tensor.wait_ge(SEM["v"], v_of(kvh, "REC", jl))
                                inst = tensor.matmul(pLb[s][:], s_ones_f[:],
                                                     rrow[s][:], start=True,
                                                     stop=True, skip_group_check=True)
                                assert inc1(inst, "t") == t_op(kvh, "R", jl)

                @blk.sync
                def _(sync):
                    sync.wait_ge(SEM["cc"], 3)
                    KrTg4 = KrTg_h.rearrange("(r kv p) (u c) -> kv p u r c",
                                             r=2, kv=NKV, c=128)
                    for kvh in range(NKV):
                        if kvh >= 3:
                            sync.wait_ge(SEM["t"], t_pair_end((kvh - 3) // 2))
                        incd(sync.dma_start(
                            out=krb[kvh % 3][:].rearrange("p (u r c) -> p u r c",
                                                          u=16, r=2),
                            in_=KrTg4[kvh]), "ld")
                        for t_ in range(32):
                            u, r_ = t_ // 2, t_ % 2
                            incd(sync.dma_start_transpose(
                                out=vtb[kvh % 3][:, t_ * 128:(t_ + 1) * 128],
                                in_=VTg_h[(r_ * NKV + kvh) * 128:
                                          (r_ * NKV + kvh + 1) * 128,
                                          u * 128:(u + 1) * 128]), "ld")
                        for h in range(4):
                            qv = qkb[kvh % 3][:].rearrange("p (sl h q) -> p h sl q",
                                                           h=4, sl=16)
                            incd(sync.dma_start(out=qv[:, h], in_=Q_h[4 * kvh + h]), "ld")
                    sync.wait_ge(SEM["ld"], CNT["ld"])

                @blk.gpsimd
                def _(gpsimd):
                    gpsimd.wait_ge(SEM["cc"], 3)
                    incd(gpsimd.dma_start(out=s_kn2[:], in_=knr_h), "ld")
                    for p in range(4):
                        for seg in MERGED:
                            s = seg[0]; kvh = 2 * p + s; kind = seg[1]
                            if kind == "CHUNK" and seg[3] == 0 and seg[2] >= 1:
                                j = seg[2]
                                gpsimd.wait_ge(SEM["v"], v_of(kvh, "EV", j - 1))
                                av = avst[s][:].rearrange("p (h q) -> p h q", h=4)
                                assert incd(gpsimd.dma_start(
                                    out=AVT_h.rearrange("n p q -> p n q")[
                                        :, 4 * kvh:4 * kvh + 4,
                                        (j - 1) * 128:j * 128],
                                    in_=av), "st") == st_avt(kvh, j - 1)
                            elif kind == "RFIN":
                                jl = NSLOT - 1
                                gpsimd.wait_ge(SEM["v"], v_of(kvh, "EV", jl))
                                av = avst[s][:].rearrange("p (h q) -> p h q", h=4)
                                assert incd(gpsimd.dma_start(
                                    out=AVT_h.rearrange("n p q -> p n q")[
                                        :, 4 * kvh:4 * kvh + 4,
                                        jl * 128:(jl + 1) * 128],
                                    in_=av), "st") == st_avt(kvh, jl)
                    gpsimd.wait_ge(SEM["st"], CNT["st"])

        ld_b = CNT["ld"]; t_b = CNT["t"]; v_b = CNT["v"]; st_b = CNT["st"]
        g_b = CNT["g"]

        def t_s(lin): return t_b + lin + 1
        def v_o(lin): return v_b + lin + 1
        def st_o(lin): return st_b + 16 * (lin + 1)
        def g_wo(idx, half): return g_b + 16 * (idx * 2 + half + 1)
        ld_avt = ld_b + 16 * 32
        def ld_gx(lin): return ld_avt + 16 * 2 * (lin + 1)

        with ExitStack() as ps:
            avres = ps.enter_context(nc.sbuf_tensor("avres", [128, 32 * TOWN], BF16))
            wob = [ps.enter_context(nc.sbuf_tensor(f"wob{i}", [128, 16 * 512], BF16))
                   for i in range(3)]
            gch = [ps.enter_context(nc.sbuf_tensor(f"gch{i}", [128, 512], F32)) for i in range(5)]
            xch = [ps.enter_context(nc.sbuf_tensor(f"xch{i}", [128, 512], F32)) for i in range(5)]
            cch = [ps.enter_context(nc.sbuf_tensor(f"cch{i}", [128, 512], F32)) for i in range(2)]
            och = [ps.enter_context(nc.sbuf_tensor(f"och{i}", [128, 512], F32)) for i in range(2)]
            ptt = [ps.enter_context(nc.psum_tensor(f"pttB{i}", [128, 512], F32))
                   for i in range(8)]

            with nc.Block() as blk:
                @blk.sync
                def _(sync):
                    for head in range(32):
                        incd(sync.dma_start(
                            out=avres[:, head * TOWN:(head + 1) * TOWN],
                            in_=AVT_h[head]), "ld")
                    for lin in range(128):
                        idx, tt = divmod(lin, 8)
                        wave, fc = divmod(idx, 8)
                        tg = wave * 8 + tt
                        if lin >= 5:
                            l2 = lin - 5
                            sync.wait_ge(SEM["v"], v_o(l2))
                            i2, tt2 = divmod(l2, 8)
                            w2, fc2 = divmod(i2, 8)
                            tg2 = w2 * 8 + tt2
                            assert incd(sync.dma_start(
                                out=out_own[tg2 * 128:(tg2 + 1) * 128,
                                            fc2 * 512:(fc2 + 1) * 512],
                                in_=och[tt2 % 2][:]), "st") == st_o(l2)
                        incd(sync.dma_start(
                            out=gch[lin % 5][:],
                            in_=G_h[tg * 128:(tg + 1) * 128,
                                    fc * 512:(fc + 1) * 512]), "ld")
                        incd(sync.dma_start(
                            out=xch[lin % 5][:],
                            in_=x_own[tg * 128:(tg + 1) * 128,
                                      fc * 512:(fc + 1) * 512]), "ld")
                    for l2 in range(123, 128):
                        i2, tt2 = divmod(l2, 8)
                        w2, fc2 = divmod(i2, 8)
                        tg2 = w2 * 8 + tt2
                        sync.wait_ge(SEM["v"], v_o(l2))
                        assert incd(sync.dma_start(
                            out=out_own[tg2 * 128:(tg2 + 1) * 128,
                                        fc2 * 512:(fc2 + 1) * 512],
                            in_=och[tt2 % 2][:]), "st") == st_o(l2)
                    sync.wait_ge(SEM["st"], CNT["st"])
                    sync.wait_ge(SEM["ld"], CNT["ld"])

                @blk.gpsimd
                def _(gpsimd):
                    for idx in range(16):
                        wave, fc = divmod(idx, 8)
                        for half in range(2):
                            prev = idx * 2 + half - 3
                            if prev >= 0:
                                gpsimd.wait_ge(SEM["t"], t_s((prev // 2) * 8 + 7))
                            incd(gpsimd.dma_start(
                                out=wob[(idx * 2 + half) % 3][:]
                                .rearrange("p (k c) -> p k c", k=16),
                                in_=WO[:, fc, half * 16:(half + 1) * 16]), "g")

                @blk.tensor
                def _(tensor):
                    tensor.wait_ge(SEM["ld"], ld_avt)
                    for idx in range(16):
                        wave, fc = divmod(idx, 8)
                        tensor.wait_ge(SEM["g"], g_wo(idx, 0))
                        tensor.wait_ge(SEM["g"], g_wo(idx, 1))
                        for tt in range(8):
                            tg = wave * 8 + tt
                            if idx > 0:
                                tensor.wait_ge(SEM["v"], v_o((idx - 1) * 8 + tt))
                            for k in range(32):
                                half, k16 = divmod(k, 16)
                                inst = tensor.matmul(
                                    ptt[tt][:],
                                    avres[:, k * TOWN + tg * 128:
                                          k * TOWN + (tg + 1) * 128],
                                    wob[(idx * 2 + half) % 3][:,
                                        k16 * 512:(k16 + 1) * 512],
                                    start=(k == 0), stop=(k == 31),
                                    skip_group_check=True)
                            assert inc1(inst, "t") == t_s(idx * 8 + tt)

                @blk.vector
                def _(vector):
                    for idx in range(16):
                        for pair in range(4):
                            tt0 = 2 * pair; tt1 = tt0 + 1
                            lin0 = idx * 8 + tt0; lin1 = lin0 + 1
                            vector.wait_ge(SEM["t"], t_s(lin0))
                            vector.wait_ge(SEM["ld"], ld_gx(lin0))
                            if lin0 >= 2:
                                vector.wait_ge(SEM["st"], st_o(lin0 - 2))
                            vector.tensor_tensor(cch[0][:], ptt[tt0][:],
                                                 gch[lin0 % 5][:], OP.mult)
                            vector.wait_ge(SEM["t"], t_s(lin1))
                            vector.wait_ge(SEM["ld"], ld_gx(lin1))
                            if lin1 >= 2:
                                vector.wait_ge(SEM["st"], st_o(lin1 - 2))
                            vector.tensor_tensor(cch[1][:], ptt[tt1][:],
                                                 gch[lin1 % 5][:], OP.mult)
                            assert inc1(vector.tensor_tensor(
                                och[0][:], cch[0][:], xch[lin0 % 5][:], OP.add),
                                "v") == v_o(lin0)
                            assert inc1(vector.tensor_tensor(
                                och[1][:], cch[1][:], xch[lin1 % 5][:], OP.add),
                                "v") == v_o(lin1)

    return nc


_NC_CACHE = {}


def kernel(L0_attn_x, L0_attn_wq, L0_attn_wk, L0_attn_wv, L0_attn_wo,
           L0_ffn_base_w, L0_ffn_base_b, L0_ffn_A, L0_ffn_B, L0_ffn_router_w):
    x = np.asarray(L0_attn_x, np.float32)
    wq = np.asarray(L0_attn_wq, np.float32)
    wk = np.asarray(L0_attn_wk, np.float32)
    wv = np.asarray(L0_attn_wv, np.float32)
    wo = np.asarray(L0_attn_wo, np.float32)
    base_w = np.asarray(L0_ffn_base_w, np.float32)
    base_b = np.asarray(L0_ffn_base_b, np.float32)
    A = np.asarray(L0_ffn_A, np.float32)
    Bx = np.asarray(L0_ffn_B, np.float32)
    router_w = np.asarray(L0_ffn_router_w, np.float32)

    WQ = np.ascontiguousarray(wq.reshape(32, 128, 32, 128).transpose(1, 2, 0, 3))
    WK = np.ascontiguousarray(wk.reshape(32, 128, 8, 128).transpose(1, 2, 0, 3))
    WV = np.ascontiguousarray(wv.reshape(32, 128, 8, 128).transpose(1, 2, 0, 3))
    WO = np.ascontiguousarray(wo.reshape(32, 128, 8, 512).transpose(1, 2, 0, 3))
    BW = np.ascontiguousarray(
        np.ascontiguousarray(base_w.T).reshape(32, 128, 8, 512).transpose(1, 2, 0, 3))
    Bst = np.ascontiguousarray(Bx.transpose(0, 2, 1).reshape(1024, 4096))
    BST = np.ascontiguousarray(Bst.reshape(8, 128, 8, 512).transpose(1, 2, 0, 3))
    ARTf = np.concatenate([A.T, router_w.T], axis=1)
    ART = np.ascontiguousarray(ARTf.reshape(32, 128, 80).transpose(1, 0, 2))
    bias_rep = np.ascontiguousarray(np.broadcast_to(base_b[None, :], (128, 4096)))

    freqs = (np.float32(1.0) /
             np.power(np.float32(10000.0),
                      np.arange(0, HD, 2, dtype=np.float32) / np.float32(HD))
             ).astype(np.float32)
    tpos = np.arange(S, dtype=np.float32)
    ang = np.outer(tpos, freqs).astype(np.float32)
    cosr = np.cos(ang).astype(np.float32)
    sinr = np.sin(ang).astype(np.float32)
    CKV = np.ascontiguousarray(np.repeat(cosr.T, 2, axis=0))
    SKV = np.empty((HD, S), np.float32)
    SKV[0::2] = -sinr.T
    SKV[1::2] = sinr.T

    tri_add = np.where(np.arange(128)[:, None] <= np.arange(128)[None, :],
                       np.float32(0.0), np.float32(-1e9))
    MSK_role = []
    for role in range(2):
        m = np.zeros((2, 128, 128), BFD)
        if role == 0:
            m[0] = tri_add.astype(BFD)
            m[1] = BFD(-1e9)
        else:
            m[1] = tri_add.astype(BFD)
        MSK_role.append(m)

    ones_col_bf = np.ones((128, 1), BFD)
    ones_row_f32 = np.ones((1, 128), np.float32)
    epscol = np.full((128, 1), EPS, np.float32)

    if "nc" not in _NC_CACHE:
        _NC_CACHE["nc"] = build_nc()
    nc = _NC_CACHE["nc"]

    in_maps = []
    for core in range(8):
        b = core // 2
        role = core % 2
        xb = x[b]
        x_own = np.ascontiguousarray(xb.reshape(32, 128, D)[role::2].reshape(TOWN, D))
        own_pos = ((np.arange(NSLOT) * 2 + role)[:, None] * 128 +
                   np.arange(128)[None, :]).reshape(-1)
        COWN = np.ascontiguousarray(CKV[:, own_pos])
        SOWN = np.ascontiguousarray(SKV[:, own_pos])
        rm0 = np.full((128, 1), 1.0 if role == 0 else 0.0, np.float32)
        rm1 = np.full((128, 1), 1.0 if role == 1 else 0.0, np.float32)
        in_maps.append(dict(
            x_own=x_own, rmask0=rm0, rmask1=rm1,
            WQ=WQ, WK=WK, WV=WV, WO=WO, BW=BW, BST=BST, ART=ART,
            bias_rep=bias_rep, COWN=COWN, SOWN=SOWN,
            MSK=MSK_role[role], ones_col_bf=ones_col_bf,
            ones_row_f32=ones_row_f32, epscol=epscol))

    res = run_bass_kernel_spmd(nc, in_maps, core_ids=list(range(8)))

    out = np.empty((4, S, D), np.float32)
    for core in range(8):
        b = core // 2
        role = core % 2
        ob = np.asarray(res.results[core]["out_own"], np.float32)
        out[b].reshape(32, 128, D)[role::2] = ob.reshape(NSLOT, 128, D)
    return out


# version 20
# speedup vs baseline: 1.0740x; 1.0740x over previous
from contextlib import ExitStack
import numpy as np
import ml_dtypes

import concourse.bass as bass
import concourse.mybir as mybir
from concourse.bass_utils import run_bass_kernel_spmd

F32 = mybir.dt.float32
BF16 = mybir.dt.bfloat16
AF = mybir.ActivationFunctionType
OP = mybir.AluOpType
AX = mybir.AxisListType.X
BFD = ml_dtypes.bfloat16

D = 4096
S = 4096
TOWN = 2048
NH = 32
NKV = 8
HD = 128
NSLOT = 16
RSQ = float(np.float32(1.0) / np.float32(np.sqrt(np.float32(128.0))))
EPS = 1e-5
SWAP_MASK = [x for i in range(16) for x in (2 * i + 1, 2 * i)]


def build_nc():
    nc = bass.Bass("TRN2", target_bir_lowering=False)

    def inp(name, shape, dt=F32):
        return nc.dram_tensor(name, shape, dt, kind="ExternalInput").ap()

    x_own = inp("x_own", [TOWN, D])
    x_kv = inp("x_kv", [S, D])
    WQ = inp("WQ", [128, 32, 32, 128])
    WK = inp("WK", [128, 8, 32, 128])
    WV = inp("WV", [128, 8, 32, 128])
    WO = inp("WO", [128, 8, 32, 512])
    BW = inp("BW", [128, 8, 32, 512])
    BST = inp("BST", [128, 8, 8, 512])
    ART = inp("ART", [128, 32, 80])
    bias_rep = inp("bias_rep", [128, D])
    CKV = inp("CKV", [128, S])
    SKV = inp("SKV", [128, S])
    COWN = inp("COWN", [128, TOWN])
    SOWN = inp("SOWN", [128, TOWN])
    MSK = inp("MSK", [2, 128, 128], BF16)
    ones_col_bf = inp("ones_col_bf", [128, 1], BF16)
    ones_row_f32 = inp("ones_row_f32", [1, 128])
    epscol_in = inp("epscol", [128, 1])

    xn_own_h = nc.dram_tensor("xn_own_h", [TOWN, D], BF16).ap()
    xn_kv_h = nc.dram_tensor("xn_kv_h", [S, D], BF16).ap()
    KrT_h = nc.dram_tensor("KrT_h", [NKV, 128, S], BF16).ap()
    VT_h = nc.dram_tensor("VT_h", [NKV, 128, S], BF16).ap()
    Q_h = nc.dram_tensor("Q_h", [NH, 128, TOWN], BF16).ap()
    Wgt_h = nc.dram_tensor("Wgt_h", [TOWN, 1024], BF16).ap()
    G_h = nc.dram_tensor("G_h", [TOWN, D], F32).ap()
    AVT_h = nc.dram_tensor("AVT_h", [NH, 128, TOWN], BF16).ap()

    out_own = nc.dram_tensor("out_own", [TOWN, D], F32, kind="ExternalOutput").ap()

    CNT = dict(ld=0, st=0, g=0, t=0, a=0, v=0)

    es = ExitStack()
    with es:
        SEM = {k: es.enter_context(nc.semaphore(k)) for k in CNT}

        def incd(instr, s):
            CNT[s] += 16
            instr.then_inc(SEM[s], 16)
            return CNT[s]

        def inc1(instr, s):
            CNT[s] += 1
            instr.then_inc(SEM[s], 1)
            return CNT[s]

        s_eps = es.enter_context(nc.sbuf_tensor("s_eps", [128, 1], F32))
        s_ones_bf = es.enter_context(nc.sbuf_tensor("s_ones_bf", [128, 1], BF16))
        s_ones_f = es.enter_context(nc.sbuf_tensor("s_ones_f", [1, 128], F32))
        s_msk = es.enter_context(nc.sbuf_tensor("s_msk", [128, 256], BF16))
        s_kn2 = es.enter_context(nc.sbuf_tensor("s_kn2", [1, 8], F32))
        s_qn2 = es.enter_context(nc.sbuf_tensor("s_qn2", [1, 32], F32))
        s_nmall = es.enter_context(nc.sbuf_tensor("s_nmall", [128, 8], F32))

        with nc.Block() as blk:
            @blk.sync
            def _(sync):
                incd(sync.dma_start(out=s_eps[:], in_=epscol_in), "ld")
                incd(sync.dma_start(out=s_ones_bf[:], in_=ones_col_bf), "ld")
                incd(sync.dma_start(out=s_ones_f[:], in_=ones_row_f32), "ld")
                incd(sync.dma_start(
                    out=s_msk[:].rearrange("p (m q) -> p m q", m=2),
                    in_=MSK.rearrange("m p q -> p m q")), "ld")
                sync.wait_ge(SEM["ld"], CNT["ld"])

            @blk.vector
            def _(vector):
                vector.memset(s_kn2[:], 0.0)
                inc1(vector.memset(s_qn2[:], 0.0), "v")

        NT0 = 16 + 32

        def p0_src(i):
            if i < 16:
                return (x_own[i * 128:(i + 1) * 128, :],
                        xn_own_h[i * 128:(i + 1) * 128, :])
            j = i - 16
            return (x_kv[j * 128:(j + 1) * 128, :],
                    xn_kv_h[j * 128:(j + 1) * 128, :])

        ld0 = CNT["ld"]; st0 = CNT["st"]; a0 = CNT["a"]; v0 = CNT["v"]

        def a_rstd(i): return a0 + 2 * i + 1
        def a_id(i): return a0 + 2 * i + 2
        def v_var(i): return v0 + 2 * i + 1
        def v_nmr(i): return v0 + 2 * i + 2
        def ld_x(i): return ld0 + 16 * (i + 1)
        def st_xn(i): return st0 + 16 * (i + 1)

        with ExitStack() as ps:
            xb = [ps.enter_context(nc.sbuf_tensor(f"p0xb{i}", [128, D], F32)) for i in range(2)]
            xnb = [ps.enter_context(nc.sbuf_tensor(f"p0xn{i}", [128, D], BF16)) for i in range(2)]
            sstat = ps.enter_context(nc.sbuf_tensor("p0st", [128, 48], F32))
            sbn = ps.enter_context(nc.sbuf_tensor("p0bn", [128, 2], F32))
            _cols = [ps.enter_context(nc.sbuf_tensor(f"p0c{i}", [128, 1], F32))
                     for i in range(4)]
            lnc, rstdc, nmrc, spc0 = [c[:] for c in _cols]

            with nc.Block() as blk:
                @blk.sync
                def _(sync):
                    for i in range(2):
                        src, _dst = p0_src(i)
                        incd(sync.dma_start(out=xb[i % 2][:], in_=src), "ld")
                    for i in range(NT0):
                        _src, dst = p0_src(i)
                        sync.wait_ge(SEM["a"], a_id(i))
                        incd(sync.dma_start(out=dst, in_=xnb[i % 2][:]), "st")
                        if i + 2 < NT0:
                            src2, _ = p0_src(i + 2)
                            incd(sync.dma_start(out=xb[i % 2][:], in_=src2), "ld")
                    sync.wait_ge(SEM["st"], CNT["st"])

                @blk.vector
                def _(vector):
                    for i in range(NT0):
                        vector.wait_ge(SEM["ld"], ld_x(i))
                        for c in range(8):
                            vector.bn_stats(sstat[:, c * 6:(c + 1) * 6],
                                            xb[i % 2][:, c * 512:(c + 1) * 512])
                        vector.tensor_copy(spc0, s_eps[:])
                        vector.bn_aggr(sbn[:], sstat[:])
                        inc1(vector.drain(), "v")
                        vector.wait_ge(SEM["a"], a_rstd(i))
                        vector.tensor_scalar(nmrc, sbn[:, 0:1], rstdc, -1.0,
                                             OP.mult, OP.mult)
                        inc1(vector.drain(), "v")

                @blk.scalar
                def _(scalar):
                    for i in range(NT0):
                        scalar.wait_ge(SEM["ld"], ld_x(i))
                        if i >= 2:
                            scalar.wait_ge(SEM["st"], st_xn(i - 2))
                        scalar.wait_ge(SEM["v"], v_var(i))
                        scalar.activation(lnc, sbn[:, 1:2], AF.Ln, bias=s_eps[:])
                        inc1(scalar.activation(rstdc, lnc, AF.Exp, scale=-0.5), "a")
                        scalar.wait_ge(SEM["v"], v_nmr(i))
                        inc1(scalar.activation(xnb[i % 2][:], xb[i % 2][:], AF.Identity,
                                               scale=rstdc, bias=nmrc), "a")

        with ExitStack() as ps:
            wres = ps.enter_context(nc.sbuf_tensor("p1kw", [128, 8 * 32 * 128], BF16))
            xbuf = [ps.enter_context(nc.sbuf_tensor(f"p1kx{i}", [128, 32 * 512], BF16))
                    for i in range(2)]
            s_c = ps.enter_context(nc.sbuf_tensor("p1kc", [128, S], F32))
            s_s = ps.enter_context(nc.sbuf_tensor("p1ks", [128, S], F32))
            qbf = [ps.enter_context(nc.sbuf_tensor(f"p1kq{i}", [128, 512], BF16))
                   for i in range(4)]
            rot = [ps.enter_context(nc.sbuf_tensor(f"p1kr{i}", [128, 512], BF16))
                   for i in range(4)]
            qsq = [ps.enter_context(nc.sbuf_tensor(f"p1kqs{i}", [128, 512], BF16))
                   for i in range(2)]
            nrow = [ps.enter_context(nc.sbuf_tensor(f"p1kn{i}", [1, 512], F32))
                    for i in range(2)]
            vncol = ps.enter_context(nc.sbuf_tensor("p1kvn", [1, 1], F32))
            tmp1 = ps.enter_context(nc.sbuf_tensor("p1kt1", [128, 512], F32))
            tmp2 = ps.enter_context(nc.sbuf_tensor("p1kt2", [128, 512], F32))
            swp = ps.enter_context(nc.sbuf_tensor("p1ksw", [128, 512], BF16))
            pk = [ps.enter_context(nc.psum_tensor(f"p1kp{i}", [128, 512], F32))
                  for i in range(4)]
            pN = [ps.enter_context(nc.psum_tensor(f"p1kpn{i}", [1, 512], F32))
                  for i in range(2)]

            t_b = CNT["t"]; a_b = CNT["a"]; v_b = CNT["v"]; st_b = CNT["st"]
            g_b = CNT["g"]; ld_b = CNT["ld"]
            NU = 64
            T_CHAIN = {}; T_N = {}; A_EVQ = {}; A_EVN = {}
            V_ROT = {}; V_SQ = {}; V_NM = {}; ST = {}
            tc = t_b
            for u in range(NU):
                tc += 1; T_CHAIN[u] = tc
                if u >= 1:
                    tc += 1; T_N[u - 1] = tc
            tc += 1; T_N[NU - 1] = tc
            ac = a_b
            for u in range(NU):
                ac += 1; A_EVQ[u] = ac
                if u >= 1:
                    ac += 1; A_EVN[u - 1] = ac
            ac += 1; A_EVN[NU - 1] = ac
            vc = v_b
            for u in range(NU):
                vc += 1; V_ROT[u] = vc
                if u >= 1:
                    vc += 1; V_NM[u - 1] = vc
                vc += 1; V_SQ[u] = vc
            vc += 1; V_NM[NU - 1] = vc
            stc = st_b
            for u in range(NU):
                stc += 16; ST[u] = stc
            def ld_grp(g_): return ld_b + 32 + 16 * 32 * (g_ + 1)

            with nc.Block() as blk:
                @blk.sync
                def _(sync):
                    incd(sync.dma_start(out=s_c[:], in_=CKV), "ld")
                    incd(sync.dma_start(out=s_s[:], in_=SKV), "ld")
                    for g_ in range(8):
                        if g_ >= 2:
                            sync.wait_ge(SEM["t"], T_CHAIN[(g_ - 2) * 8 + 7])
                        for k in range(32):
                            incd(sync.dma_start_transpose(
                                out=xbuf[g_ % 2][:, k * 512:(k + 1) * 512],
                                in_=xn_kv_h[g_ * 512:(g_ + 1) * 512,
                                            k * 128:(k + 1) * 128]), "ld")
                    sync.wait_ge(SEM["ld"], CNT["ld"])

                @blk.gpsimd
                def _(gpsimd):
                    incd(gpsimd.dma_start(
                        out=wres[:].rearrange("p (f k c) -> p f k c", f=8, k=32),
                        in_=WK), "g")
                    for u in range(NU):
                        g_, f = divmod(u, 8)
                        gpsimd.wait_ge(SEM["v"], V_ROT[u])
                        assert incd(gpsimd.dma_start(
                            out=KrT_h[f, :, g_ * 512:(g_ + 1) * 512],
                            in_=rot[u % 4][:]), "st") == ST[u]
                    gpsimd.wait_ge(SEM["st"], CNT["st"])
                g_cast = g_b + 16

                @blk.tensor
                def _(tensor):
                    tensor.wait_ge(SEM["g"], g_cast)
                    for u in range(NU):
                        g_, f = divmod(u, 8)
                        if f == 0:
                            tensor.wait_ge(SEM["ld"], ld_grp(g_))
                        if u >= 4:
                            tensor.wait_ge(SEM["a"], A_EVQ[u - 4])
                        for k in range(32):
                            inst = tensor.matmul(
                                pk[u % 4][:],
                                wres[:, (f * 32 + k) * 128:(f * 32 + k + 1) * 128],
                                xbuf[g_ % 2][:, k * 512:(k + 1) * 512],
                                start=(k == 0), stop=(k == 31))
                        assert inc1(inst, "t") == T_CHAIN[u]
                        if u >= 1:
                            up = u - 1
                            tensor.wait_ge(SEM["v"], V_SQ[up])
                            if up >= 2:
                                tensor.wait_ge(SEM["a"], A_EVN[up - 2])
                            assert inc1(tensor.matmul(pN[up % 2][:], s_ones_bf[:],
                                                      qsq[up % 2][:], start=True,
                                                      stop=True), "t") == T_N[up]
                    up = NU - 1
                    tensor.wait_ge(SEM["v"], V_SQ[up])
                    tensor.wait_ge(SEM["a"], A_EVN[up - 2])
                    assert inc1(tensor.matmul(pN[up % 2][:], s_ones_bf[:],
                                              qsq[up % 2][:], start=True,
                                              stop=True), "t") == T_N[up]

                @blk.scalar
                def _(scalar):
                    def evq(u):
                        scalar.wait_ge(SEM["t"], T_CHAIN[u])
                        if u >= 4:
                            scalar.wait_ge(SEM["v"], V_ROT[u - 4])
                        assert inc1(scalar.copy(qbf[u % 4][:], pk[u % 4][:]),
                                    "a") == A_EVQ[u]

                    def evn(u):
                        scalar.wait_ge(SEM["t"], T_N[u])
                        if u >= 2:
                            scalar.wait_ge(SEM["v"], V_NM[u - 2])
                        assert inc1(scalar.copy(nrow[u % 2][:], pN[u % 2][:]),
                                    "a") == A_EVN[u]

                    for u in range(NU):
                        evq(u)
                        if u >= 1:
                            evn(u - 1)
                    evn(NU - 1)

                @blk.vector
                def _(vector):
                    for u in range(NU):
                        g_, f = divmod(u, 8)
                        csl = s_c[:, g_ * 512:(g_ + 1) * 512]
                        ssl = s_s[:, g_ * 512:(g_ + 1) * 512]
                        vector.wait_ge(SEM["a"], A_EVQ[u])
                        vector.stream_shuffle(swp[:], qbf[u % 4][:], SWAP_MASK)
                        vector.tensor_tensor(tmp1[:], qbf[u % 4][:], csl, OP.mult)
                        vector.tensor_tensor(tmp2[:], swp[:], ssl, OP.mult)
                        if u >= 1:
                            vector.wait_ge(SEM["a"], A_EVN[u - 1])
                            vector.reduce_max(vncol[:], nrow[(u - 1) % 2][:], axis=AX)
                        else:
                            vector.tensor_copy(swp[:, 0:1], s_eps[:])
                        if u >= 4:
                            vector.wait_ge(SEM["st"], ST[u - 4])
                        assert inc1(vector.tensor_tensor(rot[u % 4][:], tmp1[:],
                                                         tmp2[:], OP.add),
                                    "v") == V_ROT[u]
                        if u >= 1:
                            _, pf = divmod(u - 1, 8)
                            nt = s_kn2[:, pf:pf + 1]
                            assert inc1(vector.tensor_tensor(nt, nt, vncol[:],
                                                             OP.max), "v") == V_NM[u - 1]
                        else:
                            vector.tensor_copy(swp[:, 0:1], s_eps[:])
                        if u >= 2:
                            vector.wait_ge(SEM["t"], T_N[u - 2])
                        assert inc1(vector.tensor_tensor(qsq[u % 2][:], rot[u % 4][:],
                                                         rot[u % 4][:], OP.mult),
                                    "v") == V_SQ[u]
                    up = NU - 1
                    _, pf = divmod(up, 8)
                    vector.wait_ge(SEM["a"], A_EVN[up])
                    vector.reduce_max(vncol[:], nrow[up % 2][:], axis=AX)
                    vector.tensor_copy(swp[:, 0:1], s_eps[:])
                    nt = s_kn2[:, pf:pf + 1]
                    assert inc1(vector.tensor_tensor(nt, nt, vncol[:],
                                                     OP.max), "v") == V_NM[up]

        with ExitStack() as ps:
            wres = ps.enter_context(nc.sbuf_tensor("p1vw", [128, 8 * 32 * 128], BF16))
            xbuf = [ps.enter_context(nc.sbuf_tensor(f"p1vx{i}", [128, 32 * 512], BF16))
                    for i in range(2)]
            vbf = [ps.enter_context(nc.sbuf_tensor(f"p1vq{i}", [128, 512], BF16))
                   for i in range(4)]
            pk = [ps.enter_context(nc.psum_tensor(f"p1vp{i}", [128, 512], F32))
                  for i in range(4)]

            t_b = CNT["t"]; a_b = CNT["a"]; st_b = CNT["st"]
            g_b = CNT["g"]; ld_b = CNT["ld"]
            NU = 64
            def tchain(u): return t_b + u + 1
            def aev(u): return a_b + u + 1
            def stv(u): return st_b + 16 * (u + 1)
            def ld_grp(g_): return ld_b + 16 * 32 * (g_ + 1)

            with nc.Block() as blk:
                @blk.sync
                def _(sync):
                    for g_ in range(8):
                        if g_ >= 2:
                            sync.wait_ge(SEM["t"], tchain((g_ - 2) * 8 + 7))
                        for k in range(32):
                            incd(sync.dma_start_transpose(
                                out=xbuf[g_ % 2][:, k * 512:(k + 1) * 512],
                                in_=xn_kv_h[g_ * 512:(g_ + 1) * 512,
                                            k * 128:(k + 1) * 128]), "ld")
                    sync.wait_ge(SEM["ld"], CNT["ld"])

                @blk.gpsimd
                def _(gpsimd):
                    incd(gpsimd.dma_start(
                        out=wres[:].rearrange("p (f k c) -> p f k c", f=8, k=32),
                        in_=WV), "g")
                    for u in range(NU):
                        g_, f = divmod(u, 8)
                        gpsimd.wait_ge(SEM["a"], aev(u))
                        assert incd(gpsimd.dma_start(
                            out=VT_h[f, :, g_ * 512:(g_ + 1) * 512],
                            in_=vbf[u % 4][:]), "st") == stv(u)
                    gpsimd.wait_ge(SEM["st"], CNT["st"])
                g_cast = g_b + 16

                @blk.tensor
                def _(tensor):
                    tensor.wait_ge(SEM["g"], g_cast)
                    for u in range(NU):
                        g_, f = divmod(u, 8)
                        if f == 0:
                            tensor.wait_ge(SEM["ld"], ld_grp(g_))
                        if u >= 4:
                            tensor.wait_ge(SEM["a"], aev(u - 4))
                        for k in range(32):
                            inst = tensor.matmul(
                                pk[u % 4][:],
                                wres[:, (f * 32 + k) * 128:(f * 32 + k + 1) * 128],
                                xbuf[g_ % 2][:, k * 512:(k + 1) * 512],
                                start=(k == 0), stop=(k == 31))
                        assert inc1(inst, "t") == tchain(u)

                @blk.scalar
                def _(scalar):
                    for u in range(NU):
                        scalar.wait_ge(SEM["t"], tchain(u))
                        if u >= 4:
                            scalar.wait_ge(SEM["st"], stv(u - 4))
                        assert inc1(scalar.copy(vbf[u % 4][:], pk[u % 4][:]),
                                    "a") == aev(u)

        with ExitStack() as span:
            with ExitStack() as ps:
                xnT = ps.enter_context(nc.sbuf_tensor("xnT", [128, 32 * TOWN], BF16))
                wqb = [ps.enter_context(nc.sbuf_tensor(f"wqb{i}", [128, 4096], BF16))
                       for i in range(2)]
                s_co = ps.enter_context(nc.sbuf_tensor("s_co", [128, TOWN], F32))
                s_so = ps.enter_context(nc.sbuf_tensor("s_so", [128, TOWN], F32))
                qbf = [ps.enter_context(nc.sbuf_tensor(f"qbf{i}", [128, 512], BF16))
                       for i in range(4)]
                rot = [ps.enter_context(nc.sbuf_tensor(f"qrot{i}", [128, 512], BF16))
                       for i in range(4)]
                qsq = [ps.enter_context(nc.sbuf_tensor(f"qsq{i}", [128, 512], BF16))
                       for i in range(2)]
                nrow = [ps.enter_context(nc.sbuf_tensor(f"qnr{i}", [1, 512], F32))
                        for i in range(2)]
                vncol = ps.enter_context(nc.sbuf_tensor("qvn", [1, 1], F32))
                tmp1 = ps.enter_context(nc.sbuf_tensor("qt1", [128, 512], F32))
                tmp2 = ps.enter_context(nc.sbuf_tensor("qt2", [128, 512], F32))
                swp = ps.enter_context(nc.sbuf_tensor("qsw", [128, 512], BF16))
                pk = [ps.enter_context(nc.psum_tensor(f"p2ap{i}", [128, 512], F32))
                      for i in range(4)]
                pN = [ps.enter_context(nc.psum_tensor(f"p2apn{i}", [1, 512], F32))
                      for i in range(2)]

                t_b = CNT["t"]; a_b = CNT["a"]; v_b = CNT["v"]; st_b = CNT["st"]
                g_b = CNT["g"]; ld_b = CNT["ld"]
                NU = 128
                T_CHAIN = {}; T_N = {}; A_EVQ = {}; A_EVN = {}
                V_ROT = {}; V_SQ = {}; V_NM = {}; ST = {}
                tc = t_b
                for u in range(NU):
                    tc += 1; T_CHAIN[u] = tc
                    if u >= 1:
                        tc += 1; T_N[u - 1] = tc
                tc += 1; T_N[NU - 1] = tc
                ac = a_b
                for u in range(NU):
                    ac += 1; A_EVQ[u] = ac
                    if u >= 1:
                        ac += 1; A_EVN[u - 1] = ac
                ac += 1; A_EVN[NU - 1] = ac
                vc = v_b
                for u in range(NU):
                    vc += 1; V_ROT[u] = vc
                    if u >= 1:
                        vc += 1; V_NM[u - 1] = vc
                    vc += 1; V_SQ[u] = vc
                vc += 1; V_NM[NU - 1] = vc
                stc = st_b
                for u in range(NU):
                    stc += 16; ST[u] = stc
                ld_xnT = ld_b + 16 * 2 + 16 * 128
                def g_wq(f): return g_b + 16 * (f + 1)

                with nc.Block() as blk:
                    @blk.sync
                    def _(sync):
                        incd(sync.dma_start(out=s_co[:], in_=COWN), "ld")
                        incd(sync.dma_start(out=s_so[:], in_=SOWN), "ld")
                        for k in range(32):
                            for gg in range(4):
                                incd(sync.dma_start_transpose(
                                    out=xnT[:, k * TOWN + gg * 512: k * TOWN + (gg + 1) * 512],
                                    in_=xn_own_h[gg * 512:(gg + 1) * 512,
                                                 k * 128:(k + 1) * 128]), "ld")
                        sync.wait_ge(SEM["ld"], CNT["ld"])

                    @blk.gpsimd
                    def _(gpsimd):
                        for f in range(2):
                            incd(gpsimd.dma_start(
                                out=wqb[f % 2][:].rearrange("p (k c) -> p k c", k=32),
                                in_=WQ[:, f]), "g")
                        for u in range(NU):
                            f, gg = divmod(u, 4)
                            if gg == 0 and f + 2 < 32:
                                gpsimd.wait_ge(SEM["t"], T_CHAIN[f * 4 + 3])
                                incd(gpsimd.dma_start(
                                    out=wqb[f % 2][:].rearrange("p (k c) -> p k c", k=32),
                                    in_=WQ[:, f + 2]), "g")
                            gpsimd.wait_ge(SEM["v"], V_ROT[u])
                            assert incd(gpsimd.dma_start(
                                out=Q_h[f, :, gg * 512:(gg + 1) * 512],
                                in_=rot[u % 4][:]), "st") == ST[u]
                        gpsimd.wait_ge(SEM["st"], CNT["st"])

                    @blk.tensor
                    def _(tensor):
                        tensor.wait_ge(SEM["ld"], ld_xnT)
                        for u in range(NU):
                            f, gg = divmod(u, 4)
                            if gg == 0:
                                tensor.wait_ge(SEM["g"], g_wq(f))
                            if u >= 4:
                                tensor.wait_ge(SEM["a"], A_EVQ[u - 4])
                            for k in range(32):
                                inst = tensor.matmul(
                                    pk[u % 4][:],
                                    wqb[f % 2][:, k * 128:(k + 1) * 128],
                                    xnT[:, k * TOWN + gg * 512: k * TOWN + (gg + 1) * 512],
                                    start=(k == 0), stop=(k == 31))
                            assert inc1(inst, "t") == T_CHAIN[u]
                            if u >= 1:
                                up = u - 1
                                tensor.wait_ge(SEM["v"], V_SQ[up])
                                if up >= 2:
                                    tensor.wait_ge(SEM["a"], A_EVN[up - 2])
                                assert inc1(tensor.matmul(pN[up % 2][:], s_ones_bf[:],
                                                          qsq[up % 2][:], start=True,
                                                          stop=True), "t") == T_N[up]
                        up = NU - 1
                        tensor.wait_ge(SEM["v"], V_SQ[up])
                        tensor.wait_ge(SEM["a"], A_EVN[up - 2])
                        assert inc1(tensor.matmul(pN[up % 2][:], s_ones_bf[:],
                                                  qsq[up % 2][:], start=True,
                                                  stop=True), "t") == T_N[up]

                    @blk.scalar
                    def _(scalar):
                        def evq(u):
                            scalar.wait_ge(SEM["t"], T_CHAIN[u])
                            if u >= 4:
                                scalar.wait_ge(SEM["v"], V_ROT[u - 4])
                            assert inc1(scalar.copy(qbf[u % 4][:], pk[u % 4][:]),
                                        "a") == A_EVQ[u]

                        def evn(u):
                            scalar.wait_ge(SEM["t"], T_N[u])
                            if u >= 2:
                                scalar.wait_ge(SEM["v"], V_NM[u - 2])
                            assert inc1(scalar.copy(nrow[u % 2][:], pN[u % 2][:]),
                                        "a") == A_EVN[u]

                        for u in range(NU):
                            evq(u)
                            if u >= 1:
                                evn(u - 1)
                        evn(NU - 1)

                    @blk.vector
                    def _(vector):
                        for u in range(NU):
                            f, gg = divmod(u, 4)
                            csl = s_co[:, gg * 512:(gg + 1) * 512]
                            ssl = s_so[:, gg * 512:(gg + 1) * 512]
                            vector.wait_ge(SEM["a"], A_EVQ[u])
                            vector.stream_shuffle(swp[:], qbf[u % 4][:], SWAP_MASK)
                            vector.tensor_tensor(tmp1[:], qbf[u % 4][:], csl, OP.mult)
                            vector.tensor_tensor(tmp2[:], swp[:], ssl, OP.mult)
                            if u >= 1:
                                vector.wait_ge(SEM["a"], A_EVN[u - 1])
                                vector.reduce_max(vncol[:], nrow[(u - 1) % 2][:], axis=AX)
                            else:
                                vector.tensor_copy(swp[:, 0:1], s_eps[:])
                            if u >= 4:
                                vector.wait_ge(SEM["st"], ST[u - 4])
                            assert inc1(vector.tensor_tensor(rot[u % 4][:], tmp1[:],
                                                             tmp2[:], OP.add),
                                        "v") == V_ROT[u]
                            if u >= 1:
                                pf, _pgg = divmod(u - 1, 4)
                                nt = s_qn2[:, pf:pf + 1]
                                assert inc1(vector.tensor_tensor(nt, nt, vncol[:],
                                                                 OP.max), "v") == V_NM[u - 1]
                            else:
                                vector.tensor_copy(swp[:, 0:1], s_eps[:])
                            if u >= 2:
                                vector.wait_ge(SEM["t"], T_N[u - 2])
                            assert inc1(vector.tensor_tensor(qsq[u % 2][:], rot[u % 4][:],
                                                             rot[u % 4][:], OP.mult),
                                        "v") == V_SQ[u]
                        up = NU - 1
                        pf, _pgg = divmod(up, 4)
                        vector.wait_ge(SEM["a"], A_EVN[up])
                        vector.reduce_max(vncol[:], nrow[up % 2][:], axis=AX)
                        vector.tensor_copy(swp[:, 0:1], s_eps[:])
                        nt = s_qn2[:, pf:pf + 1]
                        assert inc1(vector.tensor_tensor(nt, nt, vncol[:],
                                                         OP.max), "v") == V_NM[up]

                t_b = CNT["t"]; a_b = CNT["a"]; v_b = CNT["v"]; st_b = CNT["st"]
                g_b = CNT["g"]
                def t_rl(tt): return t_b + tt + 1
                def a_ex(tt): return a_b + tt + 1
                def v_pre(tt): return v_b + tt * 3 + 1
                def v_cmb(tt): return v_b + tt * 3 + 2
                def v_wg(tt): return v_b + tt * 3 + 3
                def st_wg(tt): return st_b + 16 * (tt + 1)
                g_art = g_b + 16

                with ExitStack() as ps2:
                    artb = ps2.enter_context(nc.sbuf_tensor("artb", [128, 32 * 80], BF16))
                    rh_all = ps2.enter_context(nc.sbuf_tensor("rh_all", [128, 1024], F32))
                    unorm = [ps2.enter_context(nc.sbuf_tensor(f"unorm{i}", [128, 16], F32))
                             for i in range(2)]
                    selb = ps2.enter_context(nc.sbuf_tensor("selb", [128, 16], F32))
                    comb = ps2.enter_context(nc.sbuf_tensor("comb", [128, 16], F32))
                    m8 = ps2.enter_context(nc.sbuf_tensor("m8", [128, 8], F32))
                    _rc = [ps2.enter_context(nc.sbuf_tensor(f"p2brc{i}", [128, 1], F32))
                           for i in range(8)]
                    mxc, nmxc, Sc, dnc, rfc, s1c, spcc, _x = [c[:] for c in _rc]
                    wgt = [ps2.enter_context(nc.sbuf_tensor(f"wgt{i}", [128, 1024], BF16))
                           for i in range(2)]
                    prl = [ps2.enter_context(nc.psum_tensor(f"prl{i}", [128, 80], F32))
                           for i in range(2)]

                    with nc.Block() as blk:
                        @blk.gpsimd
                        def _(gpsimd):
                            incd(gpsimd.dma_start(
                                out=artb[:].rearrange("p (k c) -> p k c", k=32),
                                in_=ART), "g")

                        @blk.tensor
                        def _(tensor):
                            tensor.wait_ge(SEM["g"], g_art)
                            for tt in range(16):
                                if tt >= 2:
                                    tensor.wait_ge(SEM["v"], v_cmb(tt - 2))
                                for k in range(32):
                                    inst = tensor.matmul(
                                        prl[tt % 2][:],
                                        xnT[:, k * TOWN + tt * 128: k * TOWN + (tt + 1) * 128],
                                        artb[:, k * 80:(k + 1) * 80],
                                        start=(k == 0), stop=(k == 31))
                                assert inc1(inst, "t") == t_rl(tt)

                        @blk.vector
                        def _(vector):
                            for tt in range(16):
                                vector.wait_ge(SEM["t"], t_rl(tt))
                                vector.reduce_max(mxc, prl[tt % 2][:, 64:80], axis=AX)
                                vector.tensor_copy(rh_all[:, tt * 64:(tt + 1) * 64],
                                                   prl[tt % 2][:, 0:64])
                                vector.tensor_scalar(nmxc, mxc, -1.0, None, OP.mult)
                                assert inc1(vector.drain(), "v") == v_pre(tt)
                                vector.wait_ge(SEM["a"], a_ex(tt))
                                vector.max(m8[:], unorm[tt % 2][:])
                                vector.tensor_scalar(dnc, Sc, 1e-6, None, OP.mult)
                                vector.tensor_tensor(s1c, m8[:, 0:1], m8[:, 1:2], OP.add)
                                vector.tensor_scalar(selb[:], unorm[tt % 2][:],
                                                     m8[:, 1:2], None, OP.is_ge)
                                vector.tensor_tensor(dnc, dnc, s1c, OP.add)
                                vector.tensor_copy(spcc, s_eps[:])
                                vector.reciprocal(rfc, dnc)
                                vector.tensor_copy(spcc, s_eps[:])
                                assert inc1(vector.scalar_tensor_tensor(
                                    comb[:], unorm[tt % 2][:], rfc, selb[:],
                                    OP.mult, OP.mult), "v") == v_cmb(tt)
                                if tt >= 2:
                                    vector.wait_ge(SEM["st"], st_wg(tt - 2))
                                vector.tensor_copy(spcc, s_eps[:])
                                for e in range(16):
                                    inst = vector.tensor_scalar(
                                        wgt[tt % 2][:, e * 64:(e + 1) * 64],
                                        rh_all[:, tt * 64:(tt + 1) * 64],
                                        comb[:, e:e + 1], 0.5, OP.mult, OP.mult)
                                assert inc1(inst, "v") == v_wg(tt)

                        @blk.scalar
                        def _(scalar):
                            for tt in range(16):
                                scalar.wait_ge(SEM["v"], v_pre(tt))
                                assert inc1(scalar.activation(
                                    unorm[tt % 2][:], prl[tt % 2][:, 64:80], AF.Exp,
                                    bias=nmxc, accum_out=Sc), "a") == a_ex(tt)

                        @blk.sync
                        def _(sync):
                            for tt in range(16):
                                sync.wait_ge(SEM["v"], v_wg(tt))
                                assert incd(sync.dma_start(
                                    out=Wgt_h[tt * 128:(tt + 1) * 128, :],
                                    in_=wgt[tt % 2][:]), "st") == st_wg(tt)
                            sync.wait_ge(SEM["st"], CNT["st"])

            t_b = CNT["t"]; a_b = CNT["a"]; v_b = CNT["v"]; st_b = CNT["st"]
            g_b = CNT["g"]; ld_b = CNT["ld"]

            def t_g(lin): return t_b + lin + 1 + (2 if lin >= 56 else 0)
            def t_xf(half): return t_b + 57 + half
            def a_silu(lin): return a_b + lin + 1
            def v_gadd(lin): return v_b + lin + 1
            def st_g(lin): return st_b + 16 * (lin + 1)
            ld_w0 = ld_b + 16 * 72
            def ld_w1(half): return ld_b + 16 * (72 + 32 * (half + 1))
            g_bias = g_b + 16
            def g_bw(idx, half): return g_bias + 16 * (idx * 3 + half + 1)
            def g_bst(idx): return g_bias + 16 * (idx * 3 + 3)

            with ExitStack() as ps:
                wgtT = ps.enter_context(nc.sbuf_tensor("wgtT", [128, 8 * TOWN], BF16))
                xnw = ps.enter_context(nc.sbuf_tensor("xnw", [128, 32 * 1024], BF16))
                bwb = [ps.enter_context(nc.sbuf_tensor(f"bwb{i}", [128, 16 * 512], BF16))
                       for i in range(4)]
                bstb = [ps.enter_context(nc.sbuf_tensor(f"bstb{i}", [128, 8 * 512], BF16))
                        for i in range(2)]
                biasB = ps.enter_context(nc.sbuf_tensor("biasB", [128, D], BF16))
                gtmp = [ps.enter_context(nc.sbuf_tensor(f"gtmp{i}", [128, 512], F32))
                        for i in range(4)]
                gout = [ps.enter_context(nc.sbuf_tensor(f"gout{i}", [128, 512], F32))
                        for i in range(4)]

                with nc.Block() as blk:
                    @blk.sync
                    def _(sync):
                        for ek in range(8):
                            incd(sync.dma_start_transpose(
                                out=wgtT[:, ek * TOWN:(ek + 1) * TOWN],
                                in_=Wgt_h[:, ek * 128:(ek + 1) * 128]), "ld")

                    @blk.gpsimd
                    def _(gpsimd):
                        assert incd(gpsimd.dma_start(out=biasB[:], in_=bias_rep),
                                    "g") == g_bias

                with ExitStack() as psm:
                    ptt = [psm.enter_context(nc.psum_tensor(f"pttA{i}", [128, 512], F32))
                           for i in range(8)]
                    with nc.Block() as blk:
                        @blk.sync
                        def _(sync):
                            def xnw_load(wave, k, hh):
                                tok0 = wave * 1024 + hh * 512
                                incd(sync.dma_start_transpose(
                                    out=xnw[:, k * 1024 + hh * 512:
                                            k * 1024 + (hh + 1) * 512],
                                    in_=xn_own_h[tok0:tok0 + 512,
                                                 k * 128:(k + 1) * 128]), "ld")

                            def g_store(lin):
                                idx, tt = divmod(lin, 8)
                                wave, fc = divmod(idx, 8)
                                tg = wave * 8 + tt
                                sync.wait_ge(SEM["a"], a_silu(lin))
                                assert incd(sync.dma_start(
                                    out=G_h[tg * 128:(tg + 1) * 128,
                                            fc * 512:(fc + 1) * 512],
                                    in_=gout[lin % 4][:]), "st") == st_g(lin)

                            for k in range(32):
                                for hh in range(2):
                                    xnw_load(0, k, hh)
                            for lin in range(56):
                                g_store(lin)
                            for half in range(2):
                                sync.wait_ge(SEM["t"], t_xf(half))
                                for k16 in range(16):
                                    for hh in range(2):
                                        xnw_load(1, half * 16 + k16, hh)
                            for lin in range(56, 128):
                                g_store(lin)
                            sync.wait_ge(SEM["st"], CNT["st"])

                        @blk.gpsimd
                        def _(gpsimd):
                            for idx in range(16):
                                wave, fc = divmod(idx, 8)
                                if idx >= 2:
                                    gpsimd.wait_ge(SEM["t"], t_g((idx - 2) * 8 + 7))
                                for half in range(2):
                                    incd(gpsimd.dma_start(
                                        out=bwb[(idx % 2) * 2 + half][:]
                                        .rearrange("p (k c) -> p k c", k=16),
                                        in_=BW[:, fc, half * 16:(half + 1) * 16]), "g")
                                incd(gpsimd.dma_start(
                                    out=bstb[idx % 2][:].rearrange("p (k c) -> p k c", k=8),
                                    in_=BST[:, fc]), "g")

                        @blk.tensor
                        def _(tensor):
                            for idx in range(16):
                                wave, fc = divmod(idx, 8)
                                if idx == 0:
                                    tensor.wait_ge(SEM["ld"], ld_w0)
                                for half in range(2):
                                    if idx == 8:
                                        tensor.wait_ge(SEM["ld"], ld_w1(half))
                                    tensor.wait_ge(SEM["g"], g_bw(idx, half))
                                    for tt in range(8):
                                        if half == 0 and idx > 0:
                                            tensor.wait_ge(
                                                SEM["v"], v_gadd((idx - 1) * 8 + tt))
                                        for k16 in range(16):
                                            k = half * 16 + k16
                                            inst = tensor.matmul(
                                                ptt[tt][:],
                                                xnw[:, k * 1024 + tt * 128:
                                                    k * 1024 + (tt + 1) * 128],
                                                bwb[(idx % 2) * 2 + half][:,
                                                    k16 * 512:(k16 + 1) * 512],
                                                start=(half == 0 and k16 == 0),
                                                stop=False,
                                                skip_group_check=True)
                                    if idx == 7:
                                        assert inc1(inst, "t") == t_xf(half)
                                tensor.wait_ge(SEM["g"], g_bst(idx))
                                for tt in range(8):
                                    tg = wave * 8 + tt
                                    for ek in range(8):
                                        inst = tensor.matmul(
                                            ptt[tt][:],
                                            wgtT[:, ek * TOWN + tg * 128:
                                                 ek * TOWN + (tg + 1) * 128],
                                            bstb[idx % 2][:, ek * 512:(ek + 1) * 512],
                                            start=False, stop=(ek == 7),
                                            skip_group_check=True)
                                    assert inc1(inst, "t") == t_g(idx * 8 + tt)

                        @blk.vector
                        def _(vector):
                            vector.wait_ge(SEM["g"], g_bias)
                            for idx in range(16):
                                wave, fc = divmod(idx, 8)
                                for tt in range(8):
                                    lin = idx * 8 + tt
                                    vector.wait_ge(SEM["t"], t_g(lin))
                                    if lin >= 4:
                                        vector.wait_ge(SEM["a"], a_silu(lin - 4))
                                    assert inc1(vector.tensor_tensor(
                                        gtmp[lin % 4][:], ptt[tt][:],
                                        biasB[:, fc * 512:(fc + 1) * 512],
                                        OP.add), "v") == v_gadd(lin)

                        @blk.scalar
                        def _(scalar):
                            for idx in range(16):
                                for tt in range(8):
                                    lin = idx * 8 + tt
                                    scalar.wait_ge(SEM["v"], v_gadd(lin))
                                    if lin >= 4:
                                        scalar.wait_ge(SEM["st"], st_g(lin - 4))
                                    assert inc1(scalar.activation(
                                        gout[lin % 4][:], gtmp[lin % 4][:],
                                        AF.Silu), "a") == a_silu(lin)

        ld_b = CNT["ld"]; t_b = CNT["t"]; a_b = CNT["a"]; v_b = CNT["v"]
        st_b = CNT["st"]

        def e_of(j): return 2 * j + 2

        def stream_segs():
            segs = []
            for j in range(NSLOT):
                segs.append(("INIT", j))
                for c0 in range(0, e_of(j), 2):
                    segs.append(("CHUNK", j, c0))
            segs.append(("RFIN",))
            return segs

        SEGS1 = stream_segs()
        MERGED = []
        for sa, sb in zip(SEGS1, SEGS1):
            MERGED.append((0,) + sa)
            MERGED.append((1,) + sb)

        TPOSr = {}; APOSr = {}; VPOSr = {}; STr = {}
        tc = ac = vc = stc = 0
        for seg in MERGED:
            s = seg[0]; kind = seg[1]
            if kind == "INIT":
                j = seg[2]
                tc += 1; TPOSr[(s, "S", j, 1)] = tc
                if j >= 1:
                    tc += 1; TPOSr[(s, "R", j - 1)] = tc
                    ac += 1; APOSr[(s, "RC", j - 1)] = ac
                    vc += 1; VPOSr[(s, "REC", j - 1)] = vc
            elif kind == "CHUNK":
                j = seg[2]; c0 = seg[3]; e = e_of(j)
                if c0 + 3 < e:
                    tc += 1; TPOSr[(s, "S", j, c0 + 3)] = tc
                if c0 == e - 2:
                    vc += 2; VPOSr[(s, "M", j)] = vc
                ac += 1; APOSr[(s, "E", j, c0)] = ac
                if c0 + 1 == e - 1:
                    tc += 1; TPOSr[(s, "L", j, e - 1)] = tc
                if c0 == 0 and j >= 1:
                    vc += 1; VPOSr[(s, "EV", j - 1)] = vc
                    stc += 16; STr[(s, j - 1)] = stc
            else:
                tc += 1; TPOSr[(s, "R", NSLOT - 1)] = tc
                ac += 1; APOSr[(s, "RC", NSLOT - 1)] = ac
                vc += 1; VPOSr[(s, "REC", NSLOT - 1)] = vc
                vc += 1; VPOSr[(s, "EV", NSLOT - 1)] = vc
                stc += 16; STr[(s, NSLOT - 1)] = stc
        TPK, APK, VPK, STK = tc, ac, vc, stc

        t_nm0 = t_b; a_nm0 = a_b; v_nm0 = v_b
        t_m = t_b + NKV
        a_m = a_b + 3 * NKV
        v_m = v_b + 4 * NKV

        def pair_of(kvh): return kvh // 2, kvh % 2
        def t_op(kvh, kind, j, t_=None):
            p, s = pair_of(kvh)
            key = (s, kind, j) if kind == "R" else (s, kind, j, t_)
            return t_m + p * TPK + TPOSr[key]
        def a_exp(kvh, j, c0):
            p, s = pair_of(kvh)
            return a_m + p * APK + APOSr[(s, "E", j, c0)]
        def a_rc(kvh, j):
            p, s = pair_of(kvh)
            return a_m + p * APK + APOSr[(s, "RC", j)]
        def v_of(kvh, kind, j):
            p, s = pair_of(kvh)
            return v_m + p * VPK + VPOSr[(s, kind, j)]
        def st_avt(kvh, j):
            p, s = pair_of(kvh)
            return st_b + p * STK + STr[(s, j)]
        def ld_kvh(kvh): return ld_b + 16 * 37 * (kvh + 1)
        def t_pair_end(p): return t_m + (p + 1) * TPK

        with ExitStack() as ps:
            krb = [ps.enter_context(nc.sbuf_tensor(f"krb{i}", [128, S], BF16)) for i in range(3)]
            vtb = [ps.enter_context(nc.sbuf_tensor(f"vtb{i}", [128, S], BF16)) for i in range(3)]
            qkb = [ps.enter_context(nc.sbuf_tensor(f"qkb{i}", [128, 8192], BF16)) for i in range(3)]
            prb = [ps.enter_context(nc.sbuf_tensor(f"prb{i}", [128, 32 * 512], BF16)) for i in range(2)]
            rsb = [ps.enter_context(nc.sbuf_tensor(f"rsb{i}", [128, 512], F32)) for i in range(2)]
            rrow = [ps.enter_context(nc.sbuf_tensor(f"rrow{i}", [1, 512], F32)) for i in range(2)]
            nmsc = ps.enter_context(nc.sbuf_tensor("nmsc", [1, 8], F32))
            avst = [ps.enter_context(nc.sbuf_tensor(f"avst{i}", [128, 512], BF16)) for i in range(2)]
            pS4 = ps.enter_context(nc.psum_tensor("pS4", [128, 2048], F32))
            pLb = [ps.enter_context(nc.psum_tensor(f"pLb{i}", [128, 512], F32)) for i in range(2)]
            pAV = [ps.enter_context(nc.psum_tensor(f"pAV{i}", [128, 512], F32)) for i in range(2)]

            def psl(s, t_):
                b = s * 1024 + (t_ % 2) * 512
                return pS4[:, b:b + 512]

            with nc.Block() as blk:
                @blk.vector
                def _(vector):
                    for kvh in range(NKV):
                        assert inc1(vector.reduce_max(
                            nmsc[:, 0:1], s_qn2[:, 4 * kvh:4 * kvh + 4], axis=AX),
                            "v") == v_nm0 + 4 * kvh + 1
                        vector.wait_ge(SEM["a"], a_nm0 + 3 * kvh + 2)
                        assert inc1(vector.tensor_tensor(
                            nmsc[:, 2:3], nmsc[:, 3:4], nmsc[:, 4:5], OP.add),
                            "v") == v_nm0 + 4 * kvh + 2
                        vector.wait_ge(SEM["a"], a_nm0 + 3 * kvh + 3)
                        vector.tensor_scalar(
                            nmsc[:, 7:8], nmsc[:, 6:7], -1.0001 * RSQ, -0.01,
                            OP.mult, OP.add)
                        assert inc1(vector.drain(), "v") == v_nm0 + 4 * kvh + 3
                        vector.wait_ge(SEM["t"], t_nm0 + kvh + 1)
                        vector.tensor_copy(s_nmall[:, kvh:kvh + 1], pLb[0][:, 0:1])
                        assert inc1(vector.drain(), "v") == v_nm0 + 4 * kvh + 4
                    for p in range(4):
                        for seg in MERGED:
                            s = seg[0]; kvh = 2 * p + s; kind = seg[1]
                            if kind == "INIT":
                                j = seg[2]
                                if j >= 1:
                                    vector.wait_ge(SEM["t"], t_op(kvh, "L", j - 1, e_of(j - 1) - 1))
                                    vector.reciprocal(rrow[s][:], pLb[s][0:1, :])
                                    assert inc1(vector.drain(), "v") == v_of(kvh, "REC", j - 1)
                            elif kind == "CHUNK":
                                j = seg[2]; c0 = seg[3]; e = e_of(j)
                                if c0 == 0 and j >= 1:
                                    vector.wait_ge(SEM["t"], t_op(kvh, "R", j - 1))
                                    vector.wait_ge(SEM["a"], a_rc(kvh, j - 1))
                                    if j >= 2:
                                        vector.wait_ge(SEM["st"], st_avt(kvh, j - 2))
                                    assert inc1(vector.tensor_tensor(
                                        avst[s][:], pAV[s][:], rsb[s][:], OP.mult),
                                        "v") == v_of(kvh, "EV", j - 1)
                                if c0 == e - 2:
                                    vector.wait_ge(SEM["t"], t_op(kvh, "S", j, e - 1))
                                    for mi, t_ in enumerate((e - 2, e - 1)):
                                        sl = psl(s, t_).rearrange("p (h q) -> p h q", h=4)
                                        mk = s_msk[:, mi * 128:(mi + 1) * 128].unsqueeze(1) \
                                            .broadcast_to([128, 4, 128])
                                        vector.tensor_tensor(sl, sl, mk, OP.add)
                                    CNT["v"] += 2
                                    vector.drain().then_inc(SEM["v"], 2)
                                    assert CNT["v"] == v_of(kvh, "M", j)
                            else:
                                jl = NSLOT - 1
                                vector.wait_ge(SEM["t"], t_op(kvh, "L", jl, e_of(jl) - 1))
                                vector.reciprocal(rrow[s][:], pLb[s][0:1, :])
                                assert inc1(vector.drain(), "v") == v_of(kvh, "REC", jl)
                                vector.wait_ge(SEM["t"], t_op(kvh, "R", jl))
                                vector.wait_ge(SEM["a"], a_rc(kvh, jl))
                                vector.wait_ge(SEM["st"], st_avt(kvh, jl - 1))
                                assert inc1(vector.tensor_tensor(
                                    avst[s][:], pAV[s][:], rsb[s][:], OP.mult),
                                    "v") == v_of(kvh, "EV", jl)

                @blk.scalar
                def _(scalar):
                    for kvh in range(NKV):
                        scalar.wait_ge(SEM["v"], v_nm0 + 4 * kvh + 1)
                        assert inc1(scalar.activation(nmsc[:, 3:4], nmsc[:, 0:1],
                                                      AF.Ln), "a") == a_nm0 + 3 * kvh + 1
                        assert inc1(scalar.activation(nmsc[:, 4:5], s_kn2[:, kvh:kvh + 1],
                                                      AF.Ln), "a") == a_nm0 + 3 * kvh + 2
                        scalar.wait_ge(SEM["v"], v_nm0 + 4 * kvh + 2)
                        assert inc1(scalar.activation(nmsc[:, 6:7], nmsc[:, 2:3],
                                                      AF.Exp, scale=0.5),
                                    "a") == a_nm0 + 3 * kvh + 3
                    for p in range(4):
                        for seg in MERGED:
                            s = seg[0]; kvh = 2 * p + s; kind = seg[1]
                            if kind == "INIT":
                                j = seg[2]
                                if j >= 1:
                                    scalar.wait_ge(SEM["t"], t_op(kvh, "R", j - 1))
                                    if j >= 2:
                                        scalar.wait_ge(SEM["v"], v_of(kvh, "EV", j - 2))
                                    assert inc1(scalar.copy(rsb[s][:], pLb[s][:]),
                                                "a") == a_rc(kvh, j - 1)
                            elif kind == "CHUNK":
                                j = seg[2]; c0 = seg[3]; e = e_of(j)
                                scalar.wait_ge(SEM["t"], t_op(kvh, "S", j, c0 + 1))
                                if c0 == e - 2:
                                    scalar.wait_ge(SEM["v"], v_of(kvh, "M", j))
                                b0 = s * 1024
                                assert inc1(scalar.activation(
                                    prb[s][:, c0 * 512:(c0 + 2) * 512],
                                    pS4[:, b0:b0 + 1024], AF.Exp, scale=RSQ,
                                    bias=s_nmall[:, kvh:kvh + 1]),
                                    "a") == a_exp(kvh, j, c0)
                            else:
                                jl = NSLOT - 1
                                scalar.wait_ge(SEM["t"], t_op(kvh, "R", jl))
                                scalar.wait_ge(SEM["v"], v_of(kvh, "EV", jl - 1))
                                assert inc1(scalar.copy(rsb[s][:], pLb[s][:]),
                                            "a") == a_rc(kvh, jl)

                @blk.tensor
                def _(tensor):
                    for kvh in range(NKV):
                        tensor.wait_ge(SEM["v"], v_nm0 + 4 * kvh + 3)
                        assert inc1(tensor.matmul(pLb[0][:, 0:1], s_ones_f[:],
                                                  nmsc[:, 7:8], start=True, stop=True),
                                    "t") == t_nm0 + kvh + 1

                    def mmS(kvh, s, j, t_):
                        inst = tensor.matmul(
                            psl(s, t_),
                            krb[kvh % 3][:, t_ * 128:(t_ + 1) * 128],
                            qkb[kvh % 3][:, j * 512:(j + 1) * 512],
                            start=True, stop=True, skip_group_check=True)
                        if t_ % 2 == 1:
                            assert inc1(inst, "t") == t_op(kvh, "S", j, t_)

                    tensor.wait_ge(SEM["v"], v_nm0 + 4 * NKV)
                    for p in range(4):
                        for s0 in range(2):
                            kvh = 2 * p + s0
                            tensor.wait_ge(SEM["ld"], ld_kvh(kvh))
                        for seg in MERGED:
                            s = seg[0]; kvh = 2 * p + s; kind = seg[1]
                            if kind == "INIT":
                                j = seg[2]
                                if j == 0:
                                    tensor.wait_ge(SEM["a"], a_m + p * APK if p == 0 or s == 1 else a_m + p * APK)
                                    tensor.wait_ge(SEM["a"], a_m + p * APK)
                                else:
                                    tensor.wait_ge(SEM["a"],
                                                   a_exp(kvh, j - 1, e_of(j - 1) - 2))
                                for t_ in range(2):
                                    mmS(kvh, s, j, t_)
                                if j >= 1:
                                    tensor.wait_ge(SEM["v"], v_of(kvh, "REC", j - 1))
                                    inst = tensor.matmul(pLb[s][:], s_ones_f[:],
                                                         rrow[s][:], start=True,
                                                         stop=True, skip_group_check=True)
                                    assert inc1(inst, "t") == t_op(kvh, "R", j - 1)
                            elif kind == "CHUNK":
                                j = seg[2]; c0 = seg[3]; e = e_of(j)
                                for t2 in range(c0 + 2, min(c0 + 4, e)):
                                    if t2 % 2 == 0:
                                        tensor.wait_ge(SEM["a"], a_exp(kvh, j, t2 - 2))
                                    mmS(kvh, s, j, t2)
                                if c0 + 2 >= e:
                                    tensor.wait_ge(SEM["a"], a_exp(kvh, j, c0))
                                for t_ in (c0, c0 + 1):
                                    if t_ == c0:
                                        if c0 == 0 and j >= 1:
                                            tensor.wait_ge(SEM["a"], a_rc(kvh, j - 1))
                                    inst = tensor.matmul(
                                        pLb[s][0:1, :], s_ones_bf[:],
                                        prb[s][:, t_ * 512:(t_ + 1) * 512],
                                        start=(t_ == 0), stop=(t_ == e - 1),
                                        skip_group_check=True)
                                    if t_ == e - 1:
                                        assert inc1(inst, "t") == t_op(kvh, "L", j, t_)
                                    if t_ == 0 and j >= 1:
                                        tensor.wait_ge(SEM["v"], v_of(kvh, "EV", j - 1))
                                    tensor.matmul(
                                        pAV[s][:],
                                        vtb[kvh % 3][:, t_ * 128:(t_ + 1) * 128],
                                        prb[s][:, t_ * 512:(t_ + 1) * 512],
                                        start=(t_ == 0), stop=(t_ == e - 1),
                                        skip_group_check=True)
                            else:
                                jl = NSLOT - 1
                                tensor.wait_ge(SEM["v"], v_of(kvh, "REC", jl))
                                inst = tensor.matmul(pLb[s][:], s_ones_f[:],
                                                     rrow[s][:], start=True,
                                                     stop=True, skip_group_check=True)
                                assert inc1(inst, "t") == t_op(kvh, "R", jl)

                @blk.sync
                def _(sync):
                    for kvh in range(NKV):
                        if kvh >= 3:
                            sync.wait_ge(SEM["t"], t_pair_end((kvh - 3) // 2))
                        incd(sync.dma_start(out=krb[kvh % 3][:], in_=KrT_h[kvh]), "ld")
                        for t_ in range(32):
                            incd(sync.dma_start_transpose(
                                out=vtb[kvh % 3][:, t_ * 128:(t_ + 1) * 128],
                                in_=VT_h[kvh, :, t_ * 128:(t_ + 1) * 128]), "ld")
                        for h in range(4):
                            qv = qkb[kvh % 3][:].rearrange("p (sl h q) -> p h sl q",
                                                           h=4, sl=16)
                            incd(sync.dma_start(out=qv[:, h], in_=Q_h[4 * kvh + h]), "ld")
                    sync.wait_ge(SEM["ld"], CNT["ld"])

                @blk.gpsimd
                def _(gpsimd):
                    for p in range(4):
                        for seg in MERGED:
                            s = seg[0]; kvh = 2 * p + s; kind = seg[1]
                            if kind == "CHUNK" and seg[3] == 0 and seg[2] >= 1:
                                j = seg[2]
                                gpsimd.wait_ge(SEM["v"], v_of(kvh, "EV", j - 1))
                                av = avst[s][:].rearrange("p (h q) -> p h q", h=4)
                                assert incd(gpsimd.dma_start(
                                    out=AVT_h.rearrange("n p q -> p n q")[
                                        :, 4 * kvh:4 * kvh + 4,
                                        (j - 1) * 128:j * 128],
                                    in_=av), "st") == st_avt(kvh, j - 1)
                            elif kind == "RFIN":
                                jl = NSLOT - 1
                                gpsimd.wait_ge(SEM["v"], v_of(kvh, "EV", jl))
                                av = avst[s][:].rearrange("p (h q) -> p h q", h=4)
                                assert incd(gpsimd.dma_start(
                                    out=AVT_h.rearrange("n p q -> p n q")[
                                        :, 4 * kvh:4 * kvh + 4,
                                        jl * 128:(jl + 1) * 128],
                                    in_=av), "st") == st_avt(kvh, jl)
                    gpsimd.wait_ge(SEM["st"], CNT["st"])

        ld_b = CNT["ld"]; t_b = CNT["t"]; v_b = CNT["v"]; st_b = CNT["st"]
        g_b = CNT["g"]

        def t_s(lin): return t_b + lin + 1 + lin // 8 + 1
        def t_h0(idx): return t_b + idx * 9 + 1
        def v_o(lin): return v_b + lin + 1
        def st_o(lin): return st_b + 16 * (lin + 1)
        def g_wo(idx, half): return g_b + 16 * 16 + 16 * (idx * 2 + half + 1)
        ld_avt = ld_b + 16 * 16
        def ld_gx(lin): return ld_avt + 16 * 2 * (lin + 1)

        with ExitStack() as ps:
            avres = ps.enter_context(nc.sbuf_tensor("avres", [128, 32 * TOWN], BF16))
            wob = [ps.enter_context(nc.sbuf_tensor(f"wob{i}", [128, 16 * 512], BF16))
                   for i in range(3)]
            gch = [ps.enter_context(nc.sbuf_tensor(f"gch{i}", [128, 512], F32)) for i in range(5)]
            xch = [ps.enter_context(nc.sbuf_tensor(f"xch{i}", [128, 512], F32)) for i in range(5)]
            cch = [ps.enter_context(nc.sbuf_tensor(f"cch{i}", [128, 512], F32)) for i in range(2)]
            och = [ps.enter_context(nc.sbuf_tensor(f"och{i}", [128, 512], F32)) for i in range(2)]
            ptt = [ps.enter_context(nc.psum_tensor(f"pttB{i}", [128, 512], F32))
                   for i in range(8)]

            with nc.Block() as blk:
                @blk.sync
                def _(sync):
                    for head in range(16):
                        incd(sync.dma_start(
                            out=avres[:, head * TOWN:(head + 1) * TOWN],
                            in_=AVT_h[head]), "ld")
                    for lin in range(128):
                        idx, tt = divmod(lin, 8)
                        wave, fc = divmod(idx, 8)
                        tg = wave * 8 + tt
                        if lin >= 5:
                            l2 = lin - 5
                            sync.wait_ge(SEM["v"], v_o(l2))
                            i2, tt2 = divmod(l2, 8)
                            w2, fc2 = divmod(i2, 8)
                            tg2 = w2 * 8 + tt2
                            assert incd(sync.dma_start(
                                out=out_own[tg2 * 128:(tg2 + 1) * 128,
                                            fc2 * 512:(fc2 + 1) * 512],
                                in_=och[tt2 % 2][:]), "st") == st_o(l2)
                        incd(sync.dma_start(
                            out=gch[lin % 5][:],
                            in_=G_h[tg * 128:(tg + 1) * 128,
                                    fc * 512:(fc + 1) * 512]), "ld")
                        incd(sync.dma_start(
                            out=xch[lin % 5][:],
                            in_=x_own[tg * 128:(tg + 1) * 128,
                                      fc * 512:(fc + 1) * 512]), "ld")
                    for l2 in range(123, 128):
                        i2, tt2 = divmod(l2, 8)
                        w2, fc2 = divmod(i2, 8)
                        tg2 = w2 * 8 + tt2
                        sync.wait_ge(SEM["v"], v_o(l2))
                        assert incd(sync.dma_start(
                            out=out_own[tg2 * 128:(tg2 + 1) * 128,
                                        fc2 * 512:(fc2 + 1) * 512],
                            in_=och[tt2 % 2][:]), "st") == st_o(l2)
                    sync.wait_ge(SEM["st"], CNT["st"])
                    sync.wait_ge(SEM["ld"], CNT["ld"])

                @blk.gpsimd
                def _(gpsimd):
                    for head in range(16, 32):
                        incd(gpsimd.dma_start(
                            out=avres[:, head * TOWN:(head + 1) * TOWN],
                            in_=AVT_h[head]), "g")
                    for idx in range(16):
                        wave, fc = divmod(idx, 8)
                        for half in range(2):
                            prev = idx * 2 + half - 3
                            if prev >= 0:
                                pidx, ph = divmod(prev, 2)
                                if ph == 1:
                                    gpsimd.wait_ge(SEM["t"], t_s(pidx * 8 + 7))
                                else:
                                    gpsimd.wait_ge(SEM["t"], t_h0(pidx))
                            incd(gpsimd.dma_start(
                                out=wob[(idx * 2 + half) % 3][:]
                                .rearrange("p (k c) -> p k c", k=16),
                                in_=WO[:, fc, half * 16:(half + 1) * 16]), "g")

                @blk.tensor
                def _(tensor):
                    tensor.wait_ge(SEM["ld"], ld_avt)
                    for idx in range(16):
                        wave, fc = divmod(idx, 8)
                        for half in range(2):
                            tensor.wait_ge(SEM["g"], g_wo(idx, half))
                            for tt in range(8):
                                tg = wave * 8 + tt
                                if half == 0 and idx > 0:
                                    tensor.wait_ge(SEM["v"], v_o((idx - 1) * 8 + tt))
                                for k16 in range(16):
                                    k = half * 16 + k16
                                    inst = tensor.matmul(
                                        ptt[tt][:],
                                        avres[:, k * TOWN + tg * 128:
                                              k * TOWN + (tg + 1) * 128],
                                        wob[(idx * 2 + half) % 3][:,
                                            k16 * 512:(k16 + 1) * 512],
                                        start=(half == 0 and k16 == 0),
                                        stop=(half == 1 and k16 == 15),
                                        skip_group_check=True)
                                if half == 1:
                                    assert inc1(inst, "t") == t_s(idx * 8 + tt)
                            if half == 0:
                                assert inc1(inst, "t") == t_h0(idx)

                @blk.vector
                def _(vector):
                    for idx in range(16):
                        for pair in range(4):
                            tt0 = 2 * pair; tt1 = tt0 + 1
                            lin0 = idx * 8 + tt0; lin1 = lin0 + 1
                            vector.wait_ge(SEM["t"], t_s(lin0))
                            vector.wait_ge(SEM["ld"], ld_gx(lin0))
                            if lin0 >= 2:
                                vector.wait_ge(SEM["st"], st_o(lin0 - 2))
                            vector.tensor_tensor(cch[0][:], ptt[tt0][:],
                                                 gch[lin0 % 5][:], OP.mult)
                            vector.wait_ge(SEM["t"], t_s(lin1))
                            vector.wait_ge(SEM["ld"], ld_gx(lin1))
                            if lin1 >= 2:
                                vector.wait_ge(SEM["st"], st_o(lin1 - 2))
                            vector.tensor_tensor(cch[1][:], ptt[tt1][:],
                                                 gch[lin1 % 5][:], OP.mult)
                            assert inc1(vector.tensor_tensor(
                                och[0][:], cch[0][:], xch[lin0 % 5][:], OP.add),
                                "v") == v_o(lin0)
                            assert inc1(vector.tensor_tensor(
                                och[1][:], cch[1][:], xch[lin1 % 5][:], OP.add),
                                "v") == v_o(lin1)

    return nc


_NC_CACHE = {}


def kernel(L0_attn_x, L0_attn_wq, L0_attn_wk, L0_attn_wv, L0_attn_wo,
           L0_ffn_base_w, L0_ffn_base_b, L0_ffn_A, L0_ffn_B, L0_ffn_router_w):
    x = np.asarray(L0_attn_x, np.float32)
    wq = np.asarray(L0_attn_wq, np.float32)
    wk = np.asarray(L0_attn_wk, np.float32)
    wv = np.asarray(L0_attn_wv, np.float32)
    wo = np.asarray(L0_attn_wo, np.float32)
    base_w = np.asarray(L0_ffn_base_w, np.float32)
    base_b = np.asarray(L0_ffn_base_b, np.float32)
    A = np.asarray(L0_ffn_A, np.float32)
    Bx = np.asarray(L0_ffn_B, np.float32)
    router_w = np.asarray(L0_ffn_router_w, np.float32)

    WQ = np.ascontiguousarray(wq.reshape(32, 128, 32, 128).transpose(1, 2, 0, 3))
    WK = np.ascontiguousarray(wk.reshape(32, 128, 8, 128).transpose(1, 2, 0, 3))
    WV = np.ascontiguousarray(wv.reshape(32, 128, 8, 128).transpose(1, 2, 0, 3))
    WO = np.ascontiguousarray(wo.reshape(32, 128, 8, 512).transpose(1, 2, 0, 3))
    BW = np.ascontiguousarray(
        np.ascontiguousarray(base_w.T).reshape(32, 128, 8, 512).transpose(1, 2, 0, 3))
    Bst = np.ascontiguousarray(Bx.transpose(0, 2, 1).reshape(1024, 4096))
    BST = np.ascontiguousarray(Bst.reshape(8, 128, 8, 512).transpose(1, 2, 0, 3))
    ARTf = np.concatenate([A.T, router_w.T], axis=1)
    ART = np.ascontiguousarray(ARTf.reshape(32, 128, 80).transpose(1, 0, 2))
    bias_rep = np.ascontiguousarray(np.broadcast_to(base_b[None, :], (128, 4096)))

    freqs = (np.float32(1.0) /
             np.power(np.float32(10000.0),
                      np.arange(0, HD, 2, dtype=np.float32) / np.float32(HD))
             ).astype(np.float32)
    tpos = np.arange(S, dtype=np.float32)
    ang = np.outer(tpos, freqs).astype(np.float32)
    cosr = np.cos(ang).astype(np.float32)
    sinr = np.sin(ang).astype(np.float32)
    CKV = np.ascontiguousarray(np.repeat(cosr.T, 2, axis=0))
    SKV = np.empty((HD, S), np.float32)
    SKV[0::2] = -sinr.T
    SKV[1::2] = sinr.T

    tri_add = np.where(np.arange(128)[:, None] <= np.arange(128)[None, :],
                       np.float32(0.0), np.float32(-1e9))
    MSK_role = []
    for role in range(2):
        m = np.zeros((2, 128, 128), BFD)
        if role == 0:
            m[0] = tri_add.astype(BFD)
            m[1] = BFD(-1e9)
        else:
            m[1] = tri_add.astype(BFD)
        MSK_role.append(m)

    ones_col_bf = np.ones((128, 1), BFD)
    ones_row_f32 = np.ones((1, 128), np.float32)
    epscol = np.full((128, 1), EPS, np.float32)

    if "nc" not in _NC_CACHE:
        _NC_CACHE["nc"] = build_nc()
    nc = _NC_CACHE["nc"]

    in_maps = []
    for core in range(8):
        b = core // 2
        role = core % 2
        xb = x[b]
        x_own = np.ascontiguousarray(xb.reshape(32, 128, D)[role::2].reshape(TOWN, D))
        own_pos = ((np.arange(NSLOT) * 2 + role)[:, None] * 128 +
                   np.arange(128)[None, :]).reshape(-1)
        COWN = np.ascontiguousarray(CKV[:, own_pos])
        SOWN = np.ascontiguousarray(SKV[:, own_pos])
        in_maps.append(dict(
            x_own=x_own, x_kv=np.ascontiguousarray(xb),
            WQ=WQ, WK=WK, WV=WV, WO=WO, BW=BW, BST=BST, ART=ART,
            bias_rep=bias_rep, CKV=CKV, SKV=SKV, COWN=COWN, SOWN=SOWN,
            MSK=MSK_role[role], ones_col_bf=ones_col_bf,
            ones_row_f32=ones_row_f32, epscol=epscol))

    res = run_bass_kernel_spmd(nc, in_maps, core_ids=list(range(8)))

    out = np.empty((4, S, D), np.float32)
    for core in range(8):
        b = core // 2
        role = core % 2
        ob = np.asarray(res.results[core]["out_own"], np.float32)
        out[b].reshape(32, 128, D)[role::2] = ob.reshape(NSLOT, 128, D)
    return out


# version 21
# speedup vs baseline: 1.0877x; 1.0127x over previous
from contextlib import ExitStack
import numpy as np
import ml_dtypes

import concourse.bass as bass
import concourse.mybir as mybir
from concourse.bass_utils import run_bass_kernel_spmd

F32 = mybir.dt.float32
BF16 = mybir.dt.bfloat16
AF = mybir.ActivationFunctionType
OP = mybir.AluOpType
AX = mybir.AxisListType.X
BFD = ml_dtypes.bfloat16

D = 4096
S = 4096
TOWN = 2048
NH = 32
NKV = 8
HD = 128
NSLOT = 16
RSQ = float(np.float32(1.0) / np.float32(np.sqrt(np.float32(128.0))))
EPS = 1e-5
SWAP_MASK = [x for i in range(16) for x in (2 * i + 1, 2 * i)]


def build_nc():
    nc = bass.Bass("TRN2", target_bir_lowering=False)

    def inp(name, shape, dt=F32):
        return nc.dram_tensor(name, shape, dt, kind="ExternalInput").ap()

    x_own = inp("x_own", [TOWN, D])
    x_kv = inp("x_kv", [S, D])
    WQ = inp("WQ", [128, 32, 32, 128])
    WK = inp("WK", [128, 8, 32, 128])
    WV = inp("WV", [128, 8, 32, 128])
    WO = inp("WO", [128, 8, 32, 512])
    BW = inp("BW", [128, 8, 32, 512])
    BST = inp("BST", [128, 8, 8, 512])
    ART = inp("ART", [128, 32, 80])
    bias_rep = inp("bias_rep", [128, D])
    CKV = inp("CKV", [128, S])
    SKV = inp("SKV", [128, S])
    COWN = inp("COWN", [128, TOWN])
    SOWN = inp("SOWN", [128, TOWN])
    MSK = inp("MSK", [2, 128, 128], BF16)
    MSK4 = inp("MSK4", [128, 1024], BF16)
    IDN = inp("IDN", [128, 128], BF16)
    ones_col_bf = inp("ones_col_bf", [128, 1], BF16)
    ones_row_f32 = inp("ones_row_f32", [1, 128])
    epscol_in = inp("epscol", [128, 1])

    xn_own_h = nc.dram_tensor("xn_own_h", [TOWN, D], BF16).ap()
    xn_kv_h = nc.dram_tensor("xn_kv_h", [S, D], BF16).ap()
    KrT_h = nc.dram_tensor("KrT_h", [NKV, 128, S], BF16).ap()
    VT_h = nc.dram_tensor("VT_h", [NKV, 128, S], BF16).ap()
    Q_h = nc.dram_tensor("Q_h", [NH, 128, TOWN], BF16).ap()
    Wgt_h = nc.dram_tensor("Wgt_h", [TOWN, 1024], BF16).ap()
    G_h = nc.dram_tensor("G_h", [TOWN, D], F32).ap()
    AVT_h = nc.dram_tensor("AVT_h", [NH, 128, TOWN], BF16).ap()

    out_own = nc.dram_tensor("out_own", [TOWN, D], F32, kind="ExternalOutput").ap()

    CNT = dict(ld=0, st=0, g=0, t=0, a=0, v=0)

    es = ExitStack()
    with es:
        SEM = {k: es.enter_context(nc.semaphore(k)) for k in CNT}

        def incd(instr, s):
            CNT[s] += 16
            instr.then_inc(SEM[s], 16)
            return CNT[s]

        def inc1(instr, s):
            CNT[s] += 1
            instr.then_inc(SEM[s], 1)
            return CNT[s]

        s_eps = es.enter_context(nc.sbuf_tensor("s_eps", [128, 1], F32))
        s_ones_bf = es.enter_context(nc.sbuf_tensor("s_ones_bf", [128, 1], BF16))
        s_ones_f = es.enter_context(nc.sbuf_tensor("s_ones_f", [1, 128], F32))
        s_msk = es.enter_context(nc.sbuf_tensor("s_msk", [128, 256], BF16))
        s_msk4 = es.enter_context(nc.sbuf_tensor("s_msk4", [128, 1024], BF16))
        s_idn = es.enter_context(nc.sbuf_tensor("s_idn", [128, 128], BF16))
        s_kn2 = es.enter_context(nc.sbuf_tensor("s_kn2", [1, 8], F32))
        s_qn2 = es.enter_context(nc.sbuf_tensor("s_qn2", [1, 32], F32))
        s_nmall = es.enter_context(nc.sbuf_tensor("s_nmall", [128, 8], F32))

        with nc.Block() as blk:
            @blk.sync
            def _(sync):
                incd(sync.dma_start(out=s_eps[:], in_=epscol_in), "ld")
                incd(sync.dma_start(out=s_ones_bf[:], in_=ones_col_bf), "ld")
                incd(sync.dma_start(out=s_ones_f[:], in_=ones_row_f32), "ld")
                incd(sync.dma_start(
                    out=s_msk[:].rearrange("p (m q) -> p m q", m=2),
                    in_=MSK.rearrange("m p q -> p m q")), "ld")
                incd(sync.dma_start(out=s_msk4[:], in_=MSK4), "ld")
                incd(sync.dma_start(out=s_idn[:], in_=IDN), "ld")
                sync.wait_ge(SEM["ld"], CNT["ld"])

            @blk.vector
            def _(vector):
                vector.memset(s_kn2[:], 0.0)
                inc1(vector.memset(s_qn2[:], 0.0), "v")

        NT0 = 16 + 32

        def p0_src(i):
            if i < 16:
                return (x_own[i * 128:(i + 1) * 128, :],
                        xn_own_h[i * 128:(i + 1) * 128, :])
            j = i - 16
            return (x_kv[j * 128:(j + 1) * 128, :],
                    xn_kv_h[j * 128:(j + 1) * 128, :])

        ld0 = CNT["ld"]; st0 = CNT["st"]; a0 = CNT["a"]; v0 = CNT["v"]

        def a_rstd(i): return a0 + 2 * i + 1
        def a_id(i): return a0 + 2 * i + 2
        def v_var(i): return v0 + 2 * i + 1
        def v_nmr(i): return v0 + 2 * i + 2
        def ld_x(i): return ld0 + 16 * (i + 1)
        def st_xn(i): return st0 + 16 * (i + 1)

        with ExitStack() as ps:
            xb = [ps.enter_context(nc.sbuf_tensor(f"p0xb{i}", [128, D], F32)) for i in range(2)]
            xnb = [ps.enter_context(nc.sbuf_tensor(f"p0xn{i}", [128, D], BF16)) for i in range(2)]
            sstat = ps.enter_context(nc.sbuf_tensor("p0st", [128, 48], F32))
            sbn = ps.enter_context(nc.sbuf_tensor("p0bn", [128, 2], F32))
            _cols = [ps.enter_context(nc.sbuf_tensor(f"p0c{i}", [128, 1], F32))
                     for i in range(4)]
            lnc, rstdc, nmrc, spc0 = [c[:] for c in _cols]

            with nc.Block() as blk:
                @blk.sync
                def _(sync):
                    for i in range(2):
                        src, _dst = p0_src(i)
                        incd(sync.dma_start(out=xb[i % 2][:], in_=src), "ld")
                    for i in range(NT0):
                        _src, dst = p0_src(i)
                        sync.wait_ge(SEM["a"], a_id(i))
                        incd(sync.dma_start(out=dst, in_=xnb[i % 2][:]), "st")
                        if i + 2 < NT0:
                            src2, _ = p0_src(i + 2)
                            incd(sync.dma_start(out=xb[i % 2][:], in_=src2), "ld")
                    sync.wait_ge(SEM["st"], CNT["st"])

                @blk.vector
                def _(vector):
                    for i in range(NT0):
                        vector.wait_ge(SEM["ld"], ld_x(i))
                        for c in range(8):
                            vector.bn_stats(sstat[:, c * 6:(c + 1) * 6],
                                            xb[i % 2][:, c * 512:(c + 1) * 512])
                        vector.tensor_copy(spc0, s_eps[:])
                        vector.bn_aggr(sbn[:], sstat[:])
                        inc1(vector.drain(), "v")
                        vector.wait_ge(SEM["a"], a_rstd(i))
                        vector.tensor_scalar(nmrc, sbn[:, 0:1], rstdc, -1.0,
                                             OP.mult, OP.mult)
                        inc1(vector.drain(), "v")

                @blk.scalar
                def _(scalar):
                    for i in range(NT0):
                        scalar.wait_ge(SEM["ld"], ld_x(i))
                        if i >= 2:
                            scalar.wait_ge(SEM["st"], st_xn(i - 2))
                        scalar.wait_ge(SEM["v"], v_var(i))
                        scalar.activation(lnc, sbn[:, 1:2], AF.Ln, bias=s_eps[:])
                        inc1(scalar.activation(rstdc, lnc, AF.Exp, scale=-0.5), "a")
                        scalar.wait_ge(SEM["v"], v_nmr(i))
                        inc1(scalar.activation(xnb[i % 2][:], xb[i % 2][:], AF.Identity,
                                               scale=rstdc, bias=nmrc), "a")

        with ExitStack() as ps:
            wres = ps.enter_context(nc.sbuf_tensor("p1kw", [128, 8 * 32 * 128], BF16))
            xbuf = [ps.enter_context(nc.sbuf_tensor(f"p1kx{i}", [128, 32 * 512], BF16))
                    for i in range(2)]
            s_c = ps.enter_context(nc.sbuf_tensor("p1kc", [128, S], F32))
            s_s = ps.enter_context(nc.sbuf_tensor("p1ks", [128, S], F32))
            qbf = [ps.enter_context(nc.sbuf_tensor(f"p1kq{i}", [128, 512], BF16))
                   for i in range(4)]
            rot = [ps.enter_context(nc.sbuf_tensor(f"p1kr{i}", [128, 512], BF16))
                   for i in range(4)]
            qsq = [ps.enter_context(nc.sbuf_tensor(f"p1kqs{i}", [128, 512], BF16))
                   for i in range(2)]
            nrow = [ps.enter_context(nc.sbuf_tensor(f"p1kn{i}", [1, 512], F32))
                    for i in range(2)]
            vncol = ps.enter_context(nc.sbuf_tensor("p1kvn", [1, 1], F32))
            tmp1 = ps.enter_context(nc.sbuf_tensor("p1kt1", [128, 512], F32))
            tmp2 = ps.enter_context(nc.sbuf_tensor("p1kt2", [128, 512], F32))
            swp = ps.enter_context(nc.sbuf_tensor("p1ksw", [128, 512], BF16))
            pk = [ps.enter_context(nc.psum_tensor(f"p1kp{i}", [128, 512], F32))
                  for i in range(4)]
            pN = [ps.enter_context(nc.psum_tensor(f"p1kpn{i}", [1, 512], F32))
                  for i in range(2)]

            t_b = CNT["t"]; a_b = CNT["a"]; v_b = CNT["v"]; st_b = CNT["st"]
            g_b = CNT["g"]; ld_b = CNT["ld"]
            NU = 64
            T_CHAIN = {}; T_N = {}; A_EVQ = {}; A_EVN = {}
            V_ROT = {}; V_SQ = {}; V_NM = {}; ST = {}
            tc = t_b
            for u in range(NU):
                tc += 1; T_CHAIN[u] = tc
                if u >= 1:
                    tc += 1; T_N[u - 1] = tc
            tc += 1; T_N[NU - 1] = tc
            ac = a_b
            for u in range(NU):
                ac += 1; A_EVQ[u] = ac
                if u >= 1:
                    ac += 1; A_EVN[u - 1] = ac
            ac += 1; A_EVN[NU - 1] = ac
            vc = v_b
            for u in range(NU):
                vc += 1; V_ROT[u] = vc
                if u >= 1:
                    vc += 1; V_NM[u - 1] = vc
                vc += 1; V_SQ[u] = vc
            vc += 1; V_NM[NU - 1] = vc
            stc = st_b
            for u in range(NU):
                stc += 16; ST[u] = stc
            def ld_grp(g_): return ld_b + 32 + 16 * 32 * (g_ + 1)

            with nc.Block() as blk:
                @blk.sync
                def _(sync):
                    incd(sync.dma_start(out=s_c[:], in_=CKV), "ld")
                    incd(sync.dma_start(out=s_s[:], in_=SKV), "ld")
                    for g_ in range(8):
                        if g_ >= 2:
                            sync.wait_ge(SEM["t"], T_CHAIN[(g_ - 2) * 8 + 7])
                        for k in range(32):
                            incd(sync.dma_start_transpose(
                                out=xbuf[g_ % 2][:, k * 512:(k + 1) * 512],
                                in_=xn_kv_h[g_ * 512:(g_ + 1) * 512,
                                            k * 128:(k + 1) * 128]), "ld")
                    sync.wait_ge(SEM["ld"], CNT["ld"])

                @blk.gpsimd
                def _(gpsimd):
                    incd(gpsimd.dma_start(
                        out=wres[:].rearrange("p (f k c) -> p f k c", f=8, k=32),
                        in_=WK), "g")
                    for u in range(NU):
                        g_, f = divmod(u, 8)
                        gpsimd.wait_ge(SEM["v"], V_ROT[u])
                        assert incd(gpsimd.dma_start(
                            out=KrT_h[f, :, g_ * 512:(g_ + 1) * 512],
                            in_=rot[u % 4][:]), "st") == ST[u]
                    gpsimd.wait_ge(SEM["st"], CNT["st"])
                g_cast = g_b + 16

                @blk.tensor
                def _(tensor):
                    tensor.wait_ge(SEM["g"], g_cast)
                    for u in range(NU):
                        g_, f = divmod(u, 8)
                        if f == 0:
                            tensor.wait_ge(SEM["ld"], ld_grp(g_))
                        if u >= 4:
                            tensor.wait_ge(SEM["a"], A_EVQ[u - 4])
                        for k in range(32):
                            inst = tensor.matmul(
                                pk[u % 4][:],
                                wres[:, (f * 32 + k) * 128:(f * 32 + k + 1) * 128],
                                xbuf[g_ % 2][:, k * 512:(k + 1) * 512],
                                start=(k == 0), stop=(k == 31))
                        assert inc1(inst, "t") == T_CHAIN[u]
                        if u >= 1:
                            up = u - 1
                            tensor.wait_ge(SEM["v"], V_SQ[up])
                            if up >= 2:
                                tensor.wait_ge(SEM["a"], A_EVN[up - 2])
                            assert inc1(tensor.matmul(pN[up % 2][:], s_ones_bf[:],
                                                      qsq[up % 2][:], start=True,
                                                      stop=True), "t") == T_N[up]
                    up = NU - 1
                    tensor.wait_ge(SEM["v"], V_SQ[up])
                    tensor.wait_ge(SEM["a"], A_EVN[up - 2])
                    assert inc1(tensor.matmul(pN[up % 2][:], s_ones_bf[:],
                                              qsq[up % 2][:], start=True,
                                              stop=True), "t") == T_N[up]

                @blk.scalar
                def _(scalar):
                    def evq(u):
                        scalar.wait_ge(SEM["t"], T_CHAIN[u])
                        if u >= 4:
                            scalar.wait_ge(SEM["v"], V_ROT[u - 4])
                        assert inc1(scalar.copy(qbf[u % 4][:], pk[u % 4][:]),
                                    "a") == A_EVQ[u]

                    def evn(u):
                        scalar.wait_ge(SEM["t"], T_N[u])
                        if u >= 2:
                            scalar.wait_ge(SEM["v"], V_NM[u - 2])
                        assert inc1(scalar.copy(nrow[u % 2][:], pN[u % 2][:]),
                                    "a") == A_EVN[u]

                    for u in range(NU):
                        evq(u)
                        if u >= 1:
                            evn(u - 1)
                    evn(NU - 1)

                @blk.vector
                def _(vector):
                    for u in range(NU):
                        g_, f = divmod(u, 8)
                        csl = s_c[:, g_ * 512:(g_ + 1) * 512]
                        ssl = s_s[:, g_ * 512:(g_ + 1) * 512]
                        vector.wait_ge(SEM["a"], A_EVQ[u])
                        vector.stream_shuffle(swp[:], qbf[u % 4][:], SWAP_MASK)
                        vector.tensor_tensor(tmp1[:], qbf[u % 4][:], csl, OP.mult)
                        vector.tensor_tensor(tmp2[:], swp[:], ssl, OP.mult)
                        if u >= 1:
                            vector.wait_ge(SEM["a"], A_EVN[u - 1])
                            vector.reduce_max(vncol[:], nrow[(u - 1) % 2][:], axis=AX)
                        else:
                            vector.tensor_copy(swp[:, 0:1], s_eps[:])
                        if u >= 4:
                            vector.wait_ge(SEM["st"], ST[u - 4])
                        assert inc1(vector.tensor_tensor(rot[u % 4][:], tmp1[:],
                                                         tmp2[:], OP.add),
                                    "v") == V_ROT[u]
                        if u >= 1:
                            _, pf = divmod(u - 1, 8)
                            nt = s_kn2[:, pf:pf + 1]
                            assert inc1(vector.tensor_tensor(nt, nt, vncol[:],
                                                             OP.max), "v") == V_NM[u - 1]
                        else:
                            vector.tensor_copy(swp[:, 0:1], s_eps[:])
                        if u >= 2:
                            vector.wait_ge(SEM["t"], T_N[u - 2])
                        assert inc1(vector.tensor_tensor(qsq[u % 2][:], rot[u % 4][:],
                                                         rot[u % 4][:], OP.mult),
                                    "v") == V_SQ[u]
                    up = NU - 1
                    _, pf = divmod(up, 8)
                    vector.wait_ge(SEM["a"], A_EVN[up])
                    vector.reduce_max(vncol[:], nrow[up % 2][:], axis=AX)
                    vector.tensor_copy(swp[:, 0:1], s_eps[:])
                    nt = s_kn2[:, pf:pf + 1]
                    assert inc1(vector.tensor_tensor(nt, nt, vncol[:],
                                                     OP.max), "v") == V_NM[up]

        with ExitStack() as ps:
            wres = ps.enter_context(nc.sbuf_tensor("p1vw", [128, 8 * 32 * 128], BF16))
            xbuf = [ps.enter_context(nc.sbuf_tensor(f"p1vx{i}", [128, 32 * 512], BF16))
                    for i in range(2)]
            vbf = [ps.enter_context(nc.sbuf_tensor(f"p1vq{i}", [128, 512], BF16))
                   for i in range(4)]
            pk = [ps.enter_context(nc.psum_tensor(f"p1vp{i}", [128, 512], F32))
                  for i in range(4)]

            t_b = CNT["t"]; a_b = CNT["a"]; st_b = CNT["st"]
            g_b = CNT["g"]; ld_b = CNT["ld"]
            NU = 64
            def tchain(u): return t_b + u + 1
            def aev(u): return a_b + u + 1
            def stv(u): return st_b + 16 * (u + 1)
            def ld_grp(g_): return ld_b + 16 * 32 * (g_ + 1)

            with nc.Block() as blk:
                @blk.sync
                def _(sync):
                    for g_ in range(8):
                        if g_ >= 2:
                            sync.wait_ge(SEM["t"], tchain((g_ - 2) * 8 + 7))
                        for k in range(32):
                            incd(sync.dma_start_transpose(
                                out=xbuf[g_ % 2][:, k * 512:(k + 1) * 512],
                                in_=xn_kv_h[g_ * 512:(g_ + 1) * 512,
                                            k * 128:(k + 1) * 128]), "ld")
                    sync.wait_ge(SEM["ld"], CNT["ld"])

                @blk.gpsimd
                def _(gpsimd):
                    incd(gpsimd.dma_start(
                        out=wres[:].rearrange("p (f k c) -> p f k c", f=8, k=32),
                        in_=WV), "g")
                    for u in range(NU):
                        g_, f = divmod(u, 8)
                        gpsimd.wait_ge(SEM["a"], aev(u))
                        assert incd(gpsimd.dma_start(
                            out=VT_h[f, :, g_ * 512:(g_ + 1) * 512],
                            in_=vbf[u % 4][:]), "st") == stv(u)
                    gpsimd.wait_ge(SEM["st"], CNT["st"])
                g_cast = g_b + 16

                @blk.tensor
                def _(tensor):
                    tensor.wait_ge(SEM["g"], g_cast)
                    for u in range(NU):
                        g_, f = divmod(u, 8)
                        if f == 0:
                            tensor.wait_ge(SEM["ld"], ld_grp(g_))
                        if u >= 4:
                            tensor.wait_ge(SEM["a"], aev(u - 4))
                        for k in range(32):
                            inst = tensor.matmul(
                                pk[u % 4][:],
                                wres[:, (f * 32 + k) * 128:(f * 32 + k + 1) * 128],
                                xbuf[g_ % 2][:, k * 512:(k + 1) * 512],
                                start=(k == 0), stop=(k == 31))
                        assert inc1(inst, "t") == tchain(u)

                @blk.scalar
                def _(scalar):
                    for u in range(NU):
                        scalar.wait_ge(SEM["t"], tchain(u))
                        if u >= 4:
                            scalar.wait_ge(SEM["st"], stv(u - 4))
                        assert inc1(scalar.copy(vbf[u % 4][:], pk[u % 4][:]),
                                    "a") == aev(u)

        with ExitStack() as span:
            with ExitStack() as ps:
                xnT = ps.enter_context(nc.sbuf_tensor("xnT", [128, 32 * TOWN], BF16))
                wqb = [ps.enter_context(nc.sbuf_tensor(f"wqb{i}", [128, 4096], BF16))
                       for i in range(2)]
                s_co = ps.enter_context(nc.sbuf_tensor("s_co", [128, TOWN], F32))
                s_so = ps.enter_context(nc.sbuf_tensor("s_so", [128, TOWN], F32))
                qbf = [ps.enter_context(nc.sbuf_tensor(f"qbf{i}", [128, 512], BF16))
                       for i in range(4)]
                rot = [ps.enter_context(nc.sbuf_tensor(f"qrot{i}", [128, 512], BF16))
                       for i in range(4)]
                qsq = [ps.enter_context(nc.sbuf_tensor(f"qsq{i}", [128, 512], BF16))
                       for i in range(2)]
                nrow = [ps.enter_context(nc.sbuf_tensor(f"qnr{i}", [1, 512], F32))
                        for i in range(2)]
                vncol = ps.enter_context(nc.sbuf_tensor("qvn", [1, 1], F32))
                tmp1 = ps.enter_context(nc.sbuf_tensor("qt1", [128, 512], F32))
                tmp2 = ps.enter_context(nc.sbuf_tensor("qt2", [128, 512], F32))
                swp = ps.enter_context(nc.sbuf_tensor("qsw", [128, 512], BF16))
                pk = [ps.enter_context(nc.psum_tensor(f"p2ap{i}", [128, 512], F32))
                      for i in range(4)]
                pN = [ps.enter_context(nc.psum_tensor(f"p2apn{i}", [1, 512], F32))
                      for i in range(2)]

                t_b = CNT["t"]; a_b = CNT["a"]; v_b = CNT["v"]; st_b = CNT["st"]
                g_b = CNT["g"]; ld_b = CNT["ld"]
                NU = 128
                T_CHAIN = {}; T_N = {}; A_EVQ = {}; A_EVN = {}
                V_ROT = {}; V_SQ = {}; V_NM = {}; ST = {}
                tc = t_b
                for u in range(NU):
                    tc += 1; T_CHAIN[u] = tc
                    if u >= 1:
                        tc += 1; T_N[u - 1] = tc
                tc += 1; T_N[NU - 1] = tc
                ac = a_b
                for u in range(NU):
                    ac += 1; A_EVQ[u] = ac
                    if u >= 1:
                        ac += 1; A_EVN[u - 1] = ac
                ac += 1; A_EVN[NU - 1] = ac
                vc = v_b
                for u in range(NU):
                    vc += 1; V_ROT[u] = vc
                    if u >= 1:
                        vc += 1; V_NM[u - 1] = vc
                    vc += 1; V_SQ[u] = vc
                vc += 1; V_NM[NU - 1] = vc
                stc = st_b
                for u in range(NU):
                    stc += 16; ST[u] = stc
                ld_xnT = ld_b + 16 * 2 + 16 * 128
                def g_wq(f): return g_b + 16 * (f + 1)

                with nc.Block() as blk:
                    @blk.sync
                    def _(sync):
                        incd(sync.dma_start(out=s_co[:], in_=COWN), "ld")
                        incd(sync.dma_start(out=s_so[:], in_=SOWN), "ld")
                        for k in range(32):
                            for gg in range(4):
                                incd(sync.dma_start_transpose(
                                    out=xnT[:, k * TOWN + gg * 512: k * TOWN + (gg + 1) * 512],
                                    in_=xn_own_h[gg * 512:(gg + 1) * 512,
                                                 k * 128:(k + 1) * 128]), "ld")
                        sync.wait_ge(SEM["ld"], CNT["ld"])

                    @blk.gpsimd
                    def _(gpsimd):
                        for f in range(2):
                            incd(gpsimd.dma_start(
                                out=wqb[f % 2][:].rearrange("p (k c) -> p k c", k=32),
                                in_=WQ[:, f]), "g")
                        for u in range(NU):
                            f, gg = divmod(u, 4)
                            if gg == 0 and f + 2 < 32:
                                gpsimd.wait_ge(SEM["t"], T_CHAIN[f * 4 + 3])
                                incd(gpsimd.dma_start(
                                    out=wqb[f % 2][:].rearrange("p (k c) -> p k c", k=32),
                                    in_=WQ[:, f + 2]), "g")
                            gpsimd.wait_ge(SEM["v"], V_ROT[u])
                            assert incd(gpsimd.dma_start(
                                out=Q_h[f, :, gg * 512:(gg + 1) * 512],
                                in_=rot[u % 4][:]), "st") == ST[u]
                        gpsimd.wait_ge(SEM["st"], CNT["st"])

                    @blk.tensor
                    def _(tensor):
                        tensor.wait_ge(SEM["ld"], ld_xnT)
                        for u in range(NU):
                            f, gg = divmod(u, 4)
                            if gg == 0:
                                tensor.wait_ge(SEM["g"], g_wq(f))
                            if u >= 4:
                                tensor.wait_ge(SEM["a"], A_EVQ[u - 4])
                            for k in range(32):
                                inst = tensor.matmul(
                                    pk[u % 4][:],
                                    wqb[f % 2][:, k * 128:(k + 1) * 128],
                                    xnT[:, k * TOWN + gg * 512: k * TOWN + (gg + 1) * 512],
                                    start=(k == 0), stop=(k == 31))
                            assert inc1(inst, "t") == T_CHAIN[u]
                            if u >= 1:
                                up = u - 1
                                tensor.wait_ge(SEM["v"], V_SQ[up])
                                if up >= 2:
                                    tensor.wait_ge(SEM["a"], A_EVN[up - 2])
                                assert inc1(tensor.matmul(pN[up % 2][:], s_ones_bf[:],
                                                          qsq[up % 2][:], start=True,
                                                          stop=True), "t") == T_N[up]
                        up = NU - 1
                        tensor.wait_ge(SEM["v"], V_SQ[up])
                        tensor.wait_ge(SEM["a"], A_EVN[up - 2])
                        assert inc1(tensor.matmul(pN[up % 2][:], s_ones_bf[:],
                                                  qsq[up % 2][:], start=True,
                                                  stop=True), "t") == T_N[up]

                    @blk.scalar
                    def _(scalar):
                        def evq(u):
                            scalar.wait_ge(SEM["t"], T_CHAIN[u])
                            if u >= 4:
                                scalar.wait_ge(SEM["v"], V_ROT[u - 4])
                            assert inc1(scalar.copy(qbf[u % 4][:], pk[u % 4][:]),
                                        "a") == A_EVQ[u]

                        def evn(u):
                            scalar.wait_ge(SEM["t"], T_N[u])
                            if u >= 2:
                                scalar.wait_ge(SEM["v"], V_NM[u - 2])
                            assert inc1(scalar.copy(nrow[u % 2][:], pN[u % 2][:]),
                                        "a") == A_EVN[u]

                        for u in range(NU):
                            evq(u)
                            if u >= 1:
                                evn(u - 1)
                        evn(NU - 1)

                    @blk.vector
                    def _(vector):
                        for u in range(NU):
                            f, gg = divmod(u, 4)
                            csl = s_co[:, gg * 512:(gg + 1) * 512]
                            ssl = s_so[:, gg * 512:(gg + 1) * 512]
                            vector.wait_ge(SEM["a"], A_EVQ[u])
                            vector.stream_shuffle(swp[:], qbf[u % 4][:], SWAP_MASK)
                            vector.tensor_tensor(tmp1[:], qbf[u % 4][:], csl, OP.mult)
                            vector.tensor_tensor(tmp2[:], swp[:], ssl, OP.mult)
                            if u >= 1:
                                vector.wait_ge(SEM["a"], A_EVN[u - 1])
                                vector.reduce_max(vncol[:], nrow[(u - 1) % 2][:], axis=AX)
                            else:
                                vector.tensor_copy(swp[:, 0:1], s_eps[:])
                            if u >= 4:
                                vector.wait_ge(SEM["st"], ST[u - 4])
                            assert inc1(vector.tensor_tensor(rot[u % 4][:], tmp1[:],
                                                             tmp2[:], OP.add),
                                        "v") == V_ROT[u]
                            if u >= 1:
                                pf, _pgg = divmod(u - 1, 4)
                                nt = s_qn2[:, pf:pf + 1]
                                assert inc1(vector.tensor_tensor(nt, nt, vncol[:],
                                                                 OP.max), "v") == V_NM[u - 1]
                            else:
                                vector.tensor_copy(swp[:, 0:1], s_eps[:])
                            if u >= 2:
                                vector.wait_ge(SEM["t"], T_N[u - 2])
                            assert inc1(vector.tensor_tensor(qsq[u % 2][:], rot[u % 4][:],
                                                             rot[u % 4][:], OP.mult),
                                        "v") == V_SQ[u]
                        up = NU - 1
                        pf, _pgg = divmod(up, 4)
                        vector.wait_ge(SEM["a"], A_EVN[up])
                        vector.reduce_max(vncol[:], nrow[up % 2][:], axis=AX)
                        vector.tensor_copy(swp[:, 0:1], s_eps[:])
                        nt = s_qn2[:, pf:pf + 1]
                        assert inc1(vector.tensor_tensor(nt, nt, vncol[:],
                                                         OP.max), "v") == V_NM[up]

                t_b = CNT["t"]; a_b = CNT["a"]; v_b = CNT["v"]; st_b = CNT["st"]
                g_b = CNT["g"]
                def t_rl(tt): return t_b + tt + 1
                def a_ex(tt): return a_b + tt + 1
                def v_pre(tt): return v_b + tt * 3 + 1
                def v_cmb(tt): return v_b + tt * 3 + 2
                def v_wg(tt): return v_b + tt * 3 + 3
                def st_wg(tt): return st_b + 16 * (tt + 1)
                g_art = g_b + 16

                with ExitStack() as ps2:
                    artb = ps2.enter_context(nc.sbuf_tensor("artb", [128, 32 * 80], BF16))
                    rh_all = ps2.enter_context(nc.sbuf_tensor("rh_all", [128, 1024], F32))
                    unorm = [ps2.enter_context(nc.sbuf_tensor(f"unorm{i}", [128, 16], F32))
                             for i in range(2)]
                    selb = ps2.enter_context(nc.sbuf_tensor("selb", [128, 16], F32))
                    comb = ps2.enter_context(nc.sbuf_tensor("comb", [128, 16], F32))
                    m8 = ps2.enter_context(nc.sbuf_tensor("m8", [128, 8], F32))
                    _rc = [ps2.enter_context(nc.sbuf_tensor(f"p2brc{i}", [128, 1], F32))
                           for i in range(8)]
                    mxc, nmxc, Sc, dnc, rfc, s1c, spcc, _x = [c[:] for c in _rc]
                    wgt = [ps2.enter_context(nc.sbuf_tensor(f"wgt{i}", [128, 1024], BF16))
                           for i in range(2)]
                    prl = [ps2.enter_context(nc.psum_tensor(f"prl{i}", [128, 80], F32))
                           for i in range(2)]

                    with nc.Block() as blk:
                        @blk.gpsimd
                        def _(gpsimd):
                            incd(gpsimd.dma_start(
                                out=artb[:].rearrange("p (k c) -> p k c", k=32),
                                in_=ART), "g")

                        @blk.tensor
                        def _(tensor):
                            tensor.wait_ge(SEM["g"], g_art)
                            for tt in range(16):
                                if tt >= 2:
                                    tensor.wait_ge(SEM["v"], v_cmb(tt - 2))
                                for k in range(32):
                                    inst = tensor.matmul(
                                        prl[tt % 2][:],
                                        xnT[:, k * TOWN + tt * 128: k * TOWN + (tt + 1) * 128],
                                        artb[:, k * 80:(k + 1) * 80],
                                        start=(k == 0), stop=(k == 31))
                                assert inc1(inst, "t") == t_rl(tt)

                        @blk.vector
                        def _(vector):
                            for tt in range(16):
                                vector.wait_ge(SEM["t"], t_rl(tt))
                                vector.reduce_max(mxc, prl[tt % 2][:, 64:80], axis=AX)
                                vector.tensor_copy(rh_all[:, tt * 64:(tt + 1) * 64],
                                                   prl[tt % 2][:, 0:64])
                                vector.tensor_scalar(nmxc, mxc, -1.0, None, OP.mult)
                                assert inc1(vector.drain(), "v") == v_pre(tt)
                                vector.wait_ge(SEM["a"], a_ex(tt))
                                vector.max(m8[:], unorm[tt % 2][:])
                                vector.tensor_scalar(dnc, Sc, 1e-6, None, OP.mult)
                                vector.tensor_tensor(s1c, m8[:, 0:1], m8[:, 1:2], OP.add)
                                vector.tensor_scalar(selb[:], unorm[tt % 2][:],
                                                     m8[:, 1:2], None, OP.is_ge)
                                vector.tensor_tensor(dnc, dnc, s1c, OP.add)
                                vector.tensor_copy(spcc, s_eps[:])
                                vector.reciprocal(rfc, dnc)
                                vector.tensor_copy(spcc, s_eps[:])
                                assert inc1(vector.scalar_tensor_tensor(
                                    comb[:], unorm[tt % 2][:], rfc, selb[:],
                                    OP.mult, OP.mult), "v") == v_cmb(tt)
                                if tt >= 2:
                                    vector.wait_ge(SEM["st"], st_wg(tt - 2))
                                vector.tensor_copy(spcc, s_eps[:])
                                for e in range(16):
                                    inst = vector.tensor_scalar(
                                        wgt[tt % 2][:, e * 64:(e + 1) * 64],
                                        rh_all[:, tt * 64:(tt + 1) * 64],
                                        comb[:, e:e + 1], 0.5, OP.mult, OP.mult)
                                assert inc1(inst, "v") == v_wg(tt)

                        @blk.scalar
                        def _(scalar):
                            for tt in range(16):
                                scalar.wait_ge(SEM["v"], v_pre(tt))
                                assert inc1(scalar.activation(
                                    unorm[tt % 2][:], prl[tt % 2][:, 64:80], AF.Exp,
                                    bias=nmxc, accum_out=Sc), "a") == a_ex(tt)

                        @blk.sync
                        def _(sync):
                            for tt in range(16):
                                sync.wait_ge(SEM["v"], v_wg(tt))
                                assert incd(sync.dma_start(
                                    out=Wgt_h[tt * 128:(tt + 1) * 128, :],
                                    in_=wgt[tt % 2][:]), "st") == st_wg(tt)
                            sync.wait_ge(SEM["st"], CNT["st"])

            t_b = CNT["t"]; a_b = CNT["a"]; v_b = CNT["v"]; st_b = CNT["st"]
            g_b = CNT["g"]; ld_b = CNT["ld"]

            def t_g(lin): return t_b + lin + 1 + (2 if lin >= 56 else 0)
            def t_xf(half): return t_b + 57 + half
            def a_silu(lin): return a_b + lin + 1
            def v_gadd(lin): return v_b + lin + 1
            def st_g(lin): return st_b + 16 * (lin + 1)
            ld_w0 = ld_b + 16 * 72
            def ld_w1(half): return ld_b + 16 * (72 + 32 * (half + 1))
            g_bias = g_b + 16
            def g_bw(idx, half): return g_bias + 16 * (idx * 3 + half + 1)
            def g_bst(idx): return g_bias + 16 * (idx * 3 + 3)

            with ExitStack() as ps:
                wgtT = ps.enter_context(nc.sbuf_tensor("wgtT", [128, 8 * TOWN], BF16))
                xnw = ps.enter_context(nc.sbuf_tensor("xnw", [128, 32 * 1024], BF16))
                bwb = [ps.enter_context(nc.sbuf_tensor(f"bwb{i}", [128, 16 * 512], BF16))
                       for i in range(4)]
                bstb = [ps.enter_context(nc.sbuf_tensor(f"bstb{i}", [128, 8 * 512], BF16))
                        for i in range(2)]
                biasB = ps.enter_context(nc.sbuf_tensor("biasB", [128, D], BF16))
                gtmp = [ps.enter_context(nc.sbuf_tensor(f"gtmp{i}", [128, 512], F32))
                        for i in range(4)]
                gout = [ps.enter_context(nc.sbuf_tensor(f"gout{i}", [128, 512], F32))
                        for i in range(4)]

                with nc.Block() as blk:
                    @blk.sync
                    def _(sync):
                        for ek in range(8):
                            incd(sync.dma_start_transpose(
                                out=wgtT[:, ek * TOWN:(ek + 1) * TOWN],
                                in_=Wgt_h[:, ek * 128:(ek + 1) * 128]), "ld")

                    @blk.gpsimd
                    def _(gpsimd):
                        assert incd(gpsimd.dma_start(out=biasB[:], in_=bias_rep),
                                    "g") == g_bias

                with ExitStack() as psm:
                    ptt = [psm.enter_context(nc.psum_tensor(f"pttA{i}", [128, 512], F32))
                           for i in range(8)]
                    with nc.Block() as blk:
                        @blk.sync
                        def _(sync):
                            def xnw_load(wave, k, hh):
                                tok0 = wave * 1024 + hh * 512
                                incd(sync.dma_start_transpose(
                                    out=xnw[:, k * 1024 + hh * 512:
                                            k * 1024 + (hh + 1) * 512],
                                    in_=xn_own_h[tok0:tok0 + 512,
                                                 k * 128:(k + 1) * 128]), "ld")

                            def g_store(lin):
                                idx, tt = divmod(lin, 8)
                                wave, fc = divmod(idx, 8)
                                tg = wave * 8 + tt
                                sync.wait_ge(SEM["a"], a_silu(lin))
                                assert incd(sync.dma_start(
                                    out=G_h[tg * 128:(tg + 1) * 128,
                                            fc * 512:(fc + 1) * 512],
                                    in_=gout[lin % 4][:]), "st") == st_g(lin)

                            for k in range(32):
                                for hh in range(2):
                                    xnw_load(0, k, hh)
                            for lin in range(56):
                                g_store(lin)
                            for half in range(2):
                                sync.wait_ge(SEM["t"], t_xf(half))
                                for k16 in range(16):
                                    for hh in range(2):
                                        xnw_load(1, half * 16 + k16, hh)
                            for lin in range(56, 128):
                                g_store(lin)
                            sync.wait_ge(SEM["st"], CNT["st"])

                        @blk.gpsimd
                        def _(gpsimd):
                            for idx in range(16):
                                wave, fc = divmod(idx, 8)
                                if idx >= 2:
                                    gpsimd.wait_ge(SEM["t"], t_g((idx - 2) * 8 + 7))
                                for half in range(2):
                                    incd(gpsimd.dma_start(
                                        out=bwb[(idx % 2) * 2 + half][:]
                                        .rearrange("p (k c) -> p k c", k=16),
                                        in_=BW[:, fc, half * 16:(half + 1) * 16]), "g")
                                incd(gpsimd.dma_start(
                                    out=bstb[idx % 2][:].rearrange("p (k c) -> p k c", k=8),
                                    in_=BST[:, fc]), "g")

                        @blk.tensor
                        def _(tensor):
                            for idx in range(16):
                                wave, fc = divmod(idx, 8)
                                if idx == 0:
                                    tensor.wait_ge(SEM["ld"], ld_w0)
                                for half in range(2):
                                    if idx == 8:
                                        tensor.wait_ge(SEM["ld"], ld_w1(half))
                                    tensor.wait_ge(SEM["g"], g_bw(idx, half))
                                    for tt in range(8):
                                        if half == 0 and idx > 0:
                                            tensor.wait_ge(
                                                SEM["v"], v_gadd((idx - 1) * 8 + tt))
                                        for k16 in range(16):
                                            k = half * 16 + k16
                                            inst = tensor.matmul(
                                                ptt[tt][:],
                                                xnw[:, k * 1024 + tt * 128:
                                                    k * 1024 + (tt + 1) * 128],
                                                bwb[(idx % 2) * 2 + half][:,
                                                    k16 * 512:(k16 + 1) * 512],
                                                start=(half == 0 and k16 == 0),
                                                stop=False,
                                                skip_group_check=True)
                                    if idx == 7:
                                        assert inc1(inst, "t") == t_xf(half)
                                tensor.wait_ge(SEM["g"], g_bst(idx))
                                for tt in range(8):
                                    tg = wave * 8 + tt
                                    for ek in range(8):
                                        inst = tensor.matmul(
                                            ptt[tt][:],
                                            wgtT[:, ek * TOWN + tg * 128:
                                                 ek * TOWN + (tg + 1) * 128],
                                            bstb[idx % 2][:, ek * 512:(ek + 1) * 512],
                                            start=False, stop=(ek == 7),
                                            skip_group_check=True)
                                    assert inc1(inst, "t") == t_g(idx * 8 + tt)

                        @blk.vector
                        def _(vector):
                            vector.wait_ge(SEM["g"], g_bias)
                            for idx in range(16):
                                wave, fc = divmod(idx, 8)
                                for tt in range(8):
                                    lin = idx * 8 + tt
                                    vector.wait_ge(SEM["t"], t_g(lin))
                                    if lin >= 4:
                                        vector.wait_ge(SEM["a"], a_silu(lin - 4))
                                    assert inc1(vector.tensor_tensor(
                                        gtmp[lin % 4][:], ptt[tt][:],
                                        biasB[:, fc * 512:(fc + 1) * 512],
                                        OP.add), "v") == v_gadd(lin)

                        @blk.scalar
                        def _(scalar):
                            for idx in range(16):
                                for tt in range(8):
                                    lin = idx * 8 + tt
                                    scalar.wait_ge(SEM["v"], v_gadd(lin))
                                    if lin >= 4:
                                        scalar.wait_ge(SEM["st"], st_g(lin - 4))
                                    assert inc1(scalar.activation(
                                        gout[lin % 4][:], gtmp[lin % 4][:],
                                        AF.Silu), "a") == a_silu(lin)

        ld_b = CNT["ld"]; t_b = CNT["t"]; a_b = CNT["a"]; v_b = CNT["v"]
        st_b = CNT["st"]

        def e_of(j): return 2 * j + 2

        def stream_segs():
            segs = []
            for j in range(NSLOT):
                segs.append(("INIT", j))
                for c0 in range(0, e_of(j), 2):
                    segs.append(("CHUNK", j, c0))
            segs.append(("RFIN",))
            return segs

        SEGS1 = stream_segs()
        MERGED = []
        for sa, sb in zip(SEGS1, SEGS1):
            MERGED.append((0,) + sa)
            MERGED.append((1,) + sb)

        TPOSr = {}; APOSr = {}; VPOSr = {}; STr = {}
        tc = ac = vc = stc = 0
        for seg in MERGED:
            s = seg[0]; kind = seg[1]
            if kind == "INIT":
                j = seg[2]
                tc += 1; TPOSr[(s, "S", j, 1)] = tc
                if j >= 1:
                    tc += 1; TPOSr[(s, "R", j - 1)] = tc
                    ac += 1; APOSr[(s, "RC", j - 1)] = ac
                    vc += 1; VPOSr[(s, "REC", j - 1)] = vc
            elif kind == "CHUNK":
                j = seg[2]; c0 = seg[3]; e = e_of(j)
                if c0 + 3 < e:
                    tc += 1; TPOSr[(s, "S", j, c0 + 3)] = tc
                ac += 1; APOSr[(s, "E", j, c0)] = ac
                if c0 + 1 == e - 1:
                    tc += 1; TPOSr[(s, "L", j, e - 1)] = tc
                if c0 == 0 and j >= 1:
                    vc += 1; VPOSr[(s, "EV", j - 1)] = vc
                    stc += 16; STr[(s, j - 1)] = stc
            else:
                tc += 1; TPOSr[(s, "R", NSLOT - 1)] = tc
                ac += 1; APOSr[(s, "RC", NSLOT - 1)] = ac
                vc += 1; VPOSr[(s, "REC", NSLOT - 1)] = vc
                vc += 1; VPOSr[(s, "EV", NSLOT - 1)] = vc
                stc += 16; STr[(s, NSLOT - 1)] = stc
        TPK, APK, VPK, STK = tc, ac, vc, stc

        t_nm0 = t_b; a_nm0 = a_b; v_nm0 = v_b
        t_m = t_b + NKV
        a_m = a_b + 3 * NKV
        v_m = v_b + 4 * NKV

        def pair_of(kvh): return kvh // 2, kvh % 2
        def t_op(kvh, kind, j, t_=None):
            p, s = pair_of(kvh)
            key = (s, kind, j) if kind == "R" else (s, kind, j, t_)
            return t_m + p * TPK + TPOSr[key]
        def a_exp(kvh, j, c0):
            p, s = pair_of(kvh)
            return a_m + p * APK + APOSr[(s, "E", j, c0)]
        def a_rc(kvh, j):
            p, s = pair_of(kvh)
            return a_m + p * APK + APOSr[(s, "RC", j)]
        def v_of(kvh, kind, j):
            p, s = pair_of(kvh)
            return v_m + p * VPK + VPOSr[(s, kind, j)]
        def st_avt(kvh, j):
            p, s = pair_of(kvh)
            return st_b + p * STK + STr[(s, j)]
        def ld_kvh(kvh): return ld_b + 16 * 37 * (kvh + 1)
        def t_pair_end(p): return t_m + (p + 1) * TPK

        with ExitStack() as ps:
            krb = [ps.enter_context(nc.sbuf_tensor(f"krb{i}", [128, S], BF16)) for i in range(4)]
            vtb = [ps.enter_context(nc.sbuf_tensor(f"vtb{i}", [128, S], BF16)) for i in range(4)]
            qkb = [ps.enter_context(nc.sbuf_tensor(f"qkb{i}", [128, 8192], BF16)) for i in range(4)]
            prb = [ps.enter_context(nc.sbuf_tensor(f"prb{i}", [128, 32 * 512], BF16)) for i in range(2)]
            rsb = [ps.enter_context(nc.sbuf_tensor(f"rsb{i}", [128, 512], F32)) for i in range(2)]
            rrow = [ps.enter_context(nc.sbuf_tensor(f"rrow{i}", [1, 512], F32)) for i in range(2)]
            nmsc = ps.enter_context(nc.sbuf_tensor("nmsc", [1, 8], F32))
            avst = [ps.enter_context(nc.sbuf_tensor(f"avst{i}", [128, 512], BF16)) for i in range(2)]
            pS4 = ps.enter_context(nc.psum_tensor("pS4", [128, 2048], F32))
            pLb = [ps.enter_context(nc.psum_tensor(f"pLb{i}", [128, 512], F32)) for i in range(2)]
            pAV = [ps.enter_context(nc.psum_tensor(f"pAV{i}", [128, 512], F32)) for i in range(2)]

            def psl(s, t_):
                b = s * 1024 + (t_ % 2) * 512
                return pS4[:, b:b + 512]

            with nc.Block() as blk:
                @blk.vector
                def _(vector):
                    for kvh in range(NKV):
                        assert inc1(vector.reduce_max(
                            nmsc[:, 0:1], s_qn2[:, 4 * kvh:4 * kvh + 4], axis=AX),
                            "v") == v_nm0 + 4 * kvh + 1
                        vector.wait_ge(SEM["a"], a_nm0 + 3 * kvh + 2)
                        assert inc1(vector.tensor_tensor(
                            nmsc[:, 2:3], nmsc[:, 3:4], nmsc[:, 4:5], OP.add),
                            "v") == v_nm0 + 4 * kvh + 2
                        vector.wait_ge(SEM["a"], a_nm0 + 3 * kvh + 3)
                        vector.tensor_scalar(
                            nmsc[:, 7:8], nmsc[:, 6:7], -1.0001 * RSQ, -0.01,
                            OP.mult, OP.add)
                        assert inc1(vector.drain(), "v") == v_nm0 + 4 * kvh + 3
                        vector.wait_ge(SEM["t"], t_nm0 + kvh + 1)
                        vector.tensor_copy(s_nmall[:, kvh:kvh + 1], pLb[0][:, 0:1])
                        assert inc1(vector.drain(), "v") == v_nm0 + 4 * kvh + 4
                    for p in range(4):
                        for seg in MERGED:
                            s = seg[0]; kvh = 2 * p + s; kind = seg[1]
                            if kind == "INIT":
                                j = seg[2]
                                if j >= 1:
                                    vector.wait_ge(SEM["t"], t_op(kvh, "L", j - 1, e_of(j - 1) - 1))
                                    vector.reciprocal(rrow[s][:], pLb[s][0:1, :])
                                    assert inc1(vector.drain(), "v") == v_of(kvh, "REC", j - 1)
                            elif kind == "CHUNK":
                                j = seg[2]; c0 = seg[3]; e = e_of(j)
                                if c0 == 0 and j >= 1:
                                    vector.wait_ge(SEM["t"], t_op(kvh, "R", j - 1))
                                    vector.wait_ge(SEM["a"], a_rc(kvh, j - 1))
                                    if j >= 2:
                                        vector.wait_ge(SEM["st"], st_avt(kvh, j - 2))
                                    assert inc1(vector.tensor_tensor(
                                        avst[s][:], pAV[s][:], rsb[s][:], OP.mult),
                                        "v") == v_of(kvh, "EV", j - 1)
                            else:
                                jl = NSLOT - 1
                                vector.wait_ge(SEM["t"], t_op(kvh, "L", jl, e_of(jl) - 1))
                                vector.reciprocal(rrow[s][:], pLb[s][0:1, :])
                                assert inc1(vector.drain(), "v") == v_of(kvh, "REC", jl)
                                vector.wait_ge(SEM["t"], t_op(kvh, "R", jl))
                                vector.wait_ge(SEM["a"], a_rc(kvh, jl))
                                vector.wait_ge(SEM["st"], st_avt(kvh, jl - 1))
                                assert inc1(vector.tensor_tensor(
                                    avst[s][:], pAV[s][:], rsb[s][:], OP.mult),
                                    "v") == v_of(kvh, "EV", jl)

                @blk.scalar
                def _(scalar):
                    for kvh in range(NKV):
                        scalar.wait_ge(SEM["v"], v_nm0 + 4 * kvh + 1)
                        assert inc1(scalar.activation(nmsc[:, 3:4], nmsc[:, 0:1],
                                                      AF.Ln), "a") == a_nm0 + 3 * kvh + 1
                        assert inc1(scalar.activation(nmsc[:, 4:5], s_kn2[:, kvh:kvh + 1],
                                                      AF.Ln), "a") == a_nm0 + 3 * kvh + 2
                        scalar.wait_ge(SEM["v"], v_nm0 + 4 * kvh + 2)
                        assert inc1(scalar.activation(nmsc[:, 6:7], nmsc[:, 2:3],
                                                      AF.Exp, scale=0.5),
                                    "a") == a_nm0 + 3 * kvh + 3
                    for p in range(4):
                        for seg in MERGED:
                            s = seg[0]; kvh = 2 * p + s; kind = seg[1]
                            if kind == "INIT":
                                j = seg[2]
                                if j >= 1:
                                    scalar.wait_ge(SEM["t"], t_op(kvh, "R", j - 1))
                                    if j >= 2:
                                        scalar.wait_ge(SEM["v"], v_of(kvh, "EV", j - 2))
                                    assert inc1(scalar.copy(rsb[s][:], pLb[s][:]),
                                                "a") == a_rc(kvh, j - 1)
                            elif kind == "CHUNK":
                                j = seg[2]; c0 = seg[3]; e = e_of(j)
                                scalar.wait_ge(SEM["t"], t_op(kvh, "S", j, c0 + 1))
                                b0 = s * 1024
                                assert inc1(scalar.activation(
                                    prb[s][:, c0 * 512:(c0 + 2) * 512],
                                    pS4[:, b0:b0 + 1024], AF.Exp, scale=RSQ,
                                    bias=s_nmall[:, kvh:kvh + 1]),
                                    "a") == a_exp(kvh, j, c0)
                            else:
                                jl = NSLOT - 1
                                scalar.wait_ge(SEM["t"], t_op(kvh, "R", jl))
                                scalar.wait_ge(SEM["v"], v_of(kvh, "EV", jl - 1))
                                assert inc1(scalar.copy(rsb[s][:], pLb[s][:]),
                                            "a") == a_rc(kvh, jl)

                @blk.tensor
                def _(tensor):
                    for kvh in range(NKV):
                        tensor.wait_ge(SEM["v"], v_nm0 + 4 * kvh + 3)
                        assert inc1(tensor.matmul(pLb[0][:, 0:1], s_ones_f[:],
                                                  nmsc[:, 7:8], start=True, stop=True),
                                    "t") == t_nm0 + kvh + 1

                    def mmS(kvh, s, j, t_):
                        e = e_of(j)
                        diag = t_ >= e - 2
                        if diag:
                            tensor.matmul(
                                psl(s, t_), s_idn[:],
                                s_msk4[:, (t_ - (e - 2)) * 512:
                                       (t_ - (e - 2) + 1) * 512],
                                start=True, stop=False, skip_group_check=True)
                        inst = tensor.matmul(
                            psl(s, t_),
                            krb[kvh % 4][:, t_ * 128:(t_ + 1) * 128],
                            qkb[kvh % 4][:, j * 512:(j + 1) * 512],
                            start=not diag, stop=True, skip_group_check=True)
                        if t_ % 2 == 1:
                            assert inc1(inst, "t") == t_op(kvh, "S", j, t_)

                    tensor.wait_ge(SEM["v"], v_nm0 + 4 * NKV)
                    for p in range(4):
                        for s0 in range(2):
                            kvh = 2 * p + s0
                            tensor.wait_ge(SEM["ld"], ld_kvh(kvh))
                        for seg in MERGED:
                            s = seg[0]; kvh = 2 * p + s; kind = seg[1]
                            if kind == "INIT":
                                j = seg[2]
                                if j == 0:
                                    tensor.wait_ge(SEM["a"], a_m + p * APK if p == 0 or s == 1 else a_m + p * APK)
                                    tensor.wait_ge(SEM["a"], a_m + p * APK)
                                else:
                                    tensor.wait_ge(SEM["a"],
                                                   a_exp(kvh, j - 1, e_of(j - 1) - 2))
                                for t_ in range(2):
                                    mmS(kvh, s, j, t_)
                                if j >= 1:
                                    tensor.wait_ge(SEM["v"], v_of(kvh, "REC", j - 1))
                                    inst = tensor.matmul(pLb[s][:], s_ones_f[:],
                                                         rrow[s][:], start=True,
                                                         stop=True, skip_group_check=True)
                                    assert inc1(inst, "t") == t_op(kvh, "R", j - 1)
                            elif kind == "CHUNK":
                                j = seg[2]; c0 = seg[3]; e = e_of(j)
                                for t2 in range(c0 + 2, min(c0 + 4, e)):
                                    if t2 % 2 == 0:
                                        tensor.wait_ge(SEM["a"], a_exp(kvh, j, t2 - 2))
                                    mmS(kvh, s, j, t2)
                                if c0 + 2 >= e:
                                    tensor.wait_ge(SEM["a"], a_exp(kvh, j, c0))
                                for t_ in (c0, c0 + 1):
                                    if t_ == c0:
                                        if c0 == 0 and j >= 1:
                                            tensor.wait_ge(SEM["a"], a_rc(kvh, j - 1))
                                    inst = tensor.matmul(
                                        pLb[s][0:1, :], s_ones_bf[:],
                                        prb[s][:, t_ * 512:(t_ + 1) * 512],
                                        start=(t_ == 0), stop=(t_ == e - 1),
                                        skip_group_check=True)
                                    if t_ == e - 1:
                                        assert inc1(inst, "t") == t_op(kvh, "L", j, t_)
                                    if t_ == 0 and j >= 1:
                                        tensor.wait_ge(SEM["v"], v_of(kvh, "EV", j - 1))
                                    tensor.matmul(
                                        pAV[s][:],
                                        vtb[kvh % 4][:, t_ * 128:(t_ + 1) * 128],
                                        prb[s][:, t_ * 512:(t_ + 1) * 512],
                                        start=(t_ == 0), stop=(t_ == e - 1),
                                        skip_group_check=True)
                            else:
                                jl = NSLOT - 1
                                tensor.wait_ge(SEM["v"], v_of(kvh, "REC", jl))
                                inst = tensor.matmul(pLb[s][:], s_ones_f[:],
                                                     rrow[s][:], start=True,
                                                     stop=True, skip_group_check=True)
                                assert inc1(inst, "t") == t_op(kvh, "R", jl)

                @blk.sync
                def _(sync):
                    for kvh in range(NKV):
                        if kvh >= 4:
                            sync.wait_ge(SEM["t"], t_pair_end((kvh - 4) // 2))
                        incd(sync.dma_start(out=krb[kvh % 4][:], in_=KrT_h[kvh]), "ld")
                        for t_ in range(32):
                            incd(sync.dma_start_transpose(
                                out=vtb[kvh % 4][:, t_ * 128:(t_ + 1) * 128],
                                in_=VT_h[kvh, :, t_ * 128:(t_ + 1) * 128]), "ld")
                        for h in range(4):
                            qv = qkb[kvh % 4][:].rearrange("p (sl h q) -> p h sl q",
                                                           h=4, sl=16)
                            incd(sync.dma_start(out=qv[:, h], in_=Q_h[4 * kvh + h]), "ld")
                    sync.wait_ge(SEM["ld"], CNT["ld"])

                @blk.gpsimd
                def _(gpsimd):
                    for p in range(4):
                        for seg in MERGED:
                            s = seg[0]; kvh = 2 * p + s; kind = seg[1]
                            if kind == "CHUNK" and seg[3] == 0 and seg[2] >= 1:
                                j = seg[2]
                                gpsimd.wait_ge(SEM["v"], v_of(kvh, "EV", j - 1))
                                av = avst[s][:].rearrange("p (h q) -> p h q", h=4)
                                assert incd(gpsimd.dma_start(
                                    out=AVT_h.rearrange("n p q -> p n q")[
                                        :, 4 * kvh:4 * kvh + 4,
                                        (j - 1) * 128:j * 128],
                                    in_=av), "st") == st_avt(kvh, j - 1)
                            elif kind == "RFIN":
                                jl = NSLOT - 1
                                gpsimd.wait_ge(SEM["v"], v_of(kvh, "EV", jl))
                                av = avst[s][:].rearrange("p (h q) -> p h q", h=4)
                                assert incd(gpsimd.dma_start(
                                    out=AVT_h.rearrange("n p q -> p n q")[
                                        :, 4 * kvh:4 * kvh + 4,
                                        jl * 128:(jl + 1) * 128],
                                    in_=av), "st") == st_avt(kvh, jl)
                    gpsimd.wait_ge(SEM["st"], CNT["st"])

        ld_b = CNT["ld"]; t_b = CNT["t"]; v_b = CNT["v"]; st_b = CNT["st"]
        g_b = CNT["g"]

        def t_s(lin): return t_b + lin + 1 + lin // 8 + 1
        def t_h0(idx): return t_b + idx * 9 + 1
        def v_o(lin): return v_b + lin + 1
        def st_o(lin): return st_b + 16 * (lin + 1)
        def g_wo(idx, half): return g_b + 16 * 16 + 16 * (idx * 2 + half + 1)
        ld_avt = ld_b + 16 * 16
        def ld_gx(lin): return ld_avt + 16 * 2 * (lin + 1)

        with ExitStack() as ps:
            avres = ps.enter_context(nc.sbuf_tensor("avres", [128, 32 * TOWN], BF16))
            wob = [ps.enter_context(nc.sbuf_tensor(f"wob{i}", [128, 16 * 512], BF16))
                   for i in range(3)]
            gch = [ps.enter_context(nc.sbuf_tensor(f"gch{i}", [128, 512], F32)) for i in range(5)]
            xch = [ps.enter_context(nc.sbuf_tensor(f"xch{i}", [128, 512], F32)) for i in range(5)]
            cch = [ps.enter_context(nc.sbuf_tensor(f"cch{i}", [128, 512], F32)) for i in range(2)]
            och = [ps.enter_context(nc.sbuf_tensor(f"och{i}", [128, 512], F32)) for i in range(2)]
            ptt = [ps.enter_context(nc.psum_tensor(f"pttB{i}", [128, 512], F32))
                   for i in range(8)]

            with nc.Block() as blk:
                @blk.sync
                def _(sync):
                    for head in range(16):
                        incd(sync.dma_start(
                            out=avres[:, head * TOWN:(head + 1) * TOWN],
                            in_=AVT_h[head]), "ld")
                    for lin in range(128):
                        idx, tt = divmod(lin, 8)
                        wave, fc = divmod(idx, 8)
                        tg = wave * 8 + tt
                        if lin >= 5:
                            l2 = lin - 5
                            sync.wait_ge(SEM["v"], v_o(l2))
                            i2, tt2 = divmod(l2, 8)
                            w2, fc2 = divmod(i2, 8)
                            tg2 = w2 * 8 + tt2
                            assert incd(sync.dma_start(
                                out=out_own[tg2 * 128:(tg2 + 1) * 128,
                                            fc2 * 512:(fc2 + 1) * 512],
                                in_=och[tt2 % 2][:]), "st") == st_o(l2)
                        incd(sync.dma_start(
                            out=gch[lin % 5][:],
                            in_=G_h[tg * 128:(tg + 1) * 128,
                                    fc * 512:(fc + 1) * 512]), "ld")
                        incd(sync.dma_start(
                            out=xch[lin % 5][:],
                            in_=x_own[tg * 128:(tg + 1) * 128,
                                      fc * 512:(fc + 1) * 512]), "ld")
                    for l2 in range(123, 128):
                        i2, tt2 = divmod(l2, 8)
                        w2, fc2 = divmod(i2, 8)
                        tg2 = w2 * 8 + tt2
                        sync.wait_ge(SEM["v"], v_o(l2))
                        assert incd(sync.dma_start(
                            out=out_own[tg2 * 128:(tg2 + 1) * 128,
                                        fc2 * 512:(fc2 + 1) * 512],
                            in_=och[tt2 % 2][:]), "st") == st_o(l2)
                    sync.wait_ge(SEM["st"], CNT["st"])
                    sync.wait_ge(SEM["ld"], CNT["ld"])

                @blk.gpsimd
                def _(gpsimd):
                    for head in range(16, 32):
                        incd(gpsimd.dma_start(
                            out=avres[:, head * TOWN:(head + 1) * TOWN],
                            in_=AVT_h[head]), "g")
                    for idx in range(16):
                        wave, fc = divmod(idx, 8)
                        for half in range(2):
                            prev = idx * 2 + half - 3
                            if prev >= 0:
                                pidx, ph = divmod(prev, 2)
                                if ph == 1:
                                    gpsimd.wait_ge(SEM["t"], t_s(pidx * 8 + 7))
                                else:
                                    gpsimd.wait_ge(SEM["t"], t_h0(pidx))
                            incd(gpsimd.dma_start(
                                out=wob[(idx * 2 + half) % 3][:]
                                .rearrange("p (k c) -> p k c", k=16),
                                in_=WO[:, fc, half * 16:(half + 1) * 16]), "g")

                @blk.tensor
                def _(tensor):
                    tensor.wait_ge(SEM["ld"], ld_avt)
                    for idx in range(16):
                        wave, fc = divmod(idx, 8)
                        for half in range(2):
                            tensor.wait_ge(SEM["g"], g_wo(idx, half))
                            for tt in range(8):
                                tg = wave * 8 + tt
                                if half == 0 and idx > 0:
                                    tensor.wait_ge(SEM["v"], v_o((idx - 1) * 8 + tt))
                                for k16 in range(16):
                                    k = half * 16 + k16
                                    inst = tensor.matmul(
                                        ptt[tt][:],
                                        avres[:, k * TOWN + tg * 128:
                                              k * TOWN + (tg + 1) * 128],
                                        wob[(idx * 2 + half) % 3][:,
                                            k16 * 512:(k16 + 1) * 512],
                                        start=(half == 0 and k16 == 0),
                                        stop=(half == 1 and k16 == 15),
                                        skip_group_check=True)
                                if half == 1:
                                    assert inc1(inst, "t") == t_s(idx * 8 + tt)
                            if half == 0:
                                assert inc1(inst, "t") == t_h0(idx)

                @blk.vector
                def _(vector):
                    for idx in range(16):
                        for pair in range(4):
                            tt0 = 2 * pair; tt1 = tt0 + 1
                            lin0 = idx * 8 + tt0; lin1 = lin0 + 1
                            vector.wait_ge(SEM["t"], t_s(lin0))
                            vector.wait_ge(SEM["ld"], ld_gx(lin0))
                            if lin0 >= 2:
                                vector.wait_ge(SEM["st"], st_o(lin0 - 2))
                            vector.tensor_tensor(cch[0][:], ptt[tt0][:],
                                                 gch[lin0 % 5][:], OP.mult)
                            vector.wait_ge(SEM["t"], t_s(lin1))
                            vector.wait_ge(SEM["ld"], ld_gx(lin1))
                            if lin1 >= 2:
                                vector.wait_ge(SEM["st"], st_o(lin1 - 2))
                            vector.tensor_tensor(cch[1][:], ptt[tt1][:],
                                                 gch[lin1 % 5][:], OP.mult)
                            assert inc1(vector.tensor_tensor(
                                och[0][:], cch[0][:], xch[lin0 % 5][:], OP.add),
                                "v") == v_o(lin0)
                            assert inc1(vector.tensor_tensor(
                                och[1][:], cch[1][:], xch[lin1 % 5][:], OP.add),
                                "v") == v_o(lin1)

    return nc


_NC_CACHE = {}


def kernel(L0_attn_x, L0_attn_wq, L0_attn_wk, L0_attn_wv, L0_attn_wo,
           L0_ffn_base_w, L0_ffn_base_b, L0_ffn_A, L0_ffn_B, L0_ffn_router_w):
    x = np.asarray(L0_attn_x, np.float32)
    wq = np.asarray(L0_attn_wq, np.float32)
    wk = np.asarray(L0_attn_wk, np.float32)
    wv = np.asarray(L0_attn_wv, np.float32)
    wo = np.asarray(L0_attn_wo, np.float32)
    base_w = np.asarray(L0_ffn_base_w, np.float32)
    base_b = np.asarray(L0_ffn_base_b, np.float32)
    A = np.asarray(L0_ffn_A, np.float32)
    Bx = np.asarray(L0_ffn_B, np.float32)
    router_w = np.asarray(L0_ffn_router_w, np.float32)

    WQ = np.ascontiguousarray(wq.reshape(32, 128, 32, 128).transpose(1, 2, 0, 3))
    WK = np.ascontiguousarray(wk.reshape(32, 128, 8, 128).transpose(1, 2, 0, 3))
    WV = np.ascontiguousarray(wv.reshape(32, 128, 8, 128).transpose(1, 2, 0, 3))
    WO = np.ascontiguousarray(wo.reshape(32, 128, 8, 512).transpose(1, 2, 0, 3))
    BW = np.ascontiguousarray(
        np.ascontiguousarray(base_w.T).reshape(32, 128, 8, 512).transpose(1, 2, 0, 3))
    Bst = np.ascontiguousarray(Bx.transpose(0, 2, 1).reshape(1024, 4096))
    BST = np.ascontiguousarray(Bst.reshape(8, 128, 8, 512).transpose(1, 2, 0, 3))
    ARTf = np.concatenate([A.T, router_w.T], axis=1)
    ART = np.ascontiguousarray(ARTf.reshape(32, 128, 80).transpose(1, 0, 2))
    bias_rep = np.ascontiguousarray(np.broadcast_to(base_b[None, :], (128, 4096)))

    freqs = (np.float32(1.0) /
             np.power(np.float32(10000.0),
                      np.arange(0, HD, 2, dtype=np.float32) / np.float32(HD))
             ).astype(np.float32)
    tpos = np.arange(S, dtype=np.float32)
    ang = np.outer(tpos, freqs).astype(np.float32)
    cosr = np.cos(ang).astype(np.float32)
    sinr = np.sin(ang).astype(np.float32)
    CKV = np.ascontiguousarray(np.repeat(cosr.T, 2, axis=0))
    SKV = np.empty((HD, S), np.float32)
    SKV[0::2] = -sinr.T
    SKV[1::2] = sinr.T

    tri_add = np.where(np.arange(128)[:, None] <= np.arange(128)[None, :],
                       np.float32(0.0), np.float32(-1e9))
    MSK_role = []
    for role in range(2):
        m = np.zeros((2, 128, 128), BFD)
        if role == 0:
            m[0] = tri_add.astype(BFD)
            m[1] = BFD(-1e9)
        else:
            m[1] = tri_add.astype(BFD)
        MSK_role.append(m)

    MSK4_role = [np.concatenate([np.tile(m[0], (1, 4)), np.tile(m[1], (1, 4))],
                                axis=1) for m in MSK_role]
    IDN = np.eye(128, dtype=BFD)

    ones_col_bf = np.ones((128, 1), BFD)
    ones_row_f32 = np.ones((1, 128), np.float32)
    epscol = np.full((128, 1), EPS, np.float32)

    if "nc" not in _NC_CACHE:
        _NC_CACHE["nc"] = build_nc()
    nc = _NC_CACHE["nc"]

    in_maps = []
    for core in range(8):
        b = core // 2
        role = core % 2
        xb = x[b]
        x_own = np.ascontiguousarray(xb.reshape(32, 128, D)[role::2].reshape(TOWN, D))
        own_pos = ((np.arange(NSLOT) * 2 + role)[:, None] * 128 +
                   np.arange(128)[None, :]).reshape(-1)
        COWN = np.ascontiguousarray(CKV[:, own_pos])
        SOWN = np.ascontiguousarray(SKV[:, own_pos])
        in_maps.append(dict(
            x_own=x_own, x_kv=np.ascontiguousarray(xb),
            WQ=WQ, WK=WK, WV=WV, WO=WO, BW=BW, BST=BST, ART=ART,
            bias_rep=bias_rep, CKV=CKV, SKV=SKV, COWN=COWN, SOWN=SOWN,
            MSK=MSK_role[role], MSK4=MSK4_role[role], IDN=IDN,
            ones_col_bf=ones_col_bf,
            ones_row_f32=ones_row_f32, epscol=epscol))

    res = run_bass_kernel_spmd(nc, in_maps, core_ids=list(range(8)))

    out = np.empty((4, S, D), np.float32)
    for core in range(8):
        b = core // 2
        role = core % 2
        ob = np.asarray(res.results[core]["out_own"], np.float32)
        out[b].reshape(32, 128, D)[role::2] = ob.reshape(NSLOT, 128, D)
    return out


# version 22
# speedup vs baseline: 1.1010x; 1.0122x over previous
from contextlib import ExitStack
import numpy as np
import ml_dtypes

import concourse.bass as bass
import concourse.mybir as mybir
from concourse.bass_utils import run_bass_kernel_spmd

F32 = mybir.dt.float32
BF16 = mybir.dt.bfloat16
AF = mybir.ActivationFunctionType
OP = mybir.AluOpType
AX = mybir.AxisListType.X
BFD = ml_dtypes.bfloat16

D = 4096
S = 4096
TOWN = 2048
NH = 32
NKV = 8
HD = 128
NSLOT = 16
RSQ = float(np.float32(1.0) / np.float32(np.sqrt(np.float32(128.0))))
EPS = 1e-5
SWAP_MASK = [x for i in range(16) for x in (2 * i + 1, 2 * i)]


def build_nc():
    nc = bass.Bass("TRN2", target_bir_lowering=False)

    def inp(name, shape, dt=F32):
        return nc.dram_tensor(name, shape, dt, kind="ExternalInput").ap()

    x_own = inp("x_own", [TOWN, D])
    x_kv = inp("x_kv", [S, D])
    WQ = inp("WQ", [128, 32, 32, 128])
    WK = inp("WK", [128, 8, 32, 128])
    WV = inp("WV", [128, 8, 32, 128])
    WO = inp("WO", [128, 8, 32, 512])
    BW = inp("BW", [128, 8, 32, 512])
    BST = inp("BST", [128, 8, 8, 512])
    ART = inp("ART", [128, 32, 80])
    bias_rep = inp("bias_rep", [128, D])
    CKV = inp("CKV", [128, S])
    SKV = inp("SKV", [128, S])
    COWN = inp("COWN", [128, TOWN])
    SOWN = inp("SOWN", [128, TOWN])
    MSK = inp("MSK", [2, 128, 128], BF16)
    MSK4 = inp("MSK4", [128, 1024], BF16)
    IDN = inp("IDN", [128, 128], BF16)
    ones_col_bf = inp("ones_col_bf", [128, 1], BF16)
    ones_row_f32 = inp("ones_row_f32", [1, 128])
    epscol_in = inp("epscol", [128, 1])

    xn_own_h = nc.dram_tensor("xn_own_h", [TOWN, D], BF16).ap()
    xn_kv_h = nc.dram_tensor("xn_kv_h", [S, D], BF16).ap()
    KrT_h = nc.dram_tensor("KrT_h", [NKV, 128, S], BF16).ap()
    VT_h = nc.dram_tensor("VT_h", [NKV, 128, S], BF16).ap()
    Q_h = nc.dram_tensor("Q_h", [NH, 128, TOWN], BF16).ap()
    Wgt_h = nc.dram_tensor("Wgt_h", [TOWN, 1024], BF16).ap()
    G_h = nc.dram_tensor("G_h", [TOWN, D], F32).ap()
    AVT_h = nc.dram_tensor("AVT_h", [NH, 128, TOWN], BF16).ap()

    out_own = nc.dram_tensor("out_own", [TOWN, D], F32, kind="ExternalOutput").ap()

    CNT = dict(ld=0, st=0, g=0, t=0, a=0, v=0)

    es = ExitStack()
    with es:
        SEM = {k: es.enter_context(nc.semaphore(k)) for k in CNT}

        def incd(instr, s):
            CNT[s] += 16
            instr.then_inc(SEM[s], 16)
            return CNT[s]

        def inc1(instr, s):
            CNT[s] += 1
            instr.then_inc(SEM[s], 1)
            return CNT[s]

        s_eps = es.enter_context(nc.sbuf_tensor("s_eps", [128, 1], F32))
        s_ones_bf = es.enter_context(nc.sbuf_tensor("s_ones_bf", [128, 1], BF16))
        s_ones_f = es.enter_context(nc.sbuf_tensor("s_ones_f", [1, 128], F32))
        s_msk = es.enter_context(nc.sbuf_tensor("s_msk", [128, 256], BF16))
        s_msk4 = es.enter_context(nc.sbuf_tensor("s_msk4", [128, 1024], BF16))
        s_idn = es.enter_context(nc.sbuf_tensor("s_idn", [128, 128], BF16))
        s_kn2 = es.enter_context(nc.sbuf_tensor("s_kn2", [1, 8], F32))
        s_qn2 = es.enter_context(nc.sbuf_tensor("s_qn2", [1, 32], F32))
        s_nmall = es.enter_context(nc.sbuf_tensor("s_nmall", [128, 8], F32))

        with nc.Block() as blk:
            @blk.sync
            def _(sync):
                incd(sync.dma_start(out=s_eps[:], in_=epscol_in), "ld")
                incd(sync.dma_start(out=s_ones_bf[:], in_=ones_col_bf), "ld")
                incd(sync.dma_start(out=s_ones_f[:], in_=ones_row_f32), "ld")
                incd(sync.dma_start(
                    out=s_msk[:].rearrange("p (m q) -> p m q", m=2),
                    in_=MSK.rearrange("m p q -> p m q")), "ld")
                incd(sync.dma_start(out=s_msk4[:], in_=MSK4), "ld")
                incd(sync.dma_start(out=s_idn[:], in_=IDN), "ld")
                sync.wait_ge(SEM["ld"], CNT["ld"])

            @blk.vector
            def _(vector):
                vector.memset(s_kn2[:], 0.0)
                inc1(vector.memset(s_qn2[:], 0.0), "v")

        NT0 = 16 + 32

        def p0_src(i):
            if i < 16:
                return (x_own[i * 128:(i + 1) * 128, :],
                        xn_own_h[i * 128:(i + 1) * 128, :])
            j = i - 16
            return (x_kv[j * 128:(j + 1) * 128, :],
                    xn_kv_h[j * 128:(j + 1) * 128, :])

        ld0 = CNT["ld"]; st0 = CNT["st"]; a0 = CNT["a"]; v0 = CNT["v"]

        def a_rstd(i): return a0 + 2 * i + 1
        def a_id(i): return a0 + 2 * i + 2
        def v_var(i): return v0 + 2 * i + 1
        def v_nmr(i): return v0 + 2 * i + 2
        def ld_x(i): return ld0 + 16 * (i + 1)
        def st_xn(i): return st0 + 16 * (i + 1)

        with ExitStack() as ps:
            xb = [ps.enter_context(nc.sbuf_tensor(f"p0xb{i}", [128, D], F32)) for i in range(2)]
            xnb = [ps.enter_context(nc.sbuf_tensor(f"p0xn{i}", [128, D], BF16)) for i in range(2)]
            sstat = ps.enter_context(nc.sbuf_tensor("p0st", [128, 48], F32))
            sbn = ps.enter_context(nc.sbuf_tensor("p0bn", [128, 2], F32))
            _cols = [ps.enter_context(nc.sbuf_tensor(f"p0c{i}", [128, 1], F32))
                     for i in range(4)]
            lnc, rstdc, nmrc, spc0 = [c[:] for c in _cols]

            with nc.Block() as blk:
                @blk.sync
                def _(sync):
                    for i in range(2):
                        src, _dst = p0_src(i)
                        incd(sync.dma_start(out=xb[i % 2][:], in_=src), "ld")
                    for i in range(NT0):
                        _src, dst = p0_src(i)
                        sync.wait_ge(SEM["a"], a_id(i))
                        incd(sync.dma_start(out=dst, in_=xnb[i % 2][:]), "st")
                        if i + 2 < NT0:
                            src2, _ = p0_src(i + 2)
                            incd(sync.dma_start(out=xb[i % 2][:], in_=src2), "ld")
                    sync.wait_ge(SEM["st"], CNT["st"])

                @blk.vector
                def _(vector):
                    for i in range(NT0):
                        vector.wait_ge(SEM["ld"], ld_x(i))
                        for c in range(8):
                            vector.bn_stats(sstat[:, c * 6:(c + 1) * 6],
                                            xb[i % 2][:, c * 512:(c + 1) * 512])
                        vector.tensor_copy(spc0, s_eps[:])
                        vector.bn_aggr(sbn[:], sstat[:])
                        inc1(vector.drain(), "v")
                        vector.wait_ge(SEM["a"], a_rstd(i))
                        vector.tensor_scalar(nmrc, sbn[:, 0:1], rstdc, -1.0,
                                             OP.mult, OP.mult)
                        inc1(vector.drain(), "v")

                @blk.scalar
                def _(scalar):
                    for i in range(NT0):
                        scalar.wait_ge(SEM["ld"], ld_x(i))
                        if i >= 2:
                            scalar.wait_ge(SEM["st"], st_xn(i - 2))
                        scalar.wait_ge(SEM["v"], v_var(i))
                        scalar.activation(lnc, sbn[:, 1:2], AF.Ln, bias=s_eps[:])
                        inc1(scalar.activation(rstdc, lnc, AF.Exp, scale=-0.5), "a")
                        scalar.wait_ge(SEM["v"], v_nmr(i))
                        inc1(scalar.activation(xnb[i % 2][:], xb[i % 2][:], AF.Identity,
                                               scale=rstdc, bias=nmrc), "a")

        with ExitStack() as ps:
            wres = ps.enter_context(nc.sbuf_tensor("p1kw", [128, 8 * 32 * 128], BF16))
            xbuf = [ps.enter_context(nc.sbuf_tensor(f"p1kx{i}", [128, 32 * 512], BF16))
                    for i in range(2)]
            s_c = ps.enter_context(nc.sbuf_tensor("p1kc", [128, S], F32))
            s_s = ps.enter_context(nc.sbuf_tensor("p1ks", [128, S], F32))
            qbf = [ps.enter_context(nc.sbuf_tensor(f"p1kq{i}", [128, 512], BF16))
                   for i in range(4)]
            rot = [ps.enter_context(nc.sbuf_tensor(f"p1kr{i}", [128, 512], BF16))
                   for i in range(4)]
            qsq = [ps.enter_context(nc.sbuf_tensor(f"p1kqs{i}", [128, 512], BF16))
                   for i in range(2)]
            nrow = [ps.enter_context(nc.sbuf_tensor(f"p1kn{i}", [1, 512], F32))
                    for i in range(2)]
            vncol = ps.enter_context(nc.sbuf_tensor("p1kvn", [1, 1], F32))
            tmp1 = ps.enter_context(nc.sbuf_tensor("p1kt1", [128, 512], F32))
            tmp2 = ps.enter_context(nc.sbuf_tensor("p1kt2", [128, 512], F32))
            swp = ps.enter_context(nc.sbuf_tensor("p1ksw", [128, 512], BF16))
            pk = [ps.enter_context(nc.psum_tensor(f"p1kp{i}", [128, 512], F32))
                  for i in range(4)]
            pN = [ps.enter_context(nc.psum_tensor(f"p1kpn{i}", [1, 512], F32))
                  for i in range(2)]

            t_b = CNT["t"]; a_b = CNT["a"]; v_b = CNT["v"]; st_b = CNT["st"]
            g_b = CNT["g"]; ld_b = CNT["ld"]
            NU = 64
            T_CHAIN = {}; T_N = {}; A_EVQ = {}; A_EVN = {}
            V_ROT = {}; V_SQ = {}; V_NM = {}; ST = {}
            tc = t_b
            for u in range(NU):
                tc += 1; T_CHAIN[u] = tc
                if u >= 1:
                    tc += 1; T_N[u - 1] = tc
            tc += 1; T_N[NU - 1] = tc
            ac = a_b
            for u in range(NU):
                ac += 1; A_EVQ[u] = ac
                if u >= 1:
                    ac += 1; A_EVN[u - 1] = ac
            ac += 1; A_EVN[NU - 1] = ac
            vc = v_b
            for u in range(NU):
                vc += 1; V_ROT[u] = vc
                if u >= 1:
                    vc += 1; V_NM[u - 1] = vc
                vc += 1; V_SQ[u] = vc
            vc += 1; V_NM[NU - 1] = vc
            stc = st_b
            for u in range(NU):
                stc += 16; ST[u] = stc
            def ld_grp(g_): return ld_b + 32 + 16 * 32 * (g_ + 1)

            with nc.Block() as blk:
                @blk.sync
                def _(sync):
                    incd(sync.dma_start(out=s_c[:], in_=CKV), "ld")
                    incd(sync.dma_start(out=s_s[:], in_=SKV), "ld")
                    for g_ in range(8):
                        if g_ >= 2:
                            sync.wait_ge(SEM["t"], T_CHAIN[(g_ - 2) * 8 + 7])
                        for k in range(32):
                            incd(sync.dma_start_transpose(
                                out=xbuf[g_ % 2][:, k * 512:(k + 1) * 512],
                                in_=xn_kv_h[g_ * 512:(g_ + 1) * 512,
                                            k * 128:(k + 1) * 128]), "ld")
                    sync.wait_ge(SEM["ld"], CNT["ld"])

                @blk.gpsimd
                def _(gpsimd):
                    incd(gpsimd.dma_start(
                        out=wres[:].rearrange("p (f k c) -> p f k c", f=8, k=32),
                        in_=WK), "g")
                    for u in range(NU):
                        g_, f = divmod(u, 8)
                        gpsimd.wait_ge(SEM["v"], V_ROT[u])
                        assert incd(gpsimd.dma_start(
                            out=KrT_h[f, :, g_ * 512:(g_ + 1) * 512],
                            in_=rot[u % 4][:]), "st") == ST[u]
                    gpsimd.wait_ge(SEM["st"], CNT["st"])
                g_cast = g_b + 16

                @blk.tensor
                def _(tensor):
                    tensor.wait_ge(SEM["g"], g_cast)
                    for u in range(NU):
                        g_, f = divmod(u, 8)
                        if f == 0:
                            tensor.wait_ge(SEM["ld"], ld_grp(g_))
                        if u >= 4:
                            tensor.wait_ge(SEM["a"], A_EVQ[u - 4])
                        for k in range(32):
                            inst = tensor.matmul(
                                pk[u % 4][:],
                                wres[:, (f * 32 + k) * 128:(f * 32 + k + 1) * 128],
                                xbuf[g_ % 2][:, k * 512:(k + 1) * 512],
                                start=(k == 0), stop=(k == 31))
                        assert inc1(inst, "t") == T_CHAIN[u]
                        if u >= 1:
                            up = u - 1
                            tensor.wait_ge(SEM["v"], V_SQ[up])
                            if up >= 2:
                                tensor.wait_ge(SEM["a"], A_EVN[up - 2])
                            assert inc1(tensor.matmul(pN[up % 2][:], s_ones_bf[:],
                                                      qsq[up % 2][:], start=True,
                                                      stop=True), "t") == T_N[up]
                    up = NU - 1
                    tensor.wait_ge(SEM["v"], V_SQ[up])
                    tensor.wait_ge(SEM["a"], A_EVN[up - 2])
                    assert inc1(tensor.matmul(pN[up % 2][:], s_ones_bf[:],
                                              qsq[up % 2][:], start=True,
                                              stop=True), "t") == T_N[up]

                @blk.scalar
                def _(scalar):
                    def evq(u):
                        scalar.wait_ge(SEM["t"], T_CHAIN[u])
                        if u >= 4:
                            scalar.wait_ge(SEM["v"], V_ROT[u - 4])
                        assert inc1(scalar.copy(qbf[u % 4][:], pk[u % 4][:]),
                                    "a") == A_EVQ[u]

                    def evn(u):
                        scalar.wait_ge(SEM["t"], T_N[u])
                        if u >= 2:
                            scalar.wait_ge(SEM["v"], V_NM[u - 2])
                        assert inc1(scalar.copy(nrow[u % 2][:], pN[u % 2][:]),
                                    "a") == A_EVN[u]

                    for u in range(NU):
                        evq(u)
                        if u >= 1:
                            evn(u - 1)
                    evn(NU - 1)

                @blk.vector
                def _(vector):
                    for u in range(NU):
                        g_, f = divmod(u, 8)
                        csl = s_c[:, g_ * 512:(g_ + 1) * 512]
                        ssl = s_s[:, g_ * 512:(g_ + 1) * 512]
                        vector.wait_ge(SEM["a"], A_EVQ[u])
                        vector.stream_shuffle(swp[:], qbf[u % 4][:], SWAP_MASK)
                        vector.tensor_tensor(tmp1[:], qbf[u % 4][:], csl, OP.mult)
                        vector.tensor_tensor(tmp2[:], swp[:], ssl, OP.mult)
                        if u >= 1:
                            vector.wait_ge(SEM["a"], A_EVN[u - 1])
                            vector.reduce_max(vncol[:], nrow[(u - 1) % 2][:], axis=AX)
                        else:
                            vector.tensor_copy(swp[:, 0:1], s_eps[:])
                        if u >= 4:
                            vector.wait_ge(SEM["st"], ST[u - 4])
                        assert inc1(vector.tensor_tensor(rot[u % 4][:], tmp1[:],
                                                         tmp2[:], OP.add),
                                    "v") == V_ROT[u]
                        if u >= 1:
                            _, pf = divmod(u - 1, 8)
                            nt = s_kn2[:, pf:pf + 1]
                            assert inc1(vector.tensor_tensor(nt, nt, vncol[:],
                                                             OP.max), "v") == V_NM[u - 1]
                        else:
                            vector.tensor_copy(swp[:, 0:1], s_eps[:])
                        if u >= 2:
                            vector.wait_ge(SEM["t"], T_N[u - 2])
                        assert inc1(vector.tensor_tensor(qsq[u % 2][:], rot[u % 4][:],
                                                         rot[u % 4][:], OP.mult),
                                    "v") == V_SQ[u]
                    up = NU - 1
                    _, pf = divmod(up, 8)
                    vector.wait_ge(SEM["a"], A_EVN[up])
                    vector.reduce_max(vncol[:], nrow[up % 2][:], axis=AX)
                    vector.tensor_copy(swp[:, 0:1], s_eps[:])
                    nt = s_kn2[:, pf:pf + 1]
                    assert inc1(vector.tensor_tensor(nt, nt, vncol[:],
                                                     OP.max), "v") == V_NM[up]

        with ExitStack() as ps:
            wres = ps.enter_context(nc.sbuf_tensor("p1vw", [128, 8 * 32 * 128], BF16))
            xbuf = [ps.enter_context(nc.sbuf_tensor(f"p1vx{i}", [128, 32 * 512], BF16))
                    for i in range(2)]
            vbf = [ps.enter_context(nc.sbuf_tensor(f"p1vq{i}", [128, 512], BF16))
                   for i in range(4)]
            pk = [ps.enter_context(nc.psum_tensor(f"p1vp{i}", [128, 512], F32))
                  for i in range(4)]

            t_b = CNT["t"]; a_b = CNT["a"]; st_b = CNT["st"]
            g_b = CNT["g"]; ld_b = CNT["ld"]
            NU = 64
            def tchain(u): return t_b + u + 1
            def aev(u): return a_b + u + 1
            def stv(u): return st_b + 16 * (u + 1)
            def ld_grp(g_): return ld_b + 16 * 32 * (g_ + 1)

            with nc.Block() as blk:
                @blk.sync
                def _(sync):
                    for g_ in range(8):
                        if g_ >= 2:
                            sync.wait_ge(SEM["t"], tchain((g_ - 2) * 8 + 7))
                        for k in range(32):
                            incd(sync.dma_start_transpose(
                                out=xbuf[g_ % 2][:, k * 512:(k + 1) * 512],
                                in_=xn_kv_h[g_ * 512:(g_ + 1) * 512,
                                            k * 128:(k + 1) * 128]), "ld")
                    sync.wait_ge(SEM["ld"], CNT["ld"])

                @blk.gpsimd
                def _(gpsimd):
                    incd(gpsimd.dma_start(
                        out=wres[:].rearrange("p (f k c) -> p f k c", f=8, k=32),
                        in_=WV), "g")
                    for u in range(NU):
                        g_, f = divmod(u, 8)
                        gpsimd.wait_ge(SEM["a"], aev(u))
                        assert incd(gpsimd.dma_start(
                            out=VT_h[f, :, g_ * 512:(g_ + 1) * 512],
                            in_=vbf[u % 4][:]), "st") == stv(u)
                    gpsimd.wait_ge(SEM["st"], CNT["st"])
                g_cast = g_b + 16

                @blk.tensor
                def _(tensor):
                    tensor.wait_ge(SEM["g"], g_cast)
                    for u in range(NU):
                        g_, f = divmod(u, 8)
                        if f == 0:
                            tensor.wait_ge(SEM["ld"], ld_grp(g_))
                        if u >= 4:
                            tensor.wait_ge(SEM["a"], aev(u - 4))
                        for k in range(32):
                            inst = tensor.matmul(
                                pk[u % 4][:],
                                wres[:, (f * 32 + k) * 128:(f * 32 + k + 1) * 128],
                                xbuf[g_ % 2][:, k * 512:(k + 1) * 512],
                                start=(k == 0), stop=(k == 31))
                        assert inc1(inst, "t") == tchain(u)

                @blk.scalar
                def _(scalar):
                    for u in range(NU):
                        scalar.wait_ge(SEM["t"], tchain(u))
                        if u >= 4:
                            scalar.wait_ge(SEM["st"], stv(u - 4))
                        assert inc1(scalar.copy(vbf[u % 4][:], pk[u % 4][:]),
                                    "a") == aev(u)

        with ExitStack() as span:
            with ExitStack() as ps:
                xnT = ps.enter_context(nc.sbuf_tensor("xnT", [128, 32 * TOWN], BF16))
                wqb = [ps.enter_context(nc.sbuf_tensor(f"wqb{i}", [128, 4096], BF16))
                       for i in range(2)]
                s_co = ps.enter_context(nc.sbuf_tensor("s_co", [128, TOWN], F32))
                s_so = ps.enter_context(nc.sbuf_tensor("s_so", [128, TOWN], F32))
                qbf = [ps.enter_context(nc.sbuf_tensor(f"qbf{i}", [128, 512], BF16))
                       for i in range(4)]
                rot = [ps.enter_context(nc.sbuf_tensor(f"qrot{i}", [128, 512], BF16))
                       for i in range(4)]
                qsq = [ps.enter_context(nc.sbuf_tensor(f"qsq{i}", [128, 512], BF16))
                       for i in range(2)]
                nrow = [ps.enter_context(nc.sbuf_tensor(f"qnr{i}", [1, 512], F32))
                        for i in range(2)]
                vncol = ps.enter_context(nc.sbuf_tensor("qvn", [1, 1], F32))
                tmp1 = ps.enter_context(nc.sbuf_tensor("qt1", [128, 512], F32))
                tmp2 = ps.enter_context(nc.sbuf_tensor("qt2", [128, 512], F32))
                swp = ps.enter_context(nc.sbuf_tensor("qsw", [128, 512], BF16))
                pk = [ps.enter_context(nc.psum_tensor(f"p2ap{i}", [128, 512], F32))
                      for i in range(4)]
                pN = [ps.enter_context(nc.psum_tensor(f"p2apn{i}", [1, 512], F32))
                      for i in range(2)]

                t_b = CNT["t"]; a_b = CNT["a"]; v_b = CNT["v"]; st_b = CNT["st"]
                g_b = CNT["g"]; ld_b = CNT["ld"]
                NU = 128
                T_CHAIN = {}; T_N = {}; A_EVQ = {}; A_EVN = {}
                V_ROT = {}; V_SQ = {}; V_NM = {}; ST = {}
                tc = t_b
                for u in range(NU):
                    tc += 1; T_CHAIN[u] = tc
                    if u >= 1:
                        tc += 1; T_N[u - 1] = tc
                tc += 1; T_N[NU - 1] = tc
                ac = a_b
                for u in range(NU):
                    ac += 1; A_EVQ[u] = ac
                    if u >= 1:
                        ac += 1; A_EVN[u - 1] = ac
                ac += 1; A_EVN[NU - 1] = ac
                vc = v_b
                for u in range(NU):
                    vc += 1; V_ROT[u] = vc
                    if u >= 1:
                        vc += 1; V_NM[u - 1] = vc
                    vc += 1; V_SQ[u] = vc
                vc += 1; V_NM[NU - 1] = vc
                stc = st_b
                for u in range(NU):
                    stc += 16; ST[u] = stc
                ld_xnT = ld_b + 16 * 2 + 16 * 128
                def g_wq(f): return g_b + 16 * (f + 1)

                with nc.Block() as blk:
                    @blk.sync
                    def _(sync):
                        incd(sync.dma_start(out=s_co[:], in_=COWN), "ld")
                        incd(sync.dma_start(out=s_so[:], in_=SOWN), "ld")
                        for k in range(32):
                            for gg in range(4):
                                incd(sync.dma_start_transpose(
                                    out=xnT[:, k * TOWN + gg * 512: k * TOWN + (gg + 1) * 512],
                                    in_=xn_own_h[gg * 512:(gg + 1) * 512,
                                                 k * 128:(k + 1) * 128]), "ld")
                        sync.wait_ge(SEM["ld"], CNT["ld"])

                    @blk.gpsimd
                    def _(gpsimd):
                        for f in range(2):
                            incd(gpsimd.dma_start(
                                out=wqb[f % 2][:].rearrange("p (k c) -> p k c", k=32),
                                in_=WQ[:, f]), "g")
                        for u in range(NU):
                            f, gg = divmod(u, 4)
                            if gg == 0 and f + 2 < 32:
                                gpsimd.wait_ge(SEM["t"], T_CHAIN[f * 4 + 3])
                                incd(gpsimd.dma_start(
                                    out=wqb[f % 2][:].rearrange("p (k c) -> p k c", k=32),
                                    in_=WQ[:, f + 2]), "g")
                            gpsimd.wait_ge(SEM["v"], V_ROT[u])
                            assert incd(gpsimd.dma_start(
                                out=Q_h[f, :, gg * 512:(gg + 1) * 512],
                                in_=rot[u % 4][:]), "st") == ST[u]
                        gpsimd.wait_ge(SEM["st"], CNT["st"])

                    @blk.tensor
                    def _(tensor):
                        tensor.wait_ge(SEM["ld"], ld_xnT)
                        for u in range(NU):
                            f, gg = divmod(u, 4)
                            if gg == 0:
                                tensor.wait_ge(SEM["g"], g_wq(f))
                            if u >= 4:
                                tensor.wait_ge(SEM["a"], A_EVQ[u - 4])
                            for k in range(32):
                                inst = tensor.matmul(
                                    pk[u % 4][:],
                                    wqb[f % 2][:, k * 128:(k + 1) * 128],
                                    xnT[:, k * TOWN + gg * 512: k * TOWN + (gg + 1) * 512],
                                    start=(k == 0), stop=(k == 31))
                            assert inc1(inst, "t") == T_CHAIN[u]
                            if u >= 1:
                                up = u - 1
                                tensor.wait_ge(SEM["v"], V_SQ[up])
                                if up >= 2:
                                    tensor.wait_ge(SEM["a"], A_EVN[up - 2])
                                assert inc1(tensor.matmul(pN[up % 2][:], s_ones_bf[:],
                                                          qsq[up % 2][:], start=True,
                                                          stop=True), "t") == T_N[up]
                        up = NU - 1
                        tensor.wait_ge(SEM["v"], V_SQ[up])
                        tensor.wait_ge(SEM["a"], A_EVN[up - 2])
                        assert inc1(tensor.matmul(pN[up % 2][:], s_ones_bf[:],
                                                  qsq[up % 2][:], start=True,
                                                  stop=True), "t") == T_N[up]

                    @blk.scalar
                    def _(scalar):
                        def evq(u):
                            scalar.wait_ge(SEM["t"], T_CHAIN[u])
                            if u >= 4:
                                scalar.wait_ge(SEM["v"], V_ROT[u - 4])
                            assert inc1(scalar.copy(qbf[u % 4][:], pk[u % 4][:]),
                                        "a") == A_EVQ[u]

                        def evn(u):
                            scalar.wait_ge(SEM["t"], T_N[u])
                            if u >= 2:
                                scalar.wait_ge(SEM["v"], V_NM[u - 2])
                            assert inc1(scalar.copy(nrow[u % 2][:], pN[u % 2][:]),
                                        "a") == A_EVN[u]

                        for u in range(NU):
                            evq(u)
                            if u >= 1:
                                evn(u - 1)
                        evn(NU - 1)

                    @blk.vector
                    def _(vector):
                        for u in range(NU):
                            f, gg = divmod(u, 4)
                            csl = s_co[:, gg * 512:(gg + 1) * 512]
                            ssl = s_so[:, gg * 512:(gg + 1) * 512]
                            vector.wait_ge(SEM["a"], A_EVQ[u])
                            vector.stream_shuffle(swp[:], qbf[u % 4][:], SWAP_MASK)
                            vector.tensor_tensor(tmp1[:], qbf[u % 4][:], csl, OP.mult)
                            vector.tensor_tensor(tmp2[:], swp[:], ssl, OP.mult)
                            if u >= 1:
                                vector.wait_ge(SEM["a"], A_EVN[u - 1])
                                vector.reduce_max(vncol[:], nrow[(u - 1) % 2][:], axis=AX)
                            else:
                                vector.tensor_copy(swp[:, 0:1], s_eps[:])
                            if u >= 4:
                                vector.wait_ge(SEM["st"], ST[u - 4])
                            assert inc1(vector.tensor_tensor(rot[u % 4][:], tmp1[:],
                                                             tmp2[:], OP.add),
                                        "v") == V_ROT[u]
                            if u >= 1:
                                pf, _pgg = divmod(u - 1, 4)
                                nt = s_qn2[:, pf:pf + 1]
                                assert inc1(vector.tensor_tensor(nt, nt, vncol[:],
                                                                 OP.max), "v") == V_NM[u - 1]
                            else:
                                vector.tensor_copy(swp[:, 0:1], s_eps[:])
                            if u >= 2:
                                vector.wait_ge(SEM["t"], T_N[u - 2])
                            assert inc1(vector.tensor_tensor(qsq[u % 2][:], rot[u % 4][:],
                                                             rot[u % 4][:], OP.mult),
                                        "v") == V_SQ[u]
                        up = NU - 1
                        pf, _pgg = divmod(up, 4)
                        vector.wait_ge(SEM["a"], A_EVN[up])
                        vector.reduce_max(vncol[:], nrow[up % 2][:], axis=AX)
                        vector.tensor_copy(swp[:, 0:1], s_eps[:])
                        nt = s_qn2[:, pf:pf + 1]
                        assert inc1(vector.tensor_tensor(nt, nt, vncol[:],
                                                         OP.max), "v") == V_NM[up]

                t_b = CNT["t"]; a_b = CNT["a"]; v_b = CNT["v"]; st_b = CNT["st"]
                g_b = CNT["g"]
                def t_rl(tt): return t_b + tt + 1
                def a_ex(tt): return a_b + tt + 1
                def v_pre(tt): return v_b + tt * 3 + 1
                def v_cmb(tt): return v_b + tt * 3 + 2
                def v_wg(tt): return v_b + tt * 3 + 3
                def st_wg(tt): return st_b + 16 * (tt + 1)
                g_art = g_b + 16

                with ExitStack() as ps2:
                    artb = ps2.enter_context(nc.sbuf_tensor("artb", [128, 32 * 80], BF16))
                    rh_all = ps2.enter_context(nc.sbuf_tensor("rh_all", [128, 1024], F32))
                    unorm = [ps2.enter_context(nc.sbuf_tensor(f"unorm{i}", [128, 16], F32))
                             for i in range(2)]
                    selb = ps2.enter_context(nc.sbuf_tensor("selb", [128, 16], F32))
                    comb = ps2.enter_context(nc.sbuf_tensor("comb", [128, 16], F32))
                    m8 = ps2.enter_context(nc.sbuf_tensor("m8", [128, 8], F32))
                    _rc = [ps2.enter_context(nc.sbuf_tensor(f"p2brc{i}", [128, 1], F32))
                           for i in range(8)]
                    mxc, nmxc, Sc, dnc, rfc, s1c, spcc, _x = [c[:] for c in _rc]
                    wgt = [ps2.enter_context(nc.sbuf_tensor(f"wgt{i}", [128, 1024], BF16))
                           for i in range(2)]
                    prl = [ps2.enter_context(nc.psum_tensor(f"prl{i}", [128, 80], F32))
                           for i in range(2)]

                    with nc.Block() as blk:
                        @blk.gpsimd
                        def _(gpsimd):
                            incd(gpsimd.dma_start(
                                out=artb[:].rearrange("p (k c) -> p k c", k=32),
                                in_=ART), "g")

                        @blk.tensor
                        def _(tensor):
                            tensor.wait_ge(SEM["g"], g_art)
                            for tt in range(16):
                                if tt >= 2:
                                    tensor.wait_ge(SEM["v"], v_cmb(tt - 2))
                                for k in range(32):
                                    inst = tensor.matmul(
                                        prl[tt % 2][:],
                                        xnT[:, k * TOWN + tt * 128: k * TOWN + (tt + 1) * 128],
                                        artb[:, k * 80:(k + 1) * 80],
                                        start=(k == 0), stop=(k == 31))
                                assert inc1(inst, "t") == t_rl(tt)

                        @blk.vector
                        def _(vector):
                            for tt in range(16):
                                vector.wait_ge(SEM["t"], t_rl(tt))
                                vector.reduce_max(mxc, prl[tt % 2][:, 64:80], axis=AX)
                                vector.tensor_copy(rh_all[:, tt * 64:(tt + 1) * 64],
                                                   prl[tt % 2][:, 0:64])
                                vector.tensor_scalar(nmxc, mxc, -1.0, None, OP.mult)
                                assert inc1(vector.drain(), "v") == v_pre(tt)
                                vector.wait_ge(SEM["a"], a_ex(tt))
                                vector.max(m8[:], unorm[tt % 2][:])
                                vector.tensor_scalar(dnc, Sc, 1e-6, None, OP.mult)
                                vector.tensor_tensor(s1c, m8[:, 0:1], m8[:, 1:2], OP.add)
                                vector.tensor_scalar(selb[:], unorm[tt % 2][:],
                                                     m8[:, 1:2], None, OP.is_ge)
                                vector.tensor_tensor(dnc, dnc, s1c, OP.add)
                                vector.tensor_copy(spcc, s_eps[:])
                                vector.reciprocal(rfc, dnc)
                                vector.tensor_copy(spcc, s_eps[:])
                                assert inc1(vector.scalar_tensor_tensor(
                                    comb[:], unorm[tt % 2][:], rfc, selb[:],
                                    OP.mult, OP.mult), "v") == v_cmb(tt)
                                if tt >= 2:
                                    vector.wait_ge(SEM["st"], st_wg(tt - 2))
                                vector.tensor_copy(spcc, s_eps[:])
                                for e in range(16):
                                    inst = vector.tensor_scalar(
                                        wgt[tt % 2][:, e * 64:(e + 1) * 64],
                                        rh_all[:, tt * 64:(tt + 1) * 64],
                                        comb[:, e:e + 1], 0.5, OP.mult, OP.mult)
                                assert inc1(inst, "v") == v_wg(tt)

                        @blk.scalar
                        def _(scalar):
                            for tt in range(16):
                                scalar.wait_ge(SEM["v"], v_pre(tt))
                                assert inc1(scalar.activation(
                                    unorm[tt % 2][:], prl[tt % 2][:, 64:80], AF.Exp,
                                    bias=nmxc, accum_out=Sc), "a") == a_ex(tt)

                        @blk.sync
                        def _(sync):
                            for tt in range(16):
                                sync.wait_ge(SEM["v"], v_wg(tt))
                                assert incd(sync.dma_start(
                                    out=Wgt_h[tt * 128:(tt + 1) * 128, :],
                                    in_=wgt[tt % 2][:]), "st") == st_wg(tt)
                            sync.wait_ge(SEM["st"], CNT["st"])

            t_b = CNT["t"]; a_b = CNT["a"]; v_b = CNT["v"]; st_b = CNT["st"]
            g_b = CNT["g"]; ld_b = CNT["ld"]

            def t_g(lin): return t_b + lin + 1 + (2 if lin >= 56 else 0)
            def t_xf(half): return t_b + 57 + half
            def a_silu(lin): return a_b + lin + 1
            def v_gadd(lin): return v_b + lin + 1
            def st_g(lin): return st_b + 16 * (lin + 1)
            ld_w0 = ld_b + 16 * 72
            def ld_w1(half): return ld_b + 16 * (72 + 32 * (half + 1))
            g_bias = g_b + 16
            def g_bw(idx, half): return g_bias + 16 * (idx * 3 + half + 1)
            def g_bst(idx): return g_bias + 16 * (idx * 3 + 3)

            with ExitStack() as ps:
                wgtT = ps.enter_context(nc.sbuf_tensor("wgtT", [128, 8 * TOWN], BF16))
                xnw = ps.enter_context(nc.sbuf_tensor("xnw", [128, 32 * 1024], BF16))
                bwb = [ps.enter_context(nc.sbuf_tensor(f"bwb{i}", [128, 16 * 512], BF16))
                       for i in range(4)]
                bstb = [ps.enter_context(nc.sbuf_tensor(f"bstb{i}", [128, 8 * 512], BF16))
                        for i in range(2)]
                biasB = ps.enter_context(nc.sbuf_tensor("biasB", [128, D], BF16))
                gtmp = [ps.enter_context(nc.sbuf_tensor(f"gtmp{i}", [128, 512], F32))
                        for i in range(4)]
                gout = [ps.enter_context(nc.sbuf_tensor(f"gout{i}", [128, 512], F32))
                        for i in range(4)]

                with nc.Block() as blk:
                    @blk.sync
                    def _(sync):
                        for ek in range(8):
                            incd(sync.dma_start_transpose(
                                out=wgtT[:, ek * TOWN:(ek + 1) * TOWN],
                                in_=Wgt_h[:, ek * 128:(ek + 1) * 128]), "ld")

                    @blk.gpsimd
                    def _(gpsimd):
                        assert incd(gpsimd.dma_start(out=biasB[:], in_=bias_rep),
                                    "g") == g_bias

                with ExitStack() as psm:
                    ptt = [psm.enter_context(nc.psum_tensor(f"pttA{i}", [128, 512], F32))
                           for i in range(8)]
                    with nc.Block() as blk:
                        @blk.sync
                        def _(sync):
                            def xnw_load(wave, k, hh):
                                tok0 = wave * 1024 + hh * 512
                                incd(sync.dma_start_transpose(
                                    out=xnw[:, k * 1024 + hh * 512:
                                            k * 1024 + (hh + 1) * 512],
                                    in_=xn_own_h[tok0:tok0 + 512,
                                                 k * 128:(k + 1) * 128]), "ld")

                            def g_store(lin):
                                idx, tt = divmod(lin, 8)
                                wave, fc = divmod(idx, 8)
                                tg = wave * 8 + tt
                                sync.wait_ge(SEM["a"], a_silu(lin))
                                assert incd(sync.dma_start(
                                    out=G_h[tg * 128:(tg + 1) * 128,
                                            fc * 512:(fc + 1) * 512],
                                    in_=gout[lin % 4][:]), "st") == st_g(lin)

                            for k in range(32):
                                for hh in range(2):
                                    xnw_load(0, k, hh)
                            for lin in range(56):
                                g_store(lin)
                            for half in range(2):
                                sync.wait_ge(SEM["t"], t_xf(half))
                                for k16 in range(16):
                                    for hh in range(2):
                                        xnw_load(1, half * 16 + k16, hh)
                            for lin in range(56, 128):
                                g_store(lin)
                            sync.wait_ge(SEM["st"], CNT["st"])

                        @blk.gpsimd
                        def _(gpsimd):
                            for idx in range(16):
                                wave, fc = divmod(idx, 8)
                                if idx >= 2:
                                    gpsimd.wait_ge(SEM["t"], t_g((idx - 2) * 8 + 7))
                                for half in range(2):
                                    incd(gpsimd.dma_start(
                                        out=bwb[(idx % 2) * 2 + half][:]
                                        .rearrange("p (k c) -> p k c", k=16),
                                        in_=BW[:, fc, half * 16:(half + 1) * 16]), "g")
                                incd(gpsimd.dma_start(
                                    out=bstb[idx % 2][:].rearrange("p (k c) -> p k c", k=8),
                                    in_=BST[:, fc]), "g")

                        @blk.tensor
                        def _(tensor):
                            for idx in range(16):
                                wave, fc = divmod(idx, 8)
                                if idx == 0:
                                    tensor.wait_ge(SEM["ld"], ld_w0)
                                for half in range(2):
                                    if idx == 8:
                                        tensor.wait_ge(SEM["ld"], ld_w1(half))
                                    tensor.wait_ge(SEM["g"], g_bw(idx, half))
                                    for tt in range(8):
                                        if half == 0 and idx > 0:
                                            tensor.wait_ge(
                                                SEM["v"], v_gadd((idx - 1) * 8 + tt))
                                        for k16 in range(16):
                                            k = half * 16 + k16
                                            inst = tensor.matmul(
                                                ptt[tt][:],
                                                xnw[:, k * 1024 + tt * 128:
                                                    k * 1024 + (tt + 1) * 128],
                                                bwb[(idx % 2) * 2 + half][:,
                                                    k16 * 512:(k16 + 1) * 512],
                                                start=(half == 0 and k16 == 0),
                                                stop=False,
                                                skip_group_check=True)
                                    if idx == 7:
                                        assert inc1(inst, "t") == t_xf(half)
                                tensor.wait_ge(SEM["g"], g_bst(idx))
                                for tt in range(8):
                                    tg = wave * 8 + tt
                                    for ek in range(8):
                                        inst = tensor.matmul(
                                            ptt[tt][:],
                                            wgtT[:, ek * TOWN + tg * 128:
                                                 ek * TOWN + (tg + 1) * 128],
                                            bstb[idx % 2][:, ek * 512:(ek + 1) * 512],
                                            start=False, stop=(ek == 7),
                                            skip_group_check=True)
                                    assert inc1(inst, "t") == t_g(idx * 8 + tt)

                        @blk.vector
                        def _(vector):
                            vector.wait_ge(SEM["g"], g_bias)
                            for idx in range(16):
                                wave, fc = divmod(idx, 8)
                                for tt in range(8):
                                    lin = idx * 8 + tt
                                    vector.wait_ge(SEM["t"], t_g(lin))
                                    if lin >= 4:
                                        vector.wait_ge(SEM["a"], a_silu(lin - 4))
                                    assert inc1(vector.tensor_tensor(
                                        gtmp[lin % 4][:], ptt[tt][:],
                                        biasB[:, fc * 512:(fc + 1) * 512],
                                        OP.add), "v") == v_gadd(lin)

                        @blk.scalar
                        def _(scalar):
                            for idx in range(16):
                                for tt in range(8):
                                    lin = idx * 8 + tt
                                    scalar.wait_ge(SEM["v"], v_gadd(lin))
                                    if lin >= 4:
                                        scalar.wait_ge(SEM["st"], st_g(lin - 4))
                                    assert inc1(scalar.activation(
                                        gout[lin % 4][:], gtmp[lin % 4][:],
                                        AF.Silu), "a") == a_silu(lin)

        ld_b = CNT["ld"]; t_b = CNT["t"]; a_b = CNT["a"]; v_b = CNT["v"]
        st_b = CNT["st"]

        def e_of(j): return 2 * j + 2

        def stream_segs():
            segs = []
            for j in range(NSLOT):
                segs.append(("INIT", j))
                for c0 in range(0, e_of(j), 2):
                    segs.append(("CHUNK", j, c0))
            segs.append(("RFIN",))
            return segs

        SEGS1 = stream_segs()
        SKEW = 3
        MERGED = []
        for i in range(len(SEGS1) + SKEW):
            if i < len(SEGS1):
                MERGED.append((0,) + SEGS1[i])
            if i - SKEW >= 0:
                MERGED.append((1,) + SEGS1[i - SKEW])

        TPOSr = {}; APOSr = {}; VPOSr = {}; STr = {}
        tc = ac = vc = stc = 0
        for seg in MERGED:
            s = seg[0]; kind = seg[1]
            if kind == "INIT":
                j = seg[2]
                tc += 1; TPOSr[(s, "S", j, 1)] = tc
                if j >= 1:
                    tc += 1; TPOSr[(s, "R", j - 1)] = tc
                    ac += 1; APOSr[(s, "RC", j - 1)] = ac
                    vc += 1; VPOSr[(s, "REC", j - 1)] = vc
            elif kind == "CHUNK":
                j = seg[2]; c0 = seg[3]; e = e_of(j)
                if c0 + 3 < e:
                    tc += 1; TPOSr[(s, "S", j, c0 + 3)] = tc
                ac += 1; APOSr[(s, "E", j, c0)] = ac
                if c0 + 1 == e - 1:
                    tc += 1; TPOSr[(s, "L", j, e - 1)] = tc
                if c0 == 0 and j >= 1:
                    vc += 1; VPOSr[(s, "EV", j - 1)] = vc
                    stc += 16; STr[(s, j - 1)] = stc
            else:
                tc += 1; TPOSr[(s, "R", NSLOT - 1)] = tc
                ac += 1; APOSr[(s, "RC", NSLOT - 1)] = ac
                vc += 1; VPOSr[(s, "REC", NSLOT - 1)] = vc
                vc += 1; VPOSr[(s, "EV", NSLOT - 1)] = vc
                stc += 16; STr[(s, NSLOT - 1)] = stc
        TPK, APK, VPK, STK = tc, ac, vc, stc

        t_nm0 = t_b; a_nm0 = a_b; v_nm0 = v_b
        t_m = t_b + NKV
        a_m = a_b + 3 * NKV
        v_m = v_b + 4 * NKV

        def pair_of(kvh): return kvh // 2, kvh % 2
        def t_op(kvh, kind, j, t_=None):
            p, s = pair_of(kvh)
            key = (s, kind, j) if kind == "R" else (s, kind, j, t_)
            return t_m + p * TPK + TPOSr[key]
        def a_exp(kvh, j, c0):
            p, s = pair_of(kvh)
            return a_m + p * APK + APOSr[(s, "E", j, c0)]
        def a_rc(kvh, j):
            p, s = pair_of(kvh)
            return a_m + p * APK + APOSr[(s, "RC", j)]
        def v_of(kvh, kind, j):
            p, s = pair_of(kvh)
            return v_m + p * VPK + VPOSr[(s, kind, j)]
        def st_avt(kvh, j):
            p, s = pair_of(kvh)
            return st_b + p * STK + STr[(s, j)]
        def ld_kvh(kvh): return ld_b + 16 * 37 * (kvh + 1)
        def t_pair_end(p): return t_m + (p + 1) * TPK

        with ExitStack() as ps:
            krb = [ps.enter_context(nc.sbuf_tensor(f"krb{i}", [128, S], BF16)) for i in range(4)]
            vtb = [ps.enter_context(nc.sbuf_tensor(f"vtb{i}", [128, S], BF16)) for i in range(4)]
            qkb = [ps.enter_context(nc.sbuf_tensor(f"qkb{i}", [128, 8192], BF16)) for i in range(4)]
            prb = [ps.enter_context(nc.sbuf_tensor(f"prb{i}", [128, 32 * 512], BF16)) for i in range(2)]
            rsb = [ps.enter_context(nc.sbuf_tensor(f"rsb{i}", [128, 512], F32)) for i in range(2)]
            rrow = [ps.enter_context(nc.sbuf_tensor(f"rrow{i}", [1, 512], F32)) for i in range(2)]
            nmsc = ps.enter_context(nc.sbuf_tensor("nmsc", [1, 8], F32))
            avst = [ps.enter_context(nc.sbuf_tensor(f"avst{i}", [128, 512], BF16)) for i in range(2)]
            pS4 = ps.enter_context(nc.psum_tensor("pS4", [128, 2048], F32))
            pLb = [ps.enter_context(nc.psum_tensor(f"pLb{i}", [128, 512], F32)) for i in range(2)]
            pAV = [ps.enter_context(nc.psum_tensor(f"pAV{i}", [128, 512], F32)) for i in range(2)]

            def psl(s, t_):
                b = s * 1024 + (t_ % 2) * 512
                return pS4[:, b:b + 512]

            with nc.Block() as blk:
                @blk.vector
                def _(vector):
                    for kvh in range(NKV):
                        assert inc1(vector.reduce_max(
                            nmsc[:, 0:1], s_qn2[:, 4 * kvh:4 * kvh + 4], axis=AX),
                            "v") == v_nm0 + 4 * kvh + 1
                        vector.wait_ge(SEM["a"], a_nm0 + 3 * kvh + 2)
                        assert inc1(vector.tensor_tensor(
                            nmsc[:, 2:3], nmsc[:, 3:4], nmsc[:, 4:5], OP.add),
                            "v") == v_nm0 + 4 * kvh + 2
                        vector.wait_ge(SEM["a"], a_nm0 + 3 * kvh + 3)
                        vector.tensor_scalar(
                            nmsc[:, 7:8], nmsc[:, 6:7], -1.0001 * RSQ, -0.01,
                            OP.mult, OP.add)
                        assert inc1(vector.drain(), "v") == v_nm0 + 4 * kvh + 3
                        vector.wait_ge(SEM["t"], t_nm0 + kvh + 1)
                        vector.tensor_copy(s_nmall[:, kvh:kvh + 1], pLb[0][:, 0:1])
                        assert inc1(vector.drain(), "v") == v_nm0 + 4 * kvh + 4
                    for p in range(4):
                        for seg in MERGED:
                            s = seg[0]; kvh = 2 * p + s; kind = seg[1]
                            if kind == "INIT":
                                j = seg[2]
                                if j >= 1:
                                    vector.wait_ge(SEM["t"], t_op(kvh, "L", j - 1, e_of(j - 1) - 1))
                                    vector.reciprocal(rrow[s][:], pLb[s][0:1, :])
                                    assert inc1(vector.drain(), "v") == v_of(kvh, "REC", j - 1)
                            elif kind == "CHUNK":
                                j = seg[2]; c0 = seg[3]; e = e_of(j)
                                if c0 == 0 and j >= 1:
                                    vector.wait_ge(SEM["t"], t_op(kvh, "R", j - 1))
                                    vector.wait_ge(SEM["a"], a_rc(kvh, j - 1))
                                    if j >= 2:
                                        vector.wait_ge(SEM["st"], st_avt(kvh, j - 2))
                                    assert inc1(vector.tensor_tensor(
                                        avst[s][:], pAV[s][:], rsb[s][:], OP.mult),
                                        "v") == v_of(kvh, "EV", j - 1)
                            else:
                                jl = NSLOT - 1
                                vector.wait_ge(SEM["t"], t_op(kvh, "L", jl, e_of(jl) - 1))
                                vector.reciprocal(rrow[s][:], pLb[s][0:1, :])
                                assert inc1(vector.drain(), "v") == v_of(kvh, "REC", jl)
                                vector.wait_ge(SEM["t"], t_op(kvh, "R", jl))
                                vector.wait_ge(SEM["a"], a_rc(kvh, jl))
                                vector.wait_ge(SEM["st"], st_avt(kvh, jl - 1))
                                assert inc1(vector.tensor_tensor(
                                    avst[s][:], pAV[s][:], rsb[s][:], OP.mult),
                                    "v") == v_of(kvh, "EV", jl)

                @blk.scalar
                def _(scalar):
                    for kvh in range(NKV):
                        scalar.wait_ge(SEM["v"], v_nm0 + 4 * kvh + 1)
                        assert inc1(scalar.activation(nmsc[:, 3:4], nmsc[:, 0:1],
                                                      AF.Ln), "a") == a_nm0 + 3 * kvh + 1
                        assert inc1(scalar.activation(nmsc[:, 4:5], s_kn2[:, kvh:kvh + 1],
                                                      AF.Ln), "a") == a_nm0 + 3 * kvh + 2
                        scalar.wait_ge(SEM["v"], v_nm0 + 4 * kvh + 2)
                        assert inc1(scalar.activation(nmsc[:, 6:7], nmsc[:, 2:3],
                                                      AF.Exp, scale=0.5),
                                    "a") == a_nm0 + 3 * kvh + 3
                    for p in range(4):
                        for seg in MERGED:
                            s = seg[0]; kvh = 2 * p + s; kind = seg[1]
                            if kind == "INIT":
                                j = seg[2]
                                if j >= 1:
                                    scalar.wait_ge(SEM["t"], t_op(kvh, "R", j - 1))
                                    if j >= 2:
                                        scalar.wait_ge(SEM["v"], v_of(kvh, "EV", j - 2))
                                    assert inc1(scalar.copy(rsb[s][:], pLb[s][:]),
                                                "a") == a_rc(kvh, j - 1)
                            elif kind == "CHUNK":
                                j = seg[2]; c0 = seg[3]; e = e_of(j)
                                scalar.wait_ge(SEM["t"], t_op(kvh, "S", j, c0 + 1))
                                b0 = s * 1024
                                assert inc1(scalar.activation(
                                    prb[s][:, c0 * 512:(c0 + 2) * 512],
                                    pS4[:, b0:b0 + 1024], AF.Exp, scale=RSQ,
                                    bias=s_nmall[:, kvh:kvh + 1]),
                                    "a") == a_exp(kvh, j, c0)
                            else:
                                jl = NSLOT - 1
                                scalar.wait_ge(SEM["t"], t_op(kvh, "R", jl))
                                scalar.wait_ge(SEM["v"], v_of(kvh, "EV", jl - 1))
                                assert inc1(scalar.copy(rsb[s][:], pLb[s][:]),
                                            "a") == a_rc(kvh, jl)

                @blk.tensor
                def _(tensor):
                    for kvh in range(NKV):
                        tensor.wait_ge(SEM["v"], v_nm0 + 4 * kvh + 3)
                        assert inc1(tensor.matmul(pLb[0][:, 0:1], s_ones_f[:],
                                                  nmsc[:, 7:8], start=True, stop=True),
                                    "t") == t_nm0 + kvh + 1

                    def mmS(kvh, s, j, t_):
                        e = e_of(j)
                        diag = t_ >= e - 2
                        if diag:
                            tensor.matmul(
                                psl(s, t_), s_idn[:],
                                s_msk4[:, (t_ - (e - 2)) * 512:
                                       (t_ - (e - 2) + 1) * 512],
                                start=True, stop=False, skip_group_check=True)
                        inst = tensor.matmul(
                            psl(s, t_),
                            krb[kvh % 4][:, t_ * 128:(t_ + 1) * 128],
                            qkb[kvh % 4][:, j * 512:(j + 1) * 512],
                            start=not diag, stop=True, skip_group_check=True)
                        if t_ % 2 == 1:
                            assert inc1(inst, "t") == t_op(kvh, "S", j, t_)

                    tensor.wait_ge(SEM["v"], v_nm0 + 4 * NKV)
                    for p in range(4):
                        for s0 in range(2):
                            kvh = 2 * p + s0
                            tensor.wait_ge(SEM["ld"], ld_kvh(kvh))
                        for seg in MERGED:
                            s = seg[0]; kvh = 2 * p + s; kind = seg[1]
                            if kind == "INIT":
                                j = seg[2]
                                if j == 0:
                                    tensor.wait_ge(SEM["a"], a_m + p * APK if p == 0 or s == 1 else a_m + p * APK)
                                    tensor.wait_ge(SEM["a"], a_m + p * APK)
                                else:
                                    tensor.wait_ge(SEM["a"],
                                                   a_exp(kvh, j - 1, e_of(j - 1) - 2))
                                for t_ in range(2):
                                    mmS(kvh, s, j, t_)
                                if j >= 1:
                                    tensor.wait_ge(SEM["v"], v_of(kvh, "REC", j - 1))
                                    inst = tensor.matmul(pLb[s][:], s_ones_f[:],
                                                         rrow[s][:], start=True,
                                                         stop=True, skip_group_check=True)
                                    assert inc1(inst, "t") == t_op(kvh, "R", j - 1)
                            elif kind == "CHUNK":
                                j = seg[2]; c0 = seg[3]; e = e_of(j)
                                for t2 in range(c0 + 2, min(c0 + 4, e)):
                                    if t2 % 2 == 0:
                                        tensor.wait_ge(SEM["a"], a_exp(kvh, j, t2 - 2))
                                    mmS(kvh, s, j, t2)
                                if c0 + 2 >= e:
                                    tensor.wait_ge(SEM["a"], a_exp(kvh, j, c0))
                                for t_ in (c0, c0 + 1):
                                    if t_ == c0:
                                        if c0 == 0 and j >= 1:
                                            tensor.wait_ge(SEM["a"], a_rc(kvh, j - 1))
                                    inst = tensor.matmul(
                                        pLb[s][0:1, :], s_ones_bf[:],
                                        prb[s][:, t_ * 512:(t_ + 1) * 512],
                                        start=(t_ == 0), stop=(t_ == e - 1),
                                        skip_group_check=True)
                                    if t_ == e - 1:
                                        assert inc1(inst, "t") == t_op(kvh, "L", j, t_)
                                    if t_ == 0 and j >= 1:
                                        tensor.wait_ge(SEM["v"], v_of(kvh, "EV", j - 1))
                                    tensor.matmul(
                                        pAV[s][:],
                                        vtb[kvh % 4][:, t_ * 128:(t_ + 1) * 128],
                                        prb[s][:, t_ * 512:(t_ + 1) * 512],
                                        start=(t_ == 0), stop=(t_ == e - 1),
                                        skip_group_check=True)
                            else:
                                jl = NSLOT - 1
                                tensor.wait_ge(SEM["v"], v_of(kvh, "REC", jl))
                                inst = tensor.matmul(pLb[s][:], s_ones_f[:],
                                                     rrow[s][:], start=True,
                                                     stop=True, skip_group_check=True)
                                assert inc1(inst, "t") == t_op(kvh, "R", jl)

                @blk.sync
                def _(sync):
                    for kvh in range(NKV):
                        if kvh >= 4:
                            sync.wait_ge(SEM["t"], t_pair_end((kvh - 4) // 2))
                        incd(sync.dma_start(out=krb[kvh % 4][:], in_=KrT_h[kvh]), "ld")
                        for t_ in range(32):
                            incd(sync.dma_start_transpose(
                                out=vtb[kvh % 4][:, t_ * 128:(t_ + 1) * 128],
                                in_=VT_h[kvh, :, t_ * 128:(t_ + 1) * 128]), "ld")
                        for h in range(4):
                            qv = qkb[kvh % 4][:].rearrange("p (sl h q) -> p h sl q",
                                                           h=4, sl=16)
                            incd(sync.dma_start(out=qv[:, h], in_=Q_h[4 * kvh + h]), "ld")
                    sync.wait_ge(SEM["ld"], CNT["ld"])

                @blk.gpsimd
                def _(gpsimd):
                    for p in range(4):
                        for seg in MERGED:
                            s = seg[0]; kvh = 2 * p + s; kind = seg[1]
                            if kind == "CHUNK" and seg[3] == 0 and seg[2] >= 1:
                                j = seg[2]
                                gpsimd.wait_ge(SEM["v"], v_of(kvh, "EV", j - 1))
                                av = avst[s][:].rearrange("p (h q) -> p h q", h=4)
                                assert incd(gpsimd.dma_start(
                                    out=AVT_h.rearrange("n p q -> p n q")[
                                        :, 4 * kvh:4 * kvh + 4,
                                        (j - 1) * 128:j * 128],
                                    in_=av), "st") == st_avt(kvh, j - 1)
                            elif kind == "RFIN":
                                jl = NSLOT - 1
                                gpsimd.wait_ge(SEM["v"], v_of(kvh, "EV", jl))
                                av = avst[s][:].rearrange("p (h q) -> p h q", h=4)
                                assert incd(gpsimd.dma_start(
                                    out=AVT_h.rearrange("n p q -> p n q")[
                                        :, 4 * kvh:4 * kvh + 4,
                                        jl * 128:(jl + 1) * 128],
                                    in_=av), "st") == st_avt(kvh, jl)
                    gpsimd.wait_ge(SEM["st"], CNT["st"])

        ld_b = CNT["ld"]; t_b = CNT["t"]; v_b = CNT["v"]; st_b = CNT["st"]
        g_b = CNT["g"]

        def t_s(lin): return t_b + lin + 1 + lin // 8 + 1
        def t_h0(idx): return t_b + idx * 9 + 1
        def v_o(lin): return v_b + lin + 1
        def st_o(lin): return st_b + 16 * (lin + 1)
        def g_wo(idx, half): return g_b + 16 * 16 + 16 * (idx * 2 + half + 1)
        ld_avt = ld_b + 16 * 16
        def ld_gx(lin): return ld_avt + 16 * 2 * (lin + 1)

        with ExitStack() as ps:
            avres = ps.enter_context(nc.sbuf_tensor("avres", [128, 32 * TOWN], BF16))
            wob = [ps.enter_context(nc.sbuf_tensor(f"wob{i}", [128, 16 * 512], BF16))
                   for i in range(3)]
            gch = [ps.enter_context(nc.sbuf_tensor(f"gch{i}", [128, 512], F32)) for i in range(5)]
            xch = [ps.enter_context(nc.sbuf_tensor(f"xch{i}", [128, 512], F32)) for i in range(5)]
            cch = [ps.enter_context(nc.sbuf_tensor(f"cch{i}", [128, 512], F32)) for i in range(2)]
            och = [ps.enter_context(nc.sbuf_tensor(f"och{i}", [128, 512], F32)) for i in range(2)]
            ptt = [ps.enter_context(nc.psum_tensor(f"pttB{i}", [128, 512], F32))
                   for i in range(8)]

            with nc.Block() as blk:
                @blk.sync
                def _(sync):
                    for head in range(16):
                        incd(sync.dma_start(
                            out=avres[:, head * TOWN:(head + 1) * TOWN],
                            in_=AVT_h[head]), "ld")
                    for lin in range(128):
                        idx, tt = divmod(lin, 8)
                        wave, fc = divmod(idx, 8)
                        tg = wave * 8 + tt
                        if lin >= 5:
                            l2 = lin - 5
                            sync.wait_ge(SEM["v"], v_o(l2))
                            i2, tt2 = divmod(l2, 8)
                            w2, fc2 = divmod(i2, 8)
                            tg2 = w2 * 8 + tt2
                            assert incd(sync.dma_start(
                                out=out_own[tg2 * 128:(tg2 + 1) * 128,
                                            fc2 * 512:(fc2 + 1) * 512],
                                in_=och[tt2 % 2][:]), "st") == st_o(l2)
                        incd(sync.dma_start(
                            out=gch[lin % 5][:],
                            in_=G_h[tg * 128:(tg + 1) * 128,
                                    fc * 512:(fc + 1) * 512]), "ld")
                        incd(sync.dma_start(
                            out=xch[lin % 5][:],
                            in_=x_own[tg * 128:(tg + 1) * 128,
                                      fc * 512:(fc + 1) * 512]), "ld")
                    for l2 in range(123, 128):
                        i2, tt2 = divmod(l2, 8)
                        w2, fc2 = divmod(i2, 8)
                        tg2 = w2 * 8 + tt2
                        sync.wait_ge(SEM["v"], v_o(l2))
                        assert incd(sync.dma_start(
                            out=out_own[tg2 * 128:(tg2 + 1) * 128,
                                        fc2 * 512:(fc2 + 1) * 512],
                            in_=och[tt2 % 2][:]), "st") == st_o(l2)
                    sync.wait_ge(SEM["st"], CNT["st"])
                    sync.wait_ge(SEM["ld"], CNT["ld"])

                @blk.gpsimd
                def _(gpsimd):
                    for head in range(16, 32):
                        incd(gpsimd.dma_start(
                            out=avres[:, head * TOWN:(head + 1) * TOWN],
                            in_=AVT_h[head]), "g")
                    for idx in range(16):
                        wave, fc = divmod(idx, 8)
                        for half in range(2):
                            prev = idx * 2 + half - 3
                            if prev >= 0:
                                pidx, ph = divmod(prev, 2)
                                if ph == 1:
                                    gpsimd.wait_ge(SEM["t"], t_s(pidx * 8 + 7))
                                else:
                                    gpsimd.wait_ge(SEM["t"], t_h0(pidx))
                            incd(gpsimd.dma_start(
                                out=wob[(idx * 2 + half) % 3][:]
                                .rearrange("p (k c) -> p k c", k=16),
                                in_=WO[:, fc, half * 16:(half + 1) * 16]), "g")

                @blk.tensor
                def _(tensor):
                    tensor.wait_ge(SEM["ld"], ld_avt)
                    for idx in range(16):
                        wave, fc = divmod(idx, 8)
                        for half in range(2):
                            tensor.wait_ge(SEM["g"], g_wo(idx, half))
                            for tt in range(8):
                                tg = wave * 8 + tt
                                if half == 0 and idx > 0:
                                    tensor.wait_ge(SEM["v"], v_o((idx - 1) * 8 + tt))
                                for k16 in range(16):
                                    k = half * 16 + k16
                                    inst = tensor.matmul(
                                        ptt[tt][:],
                                        avres[:, k * TOWN + tg * 128:
                                              k * TOWN + (tg + 1) * 128],
                                        wob[(idx * 2 + half) % 3][:,
                                            k16 * 512:(k16 + 1) * 512],
                                        start=(half == 0 and k16 == 0),
                                        stop=(half == 1 and k16 == 15),
                                        skip_group_check=True)
                                if half == 1:
                                    assert inc1(inst, "t") == t_s(idx * 8 + tt)
                            if half == 0:
                                assert inc1(inst, "t") == t_h0(idx)

                @blk.vector
                def _(vector):
                    for idx in range(16):
                        for pair in range(4):
                            tt0 = 2 * pair; tt1 = tt0 + 1
                            lin0 = idx * 8 + tt0; lin1 = lin0 + 1
                            vector.wait_ge(SEM["t"], t_s(lin0))
                            vector.wait_ge(SEM["ld"], ld_gx(lin0))
                            if lin0 >= 2:
                                vector.wait_ge(SEM["st"], st_o(lin0 - 2))
                            vector.tensor_tensor(cch[0][:], ptt[tt0][:],
                                                 gch[lin0 % 5][:], OP.mult)
                            vector.wait_ge(SEM["t"], t_s(lin1))
                            vector.wait_ge(SEM["ld"], ld_gx(lin1))
                            if lin1 >= 2:
                                vector.wait_ge(SEM["st"], st_o(lin1 - 2))
                            vector.tensor_tensor(cch[1][:], ptt[tt1][:],
                                                 gch[lin1 % 5][:], OP.mult)
                            assert inc1(vector.tensor_tensor(
                                och[0][:], cch[0][:], xch[lin0 % 5][:], OP.add),
                                "v") == v_o(lin0)
                            assert inc1(vector.tensor_tensor(
                                och[1][:], cch[1][:], xch[lin1 % 5][:], OP.add),
                                "v") == v_o(lin1)

    return nc


_NC_CACHE = {}


def kernel(L0_attn_x, L0_attn_wq, L0_attn_wk, L0_attn_wv, L0_attn_wo,
           L0_ffn_base_w, L0_ffn_base_b, L0_ffn_A, L0_ffn_B, L0_ffn_router_w):
    x = np.asarray(L0_attn_x, np.float32)
    wq = np.asarray(L0_attn_wq, np.float32)
    wk = np.asarray(L0_attn_wk, np.float32)
    wv = np.asarray(L0_attn_wv, np.float32)
    wo = np.asarray(L0_attn_wo, np.float32)
    base_w = np.asarray(L0_ffn_base_w, np.float32)
    base_b = np.asarray(L0_ffn_base_b, np.float32)
    A = np.asarray(L0_ffn_A, np.float32)
    Bx = np.asarray(L0_ffn_B, np.float32)
    router_w = np.asarray(L0_ffn_router_w, np.float32)

    WQ = np.ascontiguousarray(wq.reshape(32, 128, 32, 128).transpose(1, 2, 0, 3))
    WK = np.ascontiguousarray(wk.reshape(32, 128, 8, 128).transpose(1, 2, 0, 3))
    WV = np.ascontiguousarray(wv.reshape(32, 128, 8, 128).transpose(1, 2, 0, 3))
    WO = np.ascontiguousarray(wo.reshape(32, 128, 8, 512).transpose(1, 2, 0, 3))
    BW = np.ascontiguousarray(
        np.ascontiguousarray(base_w.T).reshape(32, 128, 8, 512).transpose(1, 2, 0, 3))
    Bst = np.ascontiguousarray(Bx.transpose(0, 2, 1).reshape(1024, 4096))
    BST = np.ascontiguousarray(Bst.reshape(8, 128, 8, 512).transpose(1, 2, 0, 3))
    ARTf = np.concatenate([A.T, router_w.T], axis=1)
    ART = np.ascontiguousarray(ARTf.reshape(32, 128, 80).transpose(1, 0, 2))
    bias_rep = np.ascontiguousarray(np.broadcast_to(base_b[None, :], (128, 4096)))

    freqs = (np.float32(1.0) /
             np.power(np.float32(10000.0),
                      np.arange(0, HD, 2, dtype=np.float32) / np.float32(HD))
             ).astype(np.float32)
    tpos = np.arange(S, dtype=np.float32)
    ang = np.outer(tpos, freqs).astype(np.float32)
    cosr = np.cos(ang).astype(np.float32)
    sinr = np.sin(ang).astype(np.float32)
    CKV = np.ascontiguousarray(np.repeat(cosr.T, 2, axis=0))
    SKV = np.empty((HD, S), np.float32)
    SKV[0::2] = -sinr.T
    SKV[1::2] = sinr.T

    tri_add = np.where(np.arange(128)[:, None] <= np.arange(128)[None, :],
                       np.float32(0.0), np.float32(-1e9))
    MSK_role = []
    for role in range(2):
        m = np.zeros((2, 128, 128), BFD)
        if role == 0:
            m[0] = tri_add.astype(BFD)
            m[1] = BFD(-1e9)
        else:
            m[1] = tri_add.astype(BFD)
        MSK_role.append(m)

    MSK4_role = [np.concatenate([np.tile(m[0], (1, 4)), np.tile(m[1], (1, 4))],
                                axis=1) for m in MSK_role]
    IDN = np.eye(128, dtype=BFD)

    ones_col_bf = np.ones((128, 1), BFD)
    ones_row_f32 = np.ones((1, 128), np.float32)
    epscol = np.full((128, 1), EPS, np.float32)

    if "nc" not in _NC_CACHE:
        _NC_CACHE["nc"] = build_nc()
    nc = _NC_CACHE["nc"]

    in_maps = []
    for core in range(8):
        b = core // 2
        role = core % 2
        xb = x[b]
        x_own = np.ascontiguousarray(xb.reshape(32, 128, D)[role::2].reshape(TOWN, D))
        own_pos = ((np.arange(NSLOT) * 2 + role)[:, None] * 128 +
                   np.arange(128)[None, :]).reshape(-1)
        COWN = np.ascontiguousarray(CKV[:, own_pos])
        SOWN = np.ascontiguousarray(SKV[:, own_pos])
        in_maps.append(dict(
            x_own=x_own, x_kv=np.ascontiguousarray(xb),
            WQ=WQ, WK=WK, WV=WV, WO=WO, BW=BW, BST=BST, ART=ART,
            bias_rep=bias_rep, CKV=CKV, SKV=SKV, COWN=COWN, SOWN=SOWN,
            MSK=MSK_role[role], MSK4=MSK4_role[role], IDN=IDN,
            ones_col_bf=ones_col_bf,
            ones_row_f32=ones_row_f32, epscol=epscol))

    res = run_bass_kernel_spmd(nc, in_maps, core_ids=list(range(8)))

    out = np.empty((4, S, D), np.float32)
    for core in range(8):
        b = core // 2
        role = core % 2
        ob = np.asarray(res.results[core]["out_own"], np.float32)
        out[b].reshape(32, 128, D)[role::2] = ob.reshape(NSLOT, 128, D)
    return out


# version 23 (final)
# speedup vs baseline: 1.1111x; 1.0092x over previous
from contextlib import ExitStack
import numpy as np
import ml_dtypes

import concourse.bass as bass
import concourse.mybir as mybir
from concourse.bass_utils import run_bass_kernel_spmd

F32 = mybir.dt.float32
BF16 = mybir.dt.bfloat16
AF = mybir.ActivationFunctionType
OP = mybir.AluOpType
AX = mybir.AxisListType.X
BFD = ml_dtypes.bfloat16

D = 4096
S = 4096
TOWN = 2048
NH = 32
NKV = 8
HD = 128
NSLOT = 16
RSQ = float(np.float32(1.0) / np.float32(np.sqrt(np.float32(128.0))))
EPS = 1e-5
SWAP_MASK = [x for i in range(16) for x in (2 * i + 1, 2 * i)]


def build_nc():
    nc = bass.Bass("TRN2", target_bir_lowering=False)

    def inp(name, shape, dt=F32):
        return nc.dram_tensor(name, shape, dt, kind="ExternalInput").ap()

    x_own = inp("x_own", [TOWN, D])
    x_kv = inp("x_kv", [S, D])
    WQ = inp("WQ", [128, 32, 32, 128])
    WK = inp("WK", [128, 8, 32, 128])
    WV = inp("WV", [128, 8, 32, 128])
    WO = inp("WO", [128, 8, 32, 512])
    BW = inp("BW", [128, 8, 32, 512])
    BST = inp("BST", [128, 8, 8, 512])
    ART = inp("ART", [128, 32, 80])
    bias_rep = inp("bias_rep", [128, D])
    CKV = inp("CKV", [128, S])
    SKV = inp("SKV", [128, S])
    COWN = inp("COWN", [128, TOWN])
    SOWN = inp("SOWN", [128, TOWN])
    MSK = inp("MSK", [2, 128, 128], BF16)
    MSK4 = inp("MSK4", [128, 1024], BF16)
    IDN = inp("IDN", [128, 128], BF16)
    ones_col_bf = inp("ones_col_bf", [128, 1], BF16)
    ones_row_f32 = inp("ones_row_f32", [1, 128])
    epscol_in = inp("epscol", [128, 1])

    xn_own_h = nc.dram_tensor("xn_own_h", [TOWN, D], BF16).ap()
    xn_kv_h = nc.dram_tensor("xn_kv_h", [S, D], BF16).ap()
    KrT_h = nc.dram_tensor("KrT_h", [NKV, 128, S], BF16).ap()
    VT_h = nc.dram_tensor("VT_h", [NKV, 128, S], BF16).ap()
    Q_h = nc.dram_tensor("Q_h", [NH, 128, TOWN], BF16).ap()
    Wgt_h = nc.dram_tensor("Wgt_h", [TOWN, 1024], BF16).ap()
    G_h = nc.dram_tensor("G_h", [TOWN, D], F32).ap()
    AVT_h = nc.dram_tensor("AVT_h", [NH, 128, TOWN], BF16).ap()

    out_own = nc.dram_tensor("out_own", [TOWN, D], F32, kind="ExternalOutput").ap()

    CNT = dict(ld=0, st=0, g=0, t=0, a=0, v=0)

    es = ExitStack()
    with es:
        SEM = {k: es.enter_context(nc.semaphore(k)) for k in CNT}

        def incd(instr, s):
            CNT[s] += 16
            instr.then_inc(SEM[s], 16)
            return CNT[s]

        def inc1(instr, s):
            CNT[s] += 1
            instr.then_inc(SEM[s], 1)
            return CNT[s]

        s_eps = es.enter_context(nc.sbuf_tensor("s_eps", [128, 1], F32))
        s_ones_bf = es.enter_context(nc.sbuf_tensor("s_ones_bf", [128, 1], BF16))
        s_ones_f = es.enter_context(nc.sbuf_tensor("s_ones_f", [1, 128], F32))
        s_msk = es.enter_context(nc.sbuf_tensor("s_msk", [128, 256], BF16))
        s_msk4 = es.enter_context(nc.sbuf_tensor("s_msk4", [128, 1024], BF16))
        s_idn = es.enter_context(nc.sbuf_tensor("s_idn", [128, 128], BF16))
        s_kn2 = es.enter_context(nc.sbuf_tensor("s_kn2", [1, 8], F32))
        s_qn2 = es.enter_context(nc.sbuf_tensor("s_qn2", [1, 32], F32))
        s_nmall = es.enter_context(nc.sbuf_tensor("s_nmall", [128, 8], F32))

        with nc.Block() as blk:
            @blk.sync
            def _(sync):
                incd(sync.dma_start(out=s_eps[:], in_=epscol_in), "ld")
                incd(sync.dma_start(out=s_ones_bf[:], in_=ones_col_bf), "ld")
                incd(sync.dma_start(out=s_ones_f[:], in_=ones_row_f32), "ld")
                incd(sync.dma_start(
                    out=s_msk[:].rearrange("p (m q) -> p m q", m=2),
                    in_=MSK.rearrange("m p q -> p m q")), "ld")
                incd(sync.dma_start(out=s_msk4[:], in_=MSK4), "ld")
                incd(sync.dma_start(out=s_idn[:], in_=IDN), "ld")
                sync.wait_ge(SEM["ld"], CNT["ld"])

            @blk.vector
            def _(vector):
                vector.memset(s_kn2[:], 0.0)
                inc1(vector.memset(s_qn2[:], 0.0), "v")

        NT0 = 16 + 32

        def p0_src(i):
            if i < 16:
                return (x_own[i * 128:(i + 1) * 128, :],
                        xn_own_h[i * 128:(i + 1) * 128, :])
            j = i - 16
            return (x_kv[j * 128:(j + 1) * 128, :],
                    xn_kv_h[j * 128:(j + 1) * 128, :])

        ld0 = CNT["ld"]; st0 = CNT["st"]; a0 = CNT["a"]; v0 = CNT["v"]

        def a_rstd(i): return a0 + 2 * i + 1
        def a_id(i): return a0 + 2 * i + 2
        def v_var(i): return v0 + 2 * i + 1
        def v_nmr(i): return v0 + 2 * i + 2
        def ld_x(i): return ld0 + 16 * (i + 1)
        def st_xn(i): return st0 + 16 * (i + 1)

        with ExitStack() as ps:
            xb = [ps.enter_context(nc.sbuf_tensor(f"p0xb{i}", [128, D], F32)) for i in range(2)]
            xnb = [ps.enter_context(nc.sbuf_tensor(f"p0xn{i}", [128, D], BF16)) for i in range(2)]
            sstat = ps.enter_context(nc.sbuf_tensor("p0st", [128, 48], F32))
            sbn = ps.enter_context(nc.sbuf_tensor("p0bn", [128, 2], F32))
            _cols = [ps.enter_context(nc.sbuf_tensor(f"p0c{i}", [128, 1], F32))
                     for i in range(4)]
            lnc, rstdc, nmrc, spc0 = [c[:] for c in _cols]

            with nc.Block() as blk:
                @blk.sync
                def _(sync):
                    for i in range(2):
                        src, _dst = p0_src(i)
                        incd(sync.dma_start(out=xb[i % 2][:], in_=src), "ld")
                    for i in range(NT0):
                        _src, dst = p0_src(i)
                        sync.wait_ge(SEM["a"], a_id(i))
                        incd(sync.dma_start(out=dst, in_=xnb[i % 2][:]), "st")
                        if i + 2 < NT0:
                            src2, _ = p0_src(i + 2)
                            incd(sync.dma_start(out=xb[i % 2][:], in_=src2), "ld")
                    sync.wait_ge(SEM["st"], CNT["st"])

                @blk.vector
                def _(vector):
                    for i in range(NT0):
                        vector.wait_ge(SEM["ld"], ld_x(i))
                        for c in range(8):
                            vector.bn_stats(sstat[:, c * 6:(c + 1) * 6],
                                            xb[i % 2][:, c * 512:(c + 1) * 512])
                        vector.tensor_copy(spc0, s_eps[:])
                        vector.bn_aggr(sbn[:], sstat[:])
                        inc1(vector.drain(), "v")
                        vector.wait_ge(SEM["a"], a_rstd(i))
                        vector.tensor_scalar(nmrc, sbn[:, 0:1], rstdc, -1.0,
                                             OP.mult, OP.mult)
                        inc1(vector.drain(), "v")

                @blk.scalar
                def _(scalar):
                    for i in range(NT0):
                        scalar.wait_ge(SEM["ld"], ld_x(i))
                        if i >= 2:
                            scalar.wait_ge(SEM["st"], st_xn(i - 2))
                        scalar.wait_ge(SEM["v"], v_var(i))
                        scalar.activation(lnc, sbn[:, 1:2], AF.Ln, bias=s_eps[:])
                        inc1(scalar.activation(rstdc, lnc, AF.Exp, scale=-0.5), "a")
                        scalar.wait_ge(SEM["v"], v_nmr(i))
                        inc1(scalar.activation(xnb[i % 2][:], xb[i % 2][:], AF.Identity,
                                               scale=rstdc, bias=nmrc), "a")

        with ExitStack() as ps:
            wres = ps.enter_context(nc.sbuf_tensor("p1kw", [128, 8 * 32 * 128], BF16))
            xbuf = [ps.enter_context(nc.sbuf_tensor(f"p1kx{i}", [128, 32 * 512], BF16))
                    for i in range(2)]
            s_c = ps.enter_context(nc.sbuf_tensor("p1kc", [128, S], F32))
            s_s = ps.enter_context(nc.sbuf_tensor("p1ks", [128, S], F32))
            qbf = [ps.enter_context(nc.sbuf_tensor(f"p1kq{i}", [128, 512], BF16))
                   for i in range(4)]
            rot = [ps.enter_context(nc.sbuf_tensor(f"p1kr{i}", [128, 512], BF16))
                   for i in range(4)]
            qsq = [ps.enter_context(nc.sbuf_tensor(f"p1kqs{i}", [128, 512], BF16))
                   for i in range(2)]
            nrow = [ps.enter_context(nc.sbuf_tensor(f"p1kn{i}", [1, 512], F32))
                    for i in range(2)]
            vncol = ps.enter_context(nc.sbuf_tensor("p1kvn", [1, 1], F32))
            tmp1 = ps.enter_context(nc.sbuf_tensor("p1kt1", [128, 512], F32))
            tmp2 = ps.enter_context(nc.sbuf_tensor("p1kt2", [128, 512], F32))
            swp = ps.enter_context(nc.sbuf_tensor("p1ksw", [128, 512], BF16))
            pk = [ps.enter_context(nc.psum_tensor(f"p1kp{i}", [128, 512], F32))
                  for i in range(4)]
            pN = [ps.enter_context(nc.psum_tensor(f"p1kpn{i}", [1, 512], F32))
                  for i in range(2)]

            t_b = CNT["t"]; a_b = CNT["a"]; v_b = CNT["v"]; st_b = CNT["st"]
            g_b = CNT["g"]; ld_b = CNT["ld"]
            NU = 64
            T_CHAIN = {}; T_N = {}; A_EVQ = {}; A_EVN = {}
            V_ROT = {}; V_SQ = {}; V_NM = {}; ST = {}
            tc = t_b
            for u in range(NU):
                tc += 1; T_CHAIN[u] = tc
                if u >= 1:
                    tc += 1; T_N[u - 1] = tc
            tc += 1; T_N[NU - 1] = tc
            ac = a_b
            for u in range(NU):
                ac += 1; A_EVQ[u] = ac
                if u >= 1:
                    ac += 1; A_EVN[u - 1] = ac
            ac += 1; A_EVN[NU - 1] = ac
            vc = v_b
            for u in range(NU):
                vc += 1; V_ROT[u] = vc
                if u >= 1:
                    vc += 1; V_NM[u - 1] = vc
                vc += 1; V_SQ[u] = vc
            vc += 1; V_NM[NU - 1] = vc
            stc = st_b
            for u in range(NU):
                stc += 16; ST[u] = stc
            def ld_grp(g_): return ld_b + 32 + 16 * 32 * (g_ + 1)

            with nc.Block() as blk:
                @blk.sync
                def _(sync):
                    incd(sync.dma_start(out=s_c[:], in_=CKV), "ld")
                    incd(sync.dma_start(out=s_s[:], in_=SKV), "ld")
                    for g_ in range(8):
                        if g_ >= 2:
                            sync.wait_ge(SEM["t"], T_CHAIN[(g_ - 2) * 8 + 7])
                        for k in range(32):
                            incd(sync.dma_start_transpose(
                                out=xbuf[g_ % 2][:, k * 512:(k + 1) * 512],
                                in_=xn_kv_h[g_ * 512:(g_ + 1) * 512,
                                            k * 128:(k + 1) * 128]), "ld")
                    sync.wait_ge(SEM["ld"], CNT["ld"])

                @blk.gpsimd
                def _(gpsimd):
                    incd(gpsimd.dma_start(
                        out=wres[:].rearrange("p (f k c) -> p f k c", f=8, k=32),
                        in_=WK), "g")
                    for u in range(NU):
                        g_, f = divmod(u, 8)
                        gpsimd.wait_ge(SEM["v"], V_ROT[u])
                        assert incd(gpsimd.dma_start(
                            out=KrT_h[f, :, g_ * 512:(g_ + 1) * 512],
                            in_=rot[u % 4][:]), "st") == ST[u]
                    gpsimd.wait_ge(SEM["st"], CNT["st"])
                g_cast = g_b + 16

                @blk.tensor
                def _(tensor):
                    tensor.wait_ge(SEM["g"], g_cast)
                    for u in range(NU):
                        g_, f = divmod(u, 8)
                        if f == 0:
                            tensor.wait_ge(SEM["ld"], ld_grp(g_))
                        if u >= 4:
                            tensor.wait_ge(SEM["a"], A_EVQ[u - 4])
                        for k in range(32):
                            inst = tensor.matmul(
                                pk[u % 4][:],
                                wres[:, (f * 32 + k) * 128:(f * 32 + k + 1) * 128],
                                xbuf[g_ % 2][:, k * 512:(k + 1) * 512],
                                start=(k == 0), stop=(k == 31))
                        assert inc1(inst, "t") == T_CHAIN[u]
                        if u >= 1:
                            up = u - 1
                            tensor.wait_ge(SEM["v"], V_SQ[up])
                            if up >= 2:
                                tensor.wait_ge(SEM["a"], A_EVN[up - 2])
                            assert inc1(tensor.matmul(pN[up % 2][:], s_ones_bf[:],
                                                      qsq[up % 2][:], start=True,
                                                      stop=True), "t") == T_N[up]
                    up = NU - 1
                    tensor.wait_ge(SEM["v"], V_SQ[up])
                    tensor.wait_ge(SEM["a"], A_EVN[up - 2])
                    assert inc1(tensor.matmul(pN[up % 2][:], s_ones_bf[:],
                                              qsq[up % 2][:], start=True,
                                              stop=True), "t") == T_N[up]

                @blk.scalar
                def _(scalar):
                    def evq(u):
                        scalar.wait_ge(SEM["t"], T_CHAIN[u])
                        if u >= 4:
                            scalar.wait_ge(SEM["v"], V_ROT[u - 4])
                        assert inc1(scalar.copy(qbf[u % 4][:], pk[u % 4][:]),
                                    "a") == A_EVQ[u]

                    def evn(u):
                        scalar.wait_ge(SEM["t"], T_N[u])
                        if u >= 2:
                            scalar.wait_ge(SEM["v"], V_NM[u - 2])
                        assert inc1(scalar.copy(nrow[u % 2][:], pN[u % 2][:]),
                                    "a") == A_EVN[u]

                    for u in range(NU):
                        evq(u)
                        if u >= 1:
                            evn(u - 1)
                    evn(NU - 1)

                @blk.vector
                def _(vector):
                    for u in range(NU):
                        g_, f = divmod(u, 8)
                        csl = s_c[:, g_ * 512:(g_ + 1) * 512]
                        ssl = s_s[:, g_ * 512:(g_ + 1) * 512]
                        vector.wait_ge(SEM["a"], A_EVQ[u])
                        vector.stream_shuffle(swp[:], qbf[u % 4][:], SWAP_MASK)
                        vector.tensor_tensor(tmp1[:], qbf[u % 4][:], csl, OP.mult)
                        vector.tensor_tensor(tmp2[:], swp[:], ssl, OP.mult)
                        if u >= 1:
                            vector.wait_ge(SEM["a"], A_EVN[u - 1])
                            vector.reduce_max(vncol[:], nrow[(u - 1) % 2][:], axis=AX)
                        else:
                            vector.tensor_copy(swp[:, 0:1], s_eps[:])
                        if u >= 4:
                            vector.wait_ge(SEM["st"], ST[u - 4])
                        assert inc1(vector.tensor_tensor(rot[u % 4][:], tmp1[:],
                                                         tmp2[:], OP.add),
                                    "v") == V_ROT[u]
                        if u >= 1:
                            _, pf = divmod(u - 1, 8)
                            nt = s_kn2[:, pf:pf + 1]
                            assert inc1(vector.tensor_tensor(nt, nt, vncol[:],
                                                             OP.max), "v") == V_NM[u - 1]
                        else:
                            vector.tensor_copy(swp[:, 0:1], s_eps[:])
                        if u >= 2:
                            vector.wait_ge(SEM["t"], T_N[u - 2])
                        assert inc1(vector.tensor_tensor(qsq[u % 2][:], rot[u % 4][:],
                                                         rot[u % 4][:], OP.mult),
                                    "v") == V_SQ[u]
                    up = NU - 1
                    _, pf = divmod(up, 8)
                    vector.wait_ge(SEM["a"], A_EVN[up])
                    vector.reduce_max(vncol[:], nrow[up % 2][:], axis=AX)
                    vector.tensor_copy(swp[:, 0:1], s_eps[:])
                    nt = s_kn2[:, pf:pf + 1]
                    assert inc1(vector.tensor_tensor(nt, nt, vncol[:],
                                                     OP.max), "v") == V_NM[up]

        with ExitStack() as ps:
            wres = ps.enter_context(nc.sbuf_tensor("p1vw", [128, 8 * 32 * 128], BF16))
            xbuf = [ps.enter_context(nc.sbuf_tensor(f"p1vx{i}", [128, 32 * 512], BF16))
                    for i in range(2)]
            vbf = [ps.enter_context(nc.sbuf_tensor(f"p1vq{i}", [128, 512], BF16))
                   for i in range(4)]
            pk = [ps.enter_context(nc.psum_tensor(f"p1vp{i}", [128, 512], F32))
                  for i in range(4)]

            t_b = CNT["t"]; a_b = CNT["a"]; st_b = CNT["st"]
            g_b = CNT["g"]; ld_b = CNT["ld"]
            NU = 64
            def tchain(u): return t_b + u + 1
            def aev(u): return a_b + u + 1
            def stv(u): return st_b + 16 * (u + 1)
            def ld_grp(g_): return ld_b + 16 * 32 * (g_ + 1)

            with nc.Block() as blk:
                @blk.sync
                def _(sync):
                    for g_ in range(8):
                        if g_ >= 2:
                            sync.wait_ge(SEM["t"], tchain((g_ - 2) * 8 + 7))
                        for k in range(32):
                            incd(sync.dma_start_transpose(
                                out=xbuf[g_ % 2][:, k * 512:(k + 1) * 512],
                                in_=xn_kv_h[g_ * 512:(g_ + 1) * 512,
                                            k * 128:(k + 1) * 128]), "ld")
                    sync.wait_ge(SEM["ld"], CNT["ld"])

                @blk.gpsimd
                def _(gpsimd):
                    incd(gpsimd.dma_start(
                        out=wres[:].rearrange("p (f k c) -> p f k c", f=8, k=32),
                        in_=WV), "g")
                    for u in range(NU):
                        g_, f = divmod(u, 8)
                        gpsimd.wait_ge(SEM["a"], aev(u))
                        assert incd(gpsimd.dma_start(
                            out=VT_h[f, :, g_ * 512:(g_ + 1) * 512],
                            in_=vbf[u % 4][:]), "st") == stv(u)
                    gpsimd.wait_ge(SEM["st"], CNT["st"])
                g_cast = g_b + 16

                @blk.tensor
                def _(tensor):
                    tensor.wait_ge(SEM["g"], g_cast)
                    for u in range(NU):
                        g_, f = divmod(u, 8)
                        if f == 0:
                            tensor.wait_ge(SEM["ld"], ld_grp(g_))
                        if u >= 4:
                            tensor.wait_ge(SEM["a"], aev(u - 4))
                        for k in range(32):
                            inst = tensor.matmul(
                                pk[u % 4][:],
                                wres[:, (f * 32 + k) * 128:(f * 32 + k + 1) * 128],
                                xbuf[g_ % 2][:, k * 512:(k + 1) * 512],
                                start=(k == 0), stop=(k == 31))
                        assert inc1(inst, "t") == tchain(u)

                @blk.scalar
                def _(scalar):
                    for u in range(NU):
                        scalar.wait_ge(SEM["t"], tchain(u))
                        if u >= 4:
                            scalar.wait_ge(SEM["st"], stv(u - 4))
                        assert inc1(scalar.copy(vbf[u % 4][:], pk[u % 4][:]),
                                    "a") == aev(u)

        with ExitStack() as span:
            with ExitStack() as ps:
                xnT = ps.enter_context(nc.sbuf_tensor("xnT", [128, 32 * TOWN], BF16))
                wqb = [ps.enter_context(nc.sbuf_tensor(f"wqb{i}", [128, 4096], BF16))
                       for i in range(2)]
                s_co = ps.enter_context(nc.sbuf_tensor("s_co", [128, TOWN], F32))
                s_so = ps.enter_context(nc.sbuf_tensor("s_so", [128, TOWN], F32))
                qbf = [ps.enter_context(nc.sbuf_tensor(f"qbf{i}", [128, 512], BF16))
                       for i in range(4)]
                rot = [ps.enter_context(nc.sbuf_tensor(f"qrot{i}", [128, 512], BF16))
                       for i in range(4)]
                qsq = [ps.enter_context(nc.sbuf_tensor(f"qsq{i}", [128, 512], BF16))
                       for i in range(2)]
                nrow = [ps.enter_context(nc.sbuf_tensor(f"qnr{i}", [1, 512], F32))
                        for i in range(2)]
                vncol = ps.enter_context(nc.sbuf_tensor("qvn", [1, 1], F32))
                tmp1 = ps.enter_context(nc.sbuf_tensor("qt1", [128, 512], F32))
                tmp2 = ps.enter_context(nc.sbuf_tensor("qt2", [128, 512], F32))
                swp = ps.enter_context(nc.sbuf_tensor("qsw", [128, 512], BF16))
                pk = [ps.enter_context(nc.psum_tensor(f"p2ap{i}", [128, 512], F32))
                      for i in range(4)]
                pN = [ps.enter_context(nc.psum_tensor(f"p2apn{i}", [1, 512], F32))
                      for i in range(2)]

                t_b = CNT["t"]; a_b = CNT["a"]; v_b = CNT["v"]; st_b = CNT["st"]
                g_b = CNT["g"]; ld_b = CNT["ld"]
                NU = 128
                T_CHAIN = {}; T_N = {}; A_EVQ = {}; A_EVN = {}
                V_ROT = {}; V_SQ = {}; V_NM = {}; ST = {}
                tc = t_b
                for u in range(NU):
                    tc += 1; T_CHAIN[u] = tc
                    if u >= 1:
                        tc += 1; T_N[u - 1] = tc
                tc += 1; T_N[NU - 1] = tc
                ac = a_b
                for u in range(NU):
                    ac += 1; A_EVQ[u] = ac
                    if u >= 1:
                        ac += 1; A_EVN[u - 1] = ac
                ac += 1; A_EVN[NU - 1] = ac
                vc = v_b
                for u in range(NU):
                    vc += 1; V_ROT[u] = vc
                    if u >= 1:
                        vc += 1; V_NM[u - 1] = vc
                    vc += 1; V_SQ[u] = vc
                vc += 1; V_NM[NU - 1] = vc
                stc = st_b
                for u in range(NU):
                    stc += 16; ST[u] = stc
                ld_xnT = ld_b + 16 * 2 + 16 * 128
                def g_wq(f): return g_b + 16 * (f + 1)

                with nc.Block() as blk:
                    @blk.sync
                    def _(sync):
                        incd(sync.dma_start(out=s_co[:], in_=COWN), "ld")
                        incd(sync.dma_start(out=s_so[:], in_=SOWN), "ld")
                        for k in range(32):
                            for gg in range(4):
                                incd(sync.dma_start_transpose(
                                    out=xnT[:, k * TOWN + gg * 512: k * TOWN + (gg + 1) * 512],
                                    in_=xn_own_h[gg * 512:(gg + 1) * 512,
                                                 k * 128:(k + 1) * 128]), "ld")
                        sync.wait_ge(SEM["ld"], CNT["ld"])

                    @blk.gpsimd
                    def _(gpsimd):
                        for f in range(2):
                            incd(gpsimd.dma_start(
                                out=wqb[f % 2][:].rearrange("p (k c) -> p k c", k=32),
                                in_=WQ[:, f]), "g")
                        for u in range(NU):
                            f, gg = divmod(u, 4)
                            if gg == 0 and f + 2 < 32:
                                gpsimd.wait_ge(SEM["t"], T_CHAIN[f * 4 + 3])
                                incd(gpsimd.dma_start(
                                    out=wqb[f % 2][:].rearrange("p (k c) -> p k c", k=32),
                                    in_=WQ[:, f + 2]), "g")
                            gpsimd.wait_ge(SEM["v"], V_ROT[u])
                            assert incd(gpsimd.dma_start(
                                out=Q_h[f, :, gg * 512:(gg + 1) * 512],
                                in_=rot[u % 4][:]), "st") == ST[u]
                        gpsimd.wait_ge(SEM["st"], CNT["st"])

                    @blk.tensor
                    def _(tensor):
                        tensor.wait_ge(SEM["ld"], ld_xnT)
                        for u in range(NU):
                            f, gg = divmod(u, 4)
                            if gg == 0:
                                tensor.wait_ge(SEM["g"], g_wq(f))
                            if u >= 4:
                                tensor.wait_ge(SEM["a"], A_EVQ[u - 4])
                            for k in range(32):
                                inst = tensor.matmul(
                                    pk[u % 4][:],
                                    wqb[f % 2][:, k * 128:(k + 1) * 128],
                                    xnT[:, k * TOWN + gg * 512: k * TOWN + (gg + 1) * 512],
                                    start=(k == 0), stop=(k == 31))
                            assert inc1(inst, "t") == T_CHAIN[u]
                            if u >= 1:
                                up = u - 1
                                tensor.wait_ge(SEM["v"], V_SQ[up])
                                if up >= 2:
                                    tensor.wait_ge(SEM["a"], A_EVN[up - 2])
                                assert inc1(tensor.matmul(pN[up % 2][:], s_ones_bf[:],
                                                          qsq[up % 2][:], start=True,
                                                          stop=True), "t") == T_N[up]
                        up = NU - 1
                        tensor.wait_ge(SEM["v"], V_SQ[up])
                        tensor.wait_ge(SEM["a"], A_EVN[up - 2])
                        assert inc1(tensor.matmul(pN[up % 2][:], s_ones_bf[:],
                                                  qsq[up % 2][:], start=True,
                                                  stop=True), "t") == T_N[up]

                    @blk.scalar
                    def _(scalar):
                        def evq(u):
                            scalar.wait_ge(SEM["t"], T_CHAIN[u])
                            if u >= 4:
                                scalar.wait_ge(SEM["v"], V_ROT[u - 4])
                            assert inc1(scalar.copy(qbf[u % 4][:], pk[u % 4][:]),
                                        "a") == A_EVQ[u]

                        def evn(u):
                            scalar.wait_ge(SEM["t"], T_N[u])
                            if u >= 2:
                                scalar.wait_ge(SEM["v"], V_NM[u - 2])
                            assert inc1(scalar.copy(nrow[u % 2][:], pN[u % 2][:]),
                                        "a") == A_EVN[u]

                        for u in range(NU):
                            evq(u)
                            if u >= 1:
                                evn(u - 1)
                        evn(NU - 1)

                    @blk.vector
                    def _(vector):
                        for u in range(NU):
                            f, gg = divmod(u, 4)
                            csl = s_co[:, gg * 512:(gg + 1) * 512]
                            ssl = s_so[:, gg * 512:(gg + 1) * 512]
                            vector.wait_ge(SEM["a"], A_EVQ[u])
                            vector.stream_shuffle(swp[:], qbf[u % 4][:], SWAP_MASK)
                            vector.tensor_tensor(tmp1[:], qbf[u % 4][:], csl, OP.mult)
                            vector.tensor_tensor(tmp2[:], swp[:], ssl, OP.mult)
                            if u >= 1:
                                vector.wait_ge(SEM["a"], A_EVN[u - 1])
                                vector.reduce_max(vncol[:], nrow[(u - 1) % 2][:], axis=AX)
                            else:
                                vector.tensor_copy(swp[:, 0:1], s_eps[:])
                            if u >= 4:
                                vector.wait_ge(SEM["st"], ST[u - 4])
                            assert inc1(vector.tensor_tensor(rot[u % 4][:], tmp1[:],
                                                             tmp2[:], OP.add),
                                        "v") == V_ROT[u]
                            if u >= 1:
                                pf, _pgg = divmod(u - 1, 4)
                                nt = s_qn2[:, pf:pf + 1]
                                assert inc1(vector.tensor_tensor(nt, nt, vncol[:],
                                                                 OP.max), "v") == V_NM[u - 1]
                            else:
                                vector.tensor_copy(swp[:, 0:1], s_eps[:])
                            if u >= 2:
                                vector.wait_ge(SEM["t"], T_N[u - 2])
                            assert inc1(vector.tensor_tensor(qsq[u % 2][:], rot[u % 4][:],
                                                             rot[u % 4][:], OP.mult),
                                        "v") == V_SQ[u]
                        up = NU - 1
                        pf, _pgg = divmod(up, 4)
                        vector.wait_ge(SEM["a"], A_EVN[up])
                        vector.reduce_max(vncol[:], nrow[up % 2][:], axis=AX)
                        vector.tensor_copy(swp[:, 0:1], s_eps[:])
                        nt = s_qn2[:, pf:pf + 1]
                        assert inc1(vector.tensor_tensor(nt, nt, vncol[:],
                                                         OP.max), "v") == V_NM[up]

                t_b = CNT["t"]; a_b = CNT["a"]; v_b = CNT["v"]; st_b = CNT["st"]
                g_b = CNT["g"]
                def t_rl(tt): return t_b + tt + 1
                def a_ex(tt): return a_b + tt + 1
                def v_pre(tt): return v_b + tt * 3 + 1
                def v_cmb(tt): return v_b + tt * 3 + 2
                def v_wg(tt): return v_b + tt * 3 + 3
                def st_wg(tt): return st_b + 16 * (tt + 1)
                g_art = g_b + 16

                with ExitStack() as ps2:
                    artb = ps2.enter_context(nc.sbuf_tensor("artb", [128, 32 * 80], BF16))
                    rh_all = ps2.enter_context(nc.sbuf_tensor("rh_all", [128, 1024], F32))
                    unorm = [ps2.enter_context(nc.sbuf_tensor(f"unorm{i}", [128, 16], F32))
                             for i in range(2)]
                    selb = ps2.enter_context(nc.sbuf_tensor("selb", [128, 16], F32))
                    comb = ps2.enter_context(nc.sbuf_tensor("comb", [128, 16], F32))
                    m8 = ps2.enter_context(nc.sbuf_tensor("m8", [128, 8], F32))
                    _rc = [ps2.enter_context(nc.sbuf_tensor(f"p2brc{i}", [128, 1], F32))
                           for i in range(8)]
                    mxc, nmxc, Sc, dnc, rfc, s1c, spcc, _x = [c[:] for c in _rc]
                    wgt = [ps2.enter_context(nc.sbuf_tensor(f"wgt{i}", [128, 1024], BF16))
                           for i in range(2)]
                    prl = [ps2.enter_context(nc.psum_tensor(f"prl{i}", [128, 80], F32))
                           for i in range(2)]

                    with nc.Block() as blk:
                        @blk.gpsimd
                        def _(gpsimd):
                            incd(gpsimd.dma_start(
                                out=artb[:].rearrange("p (k c) -> p k c", k=32),
                                in_=ART), "g")

                        @blk.tensor
                        def _(tensor):
                            tensor.wait_ge(SEM["g"], g_art)
                            for tt in range(16):
                                if tt >= 2:
                                    tensor.wait_ge(SEM["v"], v_cmb(tt - 2))
                                for k in range(32):
                                    inst = tensor.matmul(
                                        prl[tt % 2][:],
                                        xnT[:, k * TOWN + tt * 128: k * TOWN + (tt + 1) * 128],
                                        artb[:, k * 80:(k + 1) * 80],
                                        start=(k == 0), stop=(k == 31))
                                assert inc1(inst, "t") == t_rl(tt)

                        @blk.vector
                        def _(vector):
                            for tt in range(16):
                                vector.wait_ge(SEM["t"], t_rl(tt))
                                vector.reduce_max(mxc, prl[tt % 2][:, 64:80], axis=AX)
                                vector.tensor_copy(rh_all[:, tt * 64:(tt + 1) * 64],
                                                   prl[tt % 2][:, 0:64])
                                vector.tensor_scalar(nmxc, mxc, -1.0, None, OP.mult)
                                assert inc1(vector.drain(), "v") == v_pre(tt)
                                vector.wait_ge(SEM["a"], a_ex(tt))
                                vector.max(m8[:], unorm[tt % 2][:])
                                vector.tensor_scalar(dnc, Sc, 1e-6, None, OP.mult)
                                vector.tensor_tensor(s1c, m8[:, 0:1], m8[:, 1:2], OP.add)
                                vector.tensor_scalar(selb[:], unorm[tt % 2][:],
                                                     m8[:, 1:2], None, OP.is_ge)
                                vector.tensor_tensor(dnc, dnc, s1c, OP.add)
                                vector.tensor_copy(spcc, s_eps[:])
                                vector.reciprocal(rfc, dnc)
                                vector.tensor_copy(spcc, s_eps[:])
                                assert inc1(vector.scalar_tensor_tensor(
                                    comb[:], unorm[tt % 2][:], rfc, selb[:],
                                    OP.mult, OP.mult), "v") == v_cmb(tt)
                                if tt >= 2:
                                    vector.wait_ge(SEM["st"], st_wg(tt - 2))
                                vector.tensor_copy(spcc, s_eps[:])
                                for e in range(16):
                                    inst = vector.tensor_scalar(
                                        wgt[tt % 2][:, e * 64:(e + 1) * 64],
                                        rh_all[:, tt * 64:(tt + 1) * 64],
                                        comb[:, e:e + 1], 0.5, OP.mult, OP.mult)
                                assert inc1(inst, "v") == v_wg(tt)

                        @blk.scalar
                        def _(scalar):
                            for tt in range(16):
                                scalar.wait_ge(SEM["v"], v_pre(tt))
                                assert inc1(scalar.activation(
                                    unorm[tt % 2][:], prl[tt % 2][:, 64:80], AF.Exp,
                                    bias=nmxc, accum_out=Sc), "a") == a_ex(tt)

                        @blk.sync
                        def _(sync):
                            for tt in range(16):
                                sync.wait_ge(SEM["v"], v_wg(tt))
                                assert incd(sync.dma_start(
                                    out=Wgt_h[tt * 128:(tt + 1) * 128, :],
                                    in_=wgt[tt % 2][:]), "st") == st_wg(tt)
                            sync.wait_ge(SEM["st"], CNT["st"])

            t_b = CNT["t"]; a_b = CNT["a"]; v_b = CNT["v"]; st_b = CNT["st"]
            g_b = CNT["g"]; ld_b = CNT["ld"]

            def t_g(lin): return t_b + lin + 1 + (2 if lin >= 56 else 0)
            def t_xf(half): return t_b + 57 + half
            def a_silu(lin): return a_b + lin + 1
            def v_gadd(lin): return v_b + lin + 1
            def st_g(lin): return st_b + 16 * (lin + 1)
            ld_w0 = ld_b + 16 * 72
            def ld_w1(half): return ld_b + 16 * (72 + 32 * (half + 1))
            g_bias = g_b + 16
            def g_bw(idx, half): return g_bias + 16 * (idx * 3 + half + 1)
            def g_bst(idx): return g_bias + 16 * (idx * 3 + 3)

            with ExitStack() as ps:
                wgtT = ps.enter_context(nc.sbuf_tensor("wgtT", [128, 8 * TOWN], BF16))
                xnw = ps.enter_context(nc.sbuf_tensor("xnw", [128, 32 * 1024], BF16))
                bwb = [ps.enter_context(nc.sbuf_tensor(f"bwb{i}", [128, 16 * 512], BF16))
                       for i in range(4)]
                bstb = [ps.enter_context(nc.sbuf_tensor(f"bstb{i}", [128, 8 * 512], BF16))
                        for i in range(2)]
                biasB = ps.enter_context(nc.sbuf_tensor("biasB", [128, D], BF16))
                gtmp = [ps.enter_context(nc.sbuf_tensor(f"gtmp{i}", [128, 512], F32))
                        for i in range(4)]
                gout = [ps.enter_context(nc.sbuf_tensor(f"gout{i}", [128, 512], F32))
                        for i in range(4)]

                with nc.Block() as blk:
                    @blk.sync
                    def _(sync):
                        for ek in range(8):
                            incd(sync.dma_start_transpose(
                                out=wgtT[:, ek * TOWN:(ek + 1) * TOWN],
                                in_=Wgt_h[:, ek * 128:(ek + 1) * 128]), "ld")

                    @blk.gpsimd
                    def _(gpsimd):
                        assert incd(gpsimd.dma_start(out=biasB[:], in_=bias_rep),
                                    "g") == g_bias

                with ExitStack() as psm:
                    ptt = [psm.enter_context(nc.psum_tensor(f"pttA{i}", [128, 512], F32))
                           for i in range(8)]
                    with nc.Block() as blk:
                        @blk.sync
                        def _(sync):
                            def xnw_load(wave, k, hh):
                                tok0 = wave * 1024 + hh * 512
                                incd(sync.dma_start_transpose(
                                    out=xnw[:, k * 1024 + hh * 512:
                                            k * 1024 + (hh + 1) * 512],
                                    in_=xn_own_h[tok0:tok0 + 512,
                                                 k * 128:(k + 1) * 128]), "ld")

                            def g_store(lin):
                                idx, tt = divmod(lin, 8)
                                wave, fc = divmod(idx, 8)
                                tg = wave * 8 + tt
                                sync.wait_ge(SEM["a"], a_silu(lin))
                                assert incd(sync.dma_start(
                                    out=G_h[tg * 128:(tg + 1) * 128,
                                            fc * 512:(fc + 1) * 512],
                                    in_=gout[lin % 4][:]), "st") == st_g(lin)

                            for k in range(32):
                                for hh in range(2):
                                    xnw_load(0, k, hh)
                            for lin in range(56):
                                g_store(lin)
                            for half in range(2):
                                sync.wait_ge(SEM["t"], t_xf(half))
                                for k16 in range(16):
                                    for hh in range(2):
                                        xnw_load(1, half * 16 + k16, hh)
                            for lin in range(56, 128):
                                g_store(lin)
                            sync.wait_ge(SEM["st"], CNT["st"])

                        @blk.gpsimd
                        def _(gpsimd):
                            for idx in range(16):
                                wave, fc = divmod(idx, 8)
                                if idx >= 2:
                                    gpsimd.wait_ge(SEM["t"], t_g((idx - 2) * 8 + 7))
                                for half in range(2):
                                    incd(gpsimd.dma_start(
                                        out=bwb[(idx % 2) * 2 + half][:]
                                        .rearrange("p (k c) -> p k c", k=16),
                                        in_=BW[:, fc, half * 16:(half + 1) * 16]), "g")
                                incd(gpsimd.dma_start(
                                    out=bstb[idx % 2][:].rearrange("p (k c) -> p k c", k=8),
                                    in_=BST[:, fc]), "g")

                        @blk.tensor
                        def _(tensor):
                            for idx in range(16):
                                wave, fc = divmod(idx, 8)
                                if idx == 0:
                                    tensor.wait_ge(SEM["ld"], ld_w0)
                                for half in range(2):
                                    if idx == 8:
                                        tensor.wait_ge(SEM["ld"], ld_w1(half))
                                    tensor.wait_ge(SEM["g"], g_bw(idx, half))
                                    for tt in range(8):
                                        if half == 0 and idx > 0:
                                            tensor.wait_ge(
                                                SEM["v"], v_gadd((idx - 1) * 8 + tt))
                                        for k16 in range(16):
                                            k = half * 16 + k16
                                            inst = tensor.matmul(
                                                ptt[tt][:],
                                                xnw[:, k * 1024 + tt * 128:
                                                    k * 1024 + (tt + 1) * 128],
                                                bwb[(idx % 2) * 2 + half][:,
                                                    k16 * 512:(k16 + 1) * 512],
                                                start=(half == 0 and k16 == 0),
                                                stop=False,
                                                skip_group_check=True)
                                    if idx == 7:
                                        assert inc1(inst, "t") == t_xf(half)
                                tensor.wait_ge(SEM["g"], g_bst(idx))
                                for tt in range(8):
                                    tg = wave * 8 + tt
                                    for ek in range(8):
                                        inst = tensor.matmul(
                                            ptt[tt][:],
                                            wgtT[:, ek * TOWN + tg * 128:
                                                 ek * TOWN + (tg + 1) * 128],
                                            bstb[idx % 2][:, ek * 512:(ek + 1) * 512],
                                            start=False, stop=(ek == 7),
                                            skip_group_check=True)
                                    assert inc1(inst, "t") == t_g(idx * 8 + tt)

                        @blk.vector
                        def _(vector):
                            vector.wait_ge(SEM["g"], g_bias)
                            for idx in range(16):
                                wave, fc = divmod(idx, 8)
                                for tt in range(8):
                                    lin = idx * 8 + tt
                                    vector.wait_ge(SEM["t"], t_g(lin))
                                    if lin >= 4:
                                        vector.wait_ge(SEM["a"], a_silu(lin - 4))
                                    assert inc1(vector.tensor_tensor(
                                        gtmp[lin % 4][:], ptt[tt][:],
                                        biasB[:, fc * 512:(fc + 1) * 512],
                                        OP.add), "v") == v_gadd(lin)

                        @blk.scalar
                        def _(scalar):
                            for idx in range(16):
                                for tt in range(8):
                                    lin = idx * 8 + tt
                                    scalar.wait_ge(SEM["v"], v_gadd(lin))
                                    if lin >= 4:
                                        scalar.wait_ge(SEM["st"], st_g(lin - 4))
                                    assert inc1(scalar.activation(
                                        gout[lin % 4][:], gtmp[lin % 4][:],
                                        AF.Silu), "a") == a_silu(lin)

        ld_b = CNT["ld"]; t_b = CNT["t"]; a_b = CNT["a"]; v_b = CNT["v"]
        st_b = CNT["st"]; g_b3 = CNT["g"]

        def e_of(j): return 2 * j + 2

        def stream_segs():
            segs = []
            for j in range(NSLOT):
                segs.append(("INIT", j))
                for c0 in range(0, e_of(j), 2):
                    segs.append(("CHUNK", j, c0))
            segs.append(("RFIN",))
            return segs

        SEGS1 = stream_segs()
        MERGED = []
        for sa, sb in zip(SEGS1, SEGS1):
            MERGED.append((0,) + sa)
            MERGED.append((1,) + sb)

        TPOSr = {}; APOSr = {}; VPOSr = {}; STr = {}
        tc = ac = vc = stc = 0
        for seg in MERGED:
            s = seg[0]; kind = seg[1]
            if kind == "INIT":
                j = seg[2]
                tc += 1; TPOSr[(s, "S", j, 1)] = tc
                if j >= 1:
                    tc += 1; TPOSr[(s, "R", j - 1)] = tc
                    ac += 1; APOSr[(s, "RC", j - 1)] = ac
                    vc += 1; VPOSr[(s, "REC", j - 1)] = vc
            elif kind == "CHUNK":
                j = seg[2]; c0 = seg[3]; e = e_of(j)
                if c0 + 3 < e:
                    tc += 1; TPOSr[(s, "S", j, c0 + 3)] = tc
                ac += 1; APOSr[(s, "E", j, c0)] = ac
                if c0 + 1 == e - 1:
                    tc += 1; TPOSr[(s, "L", j, e - 1)] = tc
                if c0 == 0 and j >= 1:
                    vc += 1; VPOSr[(s, "EV", j - 1)] = vc
                    stc += 16; STr[(s, j - 1)] = stc
            else:
                tc += 1; TPOSr[(s, "R", NSLOT - 1)] = tc
                ac += 1; APOSr[(s, "RC", NSLOT - 1)] = ac
                vc += 1; VPOSr[(s, "REC", NSLOT - 1)] = vc
                vc += 1; VPOSr[(s, "EV", NSLOT - 1)] = vc
                stc += 16; STr[(s, NSLOT - 1)] = stc
        TPK, APK, VPK, STK = tc, ac, vc, stc

        t_nm0 = t_b; a_nm0 = a_b; v_nm0 = v_b
        t_m = t_b + NKV
        a_m = a_b + 3 * NKV
        v_m = v_b + 4 * NKV

        def pair_of(kvh): return kvh // 2, kvh % 2
        def t_op(kvh, kind, j, t_=None):
            p, s = pair_of(kvh)
            key = (s, kind, j) if kind == "R" else (s, kind, j, t_)
            return t_m + p * TPK + TPOSr[key]
        def a_exp(kvh, j, c0):
            p, s = pair_of(kvh)
            return a_m + p * APK + APOSr[(s, "E", j, c0)]
        def a_rc(kvh, j):
            p, s = pair_of(kvh)
            return a_m + p * APK + APOSr[(s, "RC", j)]
        def v_of(kvh, kind, j):
            p, s = pair_of(kvh)
            return v_m + p * VPK + VPOSr[(s, kind, j)]
        def st_avt(kvh, j):
            p, s = pair_of(kvh)
            return st_b + p * STK + STr[(s, j)]
        def ld_kvh(kvh):
            return ld_b + 16 * (33 * (kvh + 1) if kvh < 2 else 66 + 37 * (kvh - 1))
        def g_qk(kvh): return g_b3 + 16 * 4 * (kvh + 1)
        def t_pair_end(p): return t_m + (p + 1) * TPK

        with ExitStack() as ps:
            krb = [ps.enter_context(nc.sbuf_tensor(f"krb{i}", [128, S], BF16)) for i in range(4)]
            vtb = [ps.enter_context(nc.sbuf_tensor(f"vtb{i}", [128, S], BF16)) for i in range(4)]
            qkb = [ps.enter_context(nc.sbuf_tensor(f"qkb{i}", [128, 8192], BF16)) for i in range(4)]
            prb = [ps.enter_context(nc.sbuf_tensor(f"prb{i}", [128, 32 * 512], BF16)) for i in range(2)]
            rsb = [ps.enter_context(nc.sbuf_tensor(f"rsb{i}", [128, 512], F32)) for i in range(2)]
            rrow = [ps.enter_context(nc.sbuf_tensor(f"rrow{i}", [1, 512], F32)) for i in range(2)]
            nmsc = ps.enter_context(nc.sbuf_tensor("nmsc", [1, 8], F32))
            avst = [ps.enter_context(nc.sbuf_tensor(f"avst{i}", [128, 512], BF16)) for i in range(2)]
            pS4 = ps.enter_context(nc.psum_tensor("pS4", [128, 2048], F32))
            pLb = [ps.enter_context(nc.psum_tensor(f"pLb{i}", [128, 512], F32)) for i in range(2)]
            pAV = [ps.enter_context(nc.psum_tensor(f"pAV{i}", [128, 512], F32)) for i in range(2)]

            def psl(s, t_):
                b = s * 1024 + (t_ % 2) * 512
                return pS4[:, b:b + 512]

            with nc.Block() as blk:
                @blk.vector
                def _(vector):
                    for kvh in range(NKV):
                        assert inc1(vector.reduce_max(
                            nmsc[:, 0:1], s_qn2[:, 4 * kvh:4 * kvh + 4], axis=AX),
                            "v") == v_nm0 + 4 * kvh + 1
                        vector.wait_ge(SEM["a"], a_nm0 + 3 * kvh + 2)
                        assert inc1(vector.tensor_tensor(
                            nmsc[:, 2:3], nmsc[:, 3:4], nmsc[:, 4:5], OP.add),
                            "v") == v_nm0 + 4 * kvh + 2
                        vector.wait_ge(SEM["a"], a_nm0 + 3 * kvh + 3)
                        vector.tensor_scalar(
                            nmsc[:, 7:8], nmsc[:, 6:7], -1.0001 * RSQ, -0.01,
                            OP.mult, OP.add)
                        assert inc1(vector.drain(), "v") == v_nm0 + 4 * kvh + 3
                        vector.wait_ge(SEM["t"], t_nm0 + kvh + 1)
                        vector.tensor_copy(s_nmall[:, kvh:kvh + 1], pLb[0][:, 0:1])
                        assert inc1(vector.drain(), "v") == v_nm0 + 4 * kvh + 4
                    for p in range(4):
                        for seg in MERGED:
                            s = seg[0]; kvh = 2 * p + s; kind = seg[1]
                            if kind == "INIT":
                                j = seg[2]
                                if j >= 1:
                                    vector.wait_ge(SEM["t"], t_op(kvh, "L", j - 1, e_of(j - 1) - 1))
                                    vector.reciprocal(rrow[s][:], pLb[s][0:1, :])
                                    assert inc1(vector.drain(), "v") == v_of(kvh, "REC", j - 1)
                            elif kind == "CHUNK":
                                j = seg[2]; c0 = seg[3]; e = e_of(j)
                                if c0 == 0 and j >= 1:
                                    vector.wait_ge(SEM["t"], t_op(kvh, "R", j - 1))
                                    vector.wait_ge(SEM["a"], a_rc(kvh, j - 1))
                                    if j >= 2:
                                        vector.wait_ge(SEM["st"], st_avt(kvh, j - 2))
                                    assert inc1(vector.tensor_tensor(
                                        avst[s][:], pAV[s][:], rsb[s][:], OP.mult),
                                        "v") == v_of(kvh, "EV", j - 1)
                            else:
                                jl = NSLOT - 1
                                vector.wait_ge(SEM["t"], t_op(kvh, "L", jl, e_of(jl) - 1))
                                vector.reciprocal(rrow[s][:], pLb[s][0:1, :])
                                assert inc1(vector.drain(), "v") == v_of(kvh, "REC", jl)
                                vector.wait_ge(SEM["t"], t_op(kvh, "R", jl))
                                vector.wait_ge(SEM["a"], a_rc(kvh, jl))
                                vector.wait_ge(SEM["st"], st_avt(kvh, jl - 1))
                                assert inc1(vector.tensor_tensor(
                                    avst[s][:], pAV[s][:], rsb[s][:], OP.mult),
                                    "v") == v_of(kvh, "EV", jl)

                @blk.scalar
                def _(scalar):
                    for kvh in range(NKV):
                        scalar.wait_ge(SEM["v"], v_nm0 + 4 * kvh + 1)
                        assert inc1(scalar.activation(nmsc[:, 3:4], nmsc[:, 0:1],
                                                      AF.Ln), "a") == a_nm0 + 3 * kvh + 1
                        assert inc1(scalar.activation(nmsc[:, 4:5], s_kn2[:, kvh:kvh + 1],
                                                      AF.Ln), "a") == a_nm0 + 3 * kvh + 2
                        scalar.wait_ge(SEM["v"], v_nm0 + 4 * kvh + 2)
                        assert inc1(scalar.activation(nmsc[:, 6:7], nmsc[:, 2:3],
                                                      AF.Exp, scale=0.5),
                                    "a") == a_nm0 + 3 * kvh + 3
                    for p in range(4):
                        for seg in MERGED:
                            s = seg[0]; kvh = 2 * p + s; kind = seg[1]
                            if kind == "INIT":
                                j = seg[2]
                                if j >= 1:
                                    scalar.wait_ge(SEM["t"], t_op(kvh, "R", j - 1))
                                    if j >= 2:
                                        scalar.wait_ge(SEM["v"], v_of(kvh, "EV", j - 2))
                                    assert inc1(scalar.copy(rsb[s][:], pLb[s][:]),
                                                "a") == a_rc(kvh, j - 1)
                            elif kind == "CHUNK":
                                j = seg[2]; c0 = seg[3]; e = e_of(j)
                                scalar.wait_ge(SEM["t"], t_op(kvh, "S", j, c0 + 1))
                                b0 = s * 1024
                                assert inc1(scalar.activation(
                                    prb[s][:, c0 * 512:(c0 + 2) * 512],
                                    pS4[:, b0:b0 + 1024], AF.Exp, scale=RSQ,
                                    bias=s_nmall[:, kvh:kvh + 1]),
                                    "a") == a_exp(kvh, j, c0)
                            else:
                                jl = NSLOT - 1
                                scalar.wait_ge(SEM["t"], t_op(kvh, "R", jl))
                                scalar.wait_ge(SEM["v"], v_of(kvh, "EV", jl - 1))
                                assert inc1(scalar.copy(rsb[s][:], pLb[s][:]),
                                            "a") == a_rc(kvh, jl)

                @blk.tensor
                def _(tensor):
                    for kvh in range(NKV):
                        tensor.wait_ge(SEM["v"], v_nm0 + 4 * kvh + 3)
                        assert inc1(tensor.matmul(pLb[0][:, 0:1], s_ones_f[:],
                                                  nmsc[:, 7:8], start=True, stop=True),
                                    "t") == t_nm0 + kvh + 1

                    def mmS(kvh, s, j, t_):
                        e = e_of(j)
                        diag = t_ >= e - 2
                        if diag:
                            tensor.matmul(
                                psl(s, t_), s_idn[:],
                                s_msk4[:, (t_ - (e - 2)) * 512:
                                       (t_ - (e - 2) + 1) * 512],
                                start=True, stop=False, skip_group_check=True)
                        inst = tensor.matmul(
                            psl(s, t_),
                            krb[kvh % 4][:, t_ * 128:(t_ + 1) * 128],
                            qkb[kvh % 4][:, j * 512:(j + 1) * 512],
                            start=not diag, stop=True, skip_group_check=True)
                        if t_ % 2 == 1:
                            assert inc1(inst, "t") == t_op(kvh, "S", j, t_)

                    tensor.wait_ge(SEM["v"], v_nm0 + 4 * NKV)
                    for p in range(4):
                        for s0 in range(2):
                            kvh = 2 * p + s0
                            tensor.wait_ge(SEM["ld"], ld_kvh(kvh))
                            if kvh < 2:
                                tensor.wait_ge(SEM["g"], g_qk(kvh))
                        for seg in MERGED:
                            s = seg[0]; kvh = 2 * p + s; kind = seg[1]
                            if kind == "INIT":
                                j = seg[2]
                                if j == 0:
                                    tensor.wait_ge(SEM["a"], a_m + p * APK if p == 0 or s == 1 else a_m + p * APK)
                                    tensor.wait_ge(SEM["a"], a_m + p * APK)
                                else:
                                    tensor.wait_ge(SEM["a"],
                                                   a_exp(kvh, j - 1, e_of(j - 1) - 2))
                                for t_ in range(2):
                                    mmS(kvh, s, j, t_)
                                if j >= 1:
                                    tensor.wait_ge(SEM["v"], v_of(kvh, "REC", j - 1))
                                    inst = tensor.matmul(pLb[s][:], s_ones_f[:],
                                                         rrow[s][:], start=True,
                                                         stop=True, skip_group_check=True)
                                    assert inc1(inst, "t") == t_op(kvh, "R", j - 1)
                            elif kind == "CHUNK":
                                j = seg[2]; c0 = seg[3]; e = e_of(j)
                                for t2 in range(c0 + 2, min(c0 + 4, e)):
                                    if t2 % 2 == 0:
                                        tensor.wait_ge(SEM["a"], a_exp(kvh, j, t2 - 2))
                                    mmS(kvh, s, j, t2)
                                if c0 + 2 >= e:
                                    tensor.wait_ge(SEM["a"], a_exp(kvh, j, c0))
                                for t_ in (c0, c0 + 1):
                                    if t_ == c0:
                                        if c0 == 0 and j >= 1:
                                            tensor.wait_ge(SEM["a"], a_rc(kvh, j - 1))
                                    inst = tensor.matmul(
                                        pLb[s][0:1, :], s_ones_bf[:],
                                        prb[s][:, t_ * 512:(t_ + 1) * 512],
                                        start=(t_ == 0), stop=(t_ == e - 1),
                                        skip_group_check=True)
                                    if t_ == e - 1:
                                        assert inc1(inst, "t") == t_op(kvh, "L", j, t_)
                                    if t_ == 0 and j >= 1:
                                        tensor.wait_ge(SEM["v"], v_of(kvh, "EV", j - 1))
                                    tensor.matmul(
                                        pAV[s][:],
                                        vtb[kvh % 4][:, t_ * 128:(t_ + 1) * 128],
                                        prb[s][:, t_ * 512:(t_ + 1) * 512],
                                        start=(t_ == 0), stop=(t_ == e - 1),
                                        skip_group_check=True)
                            else:
                                jl = NSLOT - 1
                                tensor.wait_ge(SEM["v"], v_of(kvh, "REC", jl))
                                inst = tensor.matmul(pLb[s][:], s_ones_f[:],
                                                     rrow[s][:], start=True,
                                                     stop=True, skip_group_check=True)
                                assert inc1(inst, "t") == t_op(kvh, "R", jl)

                @blk.sync
                def _(sync):
                    for kvh in range(NKV):
                        if kvh >= 4:
                            sync.wait_ge(SEM["t"], t_pair_end((kvh - 4) // 2))
                        incd(sync.dma_start(out=krb[kvh % 4][:], in_=KrT_h[kvh]), "ld")
                        for t_ in range(32):
                            incd(sync.dma_start_transpose(
                                out=vtb[kvh % 4][:, t_ * 128:(t_ + 1) * 128],
                                in_=VT_h[kvh, :, t_ * 128:(t_ + 1) * 128]), "ld")
                        if kvh >= 2:
                            for h in range(4):
                                qv = qkb[kvh % 4][:].rearrange(
                                    "p (sl h q) -> p h sl q", h=4, sl=16)
                                incd(sync.dma_start(out=qv[:, h],
                                                    in_=Q_h[4 * kvh + h]), "ld")
                    sync.wait_ge(SEM["ld"], CNT["ld"])

                @blk.gpsimd
                def _(gpsimd):
                    for kvh0 in range(2):
                        qv = qkb[kvh0 % 4][:].rearrange(
                            "p (sl h q) -> p h sl q", h=4, sl=16)
                        for h in range(4):
                            incd(gpsimd.dma_start(out=qv[:, h],
                                                  in_=Q_h[4 * kvh0 + h]), "g")
                    for p in range(4):
                        for seg in MERGED:
                            s = seg[0]; kvh = 2 * p + s; kind = seg[1]
                            if kind == "CHUNK" and seg[3] == 0 and seg[2] >= 1:
                                j = seg[2]
                                gpsimd.wait_ge(SEM["v"], v_of(kvh, "EV", j - 1))
                                av = avst[s][:].rearrange("p (h q) -> p h q", h=4)
                                assert incd(gpsimd.dma_start(
                                    out=AVT_h.rearrange("n p q -> p n q")[
                                        :, 4 * kvh:4 * kvh + 4,
                                        (j - 1) * 128:j * 128],
                                    in_=av), "st") == st_avt(kvh, j - 1)
                            elif kind == "RFIN":
                                jl = NSLOT - 1
                                gpsimd.wait_ge(SEM["v"], v_of(kvh, "EV", jl))
                                av = avst[s][:].rearrange("p (h q) -> p h q", h=4)
                                assert incd(gpsimd.dma_start(
                                    out=AVT_h.rearrange("n p q -> p n q")[
                                        :, 4 * kvh:4 * kvh + 4,
                                        jl * 128:(jl + 1) * 128],
                                    in_=av), "st") == st_avt(kvh, jl)
                    gpsimd.wait_ge(SEM["st"], CNT["st"])

        ld_b = CNT["ld"]; t_b = CNT["t"]; v_b = CNT["v"]; st_b = CNT["st"]
        g_b = CNT["g"]

        def t_s(lin): return t_b + lin + 1 + lin // 8 + 1
        def t_h0(idx): return t_b + idx * 9 + 1
        def v_o(lin): return v_b + lin + 1
        def st_o(lin): return st_b + 16 * (lin + 1)
        def g_wo(idx, half): return g_b + 16 * 16 + 16 * (idx * 2 + half + 1)
        ld_avt = ld_b + 16 * 16
        def ld_gx(lin): return ld_avt + 16 * 2 * (lin + 1)

        with ExitStack() as ps:
            avres = ps.enter_context(nc.sbuf_tensor("avres", [128, 32 * TOWN], BF16))
            wob = [ps.enter_context(nc.sbuf_tensor(f"wob{i}", [128, 16 * 512], BF16))
                   for i in range(3)]
            gch = [ps.enter_context(nc.sbuf_tensor(f"gch{i}", [128, 512], F32)) for i in range(5)]
            xch = [ps.enter_context(nc.sbuf_tensor(f"xch{i}", [128, 512], F32)) for i in range(5)]
            cch = [ps.enter_context(nc.sbuf_tensor(f"cch{i}", [128, 512], F32)) for i in range(2)]
            och = [ps.enter_context(nc.sbuf_tensor(f"och{i}", [128, 512], F32)) for i in range(2)]
            ptt = [ps.enter_context(nc.psum_tensor(f"pttB{i}", [128, 512], F32))
                   for i in range(8)]

            with nc.Block() as blk:
                @blk.sync
                def _(sync):
                    for head in range(16):
                        incd(sync.dma_start(
                            out=avres[:, head * TOWN:(head + 1) * TOWN],
                            in_=AVT_h[head]), "ld")
                    for lin in range(128):
                        idx, tt = divmod(lin, 8)
                        wave, fc = divmod(idx, 8)
                        tg = wave * 8 + tt
                        if lin >= 5:
                            l2 = lin - 5
                            sync.wait_ge(SEM["v"], v_o(l2))
                            i2, tt2 = divmod(l2, 8)
                            w2, fc2 = divmod(i2, 8)
                            tg2 = w2 * 8 + tt2
                            assert incd(sync.dma_start(
                                out=out_own[tg2 * 128:(tg2 + 1) * 128,
                                            fc2 * 512:(fc2 + 1) * 512],
                                in_=och[tt2 % 2][:]), "st") == st_o(l2)
                        incd(sync.dma_start(
                            out=gch[lin % 5][:],
                            in_=G_h[tg * 128:(tg + 1) * 128,
                                    fc * 512:(fc + 1) * 512]), "ld")
                        incd(sync.dma_start(
                            out=xch[lin % 5][:],
                            in_=x_own[tg * 128:(tg + 1) * 128,
                                      fc * 512:(fc + 1) * 512]), "ld")
                    for l2 in range(123, 128):
                        i2, tt2 = divmod(l2, 8)
                        w2, fc2 = divmod(i2, 8)
                        tg2 = w2 * 8 + tt2
                        sync.wait_ge(SEM["v"], v_o(l2))
                        assert incd(sync.dma_start(
                            out=out_own[tg2 * 128:(tg2 + 1) * 128,
                                        fc2 * 512:(fc2 + 1) * 512],
                            in_=och[tt2 % 2][:]), "st") == st_o(l2)
                    sync.wait_ge(SEM["st"], CNT["st"])
                    sync.wait_ge(SEM["ld"], CNT["ld"])

                @blk.gpsimd
                def _(gpsimd):
                    for head in range(16, 32):
                        incd(gpsimd.dma_start(
                            out=avres[:, head * TOWN:(head + 1) * TOWN],
                            in_=AVT_h[head]), "g")
                    for idx in range(16):
                        wave, fc = divmod(idx, 8)
                        for half in range(2):
                            prev = idx * 2 + half - 3
                            if prev >= 0:
                                pidx, ph = divmod(prev, 2)
                                if ph == 1:
                                    gpsimd.wait_ge(SEM["t"], t_s(pidx * 8 + 7))
                                else:
                                    gpsimd.wait_ge(SEM["t"], t_h0(pidx))
                            incd(gpsimd.dma_start(
                                out=wob[(idx * 2 + half) % 3][:]
                                .rearrange("p (k c) -> p k c", k=16),
                                in_=WO[:, fc, half * 16:(half + 1) * 16]), "g")

                @blk.tensor
                def _(tensor):
                    tensor.wait_ge(SEM["ld"], ld_avt)
                    for idx in range(16):
                        wave, fc = divmod(idx, 8)
                        for half in range(2):
                            tensor.wait_ge(SEM["g"], g_wo(idx, half))
                            for tt in range(8):
                                tg = wave * 8 + tt
                                if half == 0 and idx > 0:
                                    tensor.wait_ge(SEM["v"], v_o((idx - 1) * 8 + tt))
                                for k16 in range(16):
                                    k = half * 16 + k16
                                    inst = tensor.matmul(
                                        ptt[tt][:],
                                        avres[:, k * TOWN + tg * 128:
                                              k * TOWN + (tg + 1) * 128],
                                        wob[(idx * 2 + half) % 3][:,
                                            k16 * 512:(k16 + 1) * 512],
                                        start=(half == 0 and k16 == 0),
                                        stop=(half == 1 and k16 == 15),
                                        skip_group_check=True)
                                if half == 1:
                                    assert inc1(inst, "t") == t_s(idx * 8 + tt)
                            if half == 0:
                                assert inc1(inst, "t") == t_h0(idx)

                @blk.vector
                def _(vector):
                    for idx in range(16):
                        for pair in range(4):
                            tt0 = 2 * pair; tt1 = tt0 + 1
                            lin0 = idx * 8 + tt0; lin1 = lin0 + 1
                            vector.wait_ge(SEM["t"], t_s(lin0))
                            vector.wait_ge(SEM["ld"], ld_gx(lin0))
                            if lin0 >= 2:
                                vector.wait_ge(SEM["st"], st_o(lin0 - 2))
                            vector.tensor_tensor(cch[0][:], ptt[tt0][:],
                                                 gch[lin0 % 5][:], OP.mult)
                            vector.wait_ge(SEM["t"], t_s(lin1))
                            vector.wait_ge(SEM["ld"], ld_gx(lin1))
                            if lin1 >= 2:
                                vector.wait_ge(SEM["st"], st_o(lin1 - 2))
                            vector.tensor_tensor(cch[1][:], ptt[tt1][:],
                                                 gch[lin1 % 5][:], OP.mult)
                            assert inc1(vector.tensor_tensor(
                                och[0][:], cch[0][:], xch[lin0 % 5][:], OP.add),
                                "v") == v_o(lin0)
                            assert inc1(vector.tensor_tensor(
                                och[1][:], cch[1][:], xch[lin1 % 5][:], OP.add),
                                "v") == v_o(lin1)

    return nc


_NC_CACHE = {}


def kernel(L0_attn_x, L0_attn_wq, L0_attn_wk, L0_attn_wv, L0_attn_wo,
           L0_ffn_base_w, L0_ffn_base_b, L0_ffn_A, L0_ffn_B, L0_ffn_router_w):
    x = np.asarray(L0_attn_x, np.float32)
    wq = np.asarray(L0_attn_wq, np.float32)
    wk = np.asarray(L0_attn_wk, np.float32)
    wv = np.asarray(L0_attn_wv, np.float32)
    wo = np.asarray(L0_attn_wo, np.float32)
    base_w = np.asarray(L0_ffn_base_w, np.float32)
    base_b = np.asarray(L0_ffn_base_b, np.float32)
    A = np.asarray(L0_ffn_A, np.float32)
    Bx = np.asarray(L0_ffn_B, np.float32)
    router_w = np.asarray(L0_ffn_router_w, np.float32)

    WQ = np.ascontiguousarray(wq.reshape(32, 128, 32, 128).transpose(1, 2, 0, 3))
    WK = np.ascontiguousarray(wk.reshape(32, 128, 8, 128).transpose(1, 2, 0, 3))
    WV = np.ascontiguousarray(wv.reshape(32, 128, 8, 128).transpose(1, 2, 0, 3))
    WO = np.ascontiguousarray(wo.reshape(32, 128, 8, 512).transpose(1, 2, 0, 3))
    BW = np.ascontiguousarray(
        np.ascontiguousarray(base_w.T).reshape(32, 128, 8, 512).transpose(1, 2, 0, 3))
    Bst = np.ascontiguousarray(Bx.transpose(0, 2, 1).reshape(1024, 4096))
    BST = np.ascontiguousarray(Bst.reshape(8, 128, 8, 512).transpose(1, 2, 0, 3))
    ARTf = np.concatenate([A.T, router_w.T], axis=1)
    ART = np.ascontiguousarray(ARTf.reshape(32, 128, 80).transpose(1, 0, 2))
    bias_rep = np.ascontiguousarray(np.broadcast_to(base_b[None, :], (128, 4096)))

    freqs = (np.float32(1.0) /
             np.power(np.float32(10000.0),
                      np.arange(0, HD, 2, dtype=np.float32) / np.float32(HD))
             ).astype(np.float32)
    tpos = np.arange(S, dtype=np.float32)
    ang = np.outer(tpos, freqs).astype(np.float32)
    cosr = np.cos(ang).astype(np.float32)
    sinr = np.sin(ang).astype(np.float32)
    CKV = np.ascontiguousarray(np.repeat(cosr.T, 2, axis=0))
    SKV = np.empty((HD, S), np.float32)
    SKV[0::2] = -sinr.T
    SKV[1::2] = sinr.T

    tri_add = np.where(np.arange(128)[:, None] <= np.arange(128)[None, :],
                       np.float32(0.0), np.float32(-1e9))
    MSK_role = []
    for role in range(2):
        m = np.zeros((2, 128, 128), BFD)
        if role == 0:
            m[0] = tri_add.astype(BFD)
            m[1] = BFD(-1e9)
        else:
            m[1] = tri_add.astype(BFD)
        MSK_role.append(m)

    MSK4_role = [np.concatenate([np.tile(m[0], (1, 4)), np.tile(m[1], (1, 4))],
                                axis=1) for m in MSK_role]
    IDN = np.eye(128, dtype=BFD)

    ones_col_bf = np.ones((128, 1), BFD)
    ones_row_f32 = np.ones((1, 128), np.float32)
    epscol = np.full((128, 1), EPS, np.float32)

    if "nc" not in _NC_CACHE:
        _NC_CACHE["nc"] = build_nc()
    nc = _NC_CACHE["nc"]

    in_maps = []
    for core in range(8):
        b = core // 2
        role = core % 2
        xb = x[b]
        x_own = np.ascontiguousarray(xb.reshape(32, 128, D)[role::2].reshape(TOWN, D))
        own_pos = ((np.arange(NSLOT) * 2 + role)[:, None] * 128 +
                   np.arange(128)[None, :]).reshape(-1)
        COWN = np.ascontiguousarray(CKV[:, own_pos])
        SOWN = np.ascontiguousarray(SKV[:, own_pos])
        in_maps.append(dict(
            x_own=x_own, x_kv=np.ascontiguousarray(xb),
            WQ=WQ, WK=WK, WV=WV, WO=WO, BW=BW, BST=BST, ART=ART,
            bias_rep=bias_rep, CKV=CKV, SKV=SKV, COWN=COWN, SOWN=SOWN,
            MSK=MSK_role[role], MSK4=MSK4_role[role], IDN=IDN,
            ones_col_bf=ones_col_bf,
            ones_row_f32=ones_row_f32, epscol=epscol))

    res = run_bass_kernel_spmd(nc, in_maps, core_ids=list(range(8)))

    out = np.empty((4, S, D), np.float32)
    for core in range(8):
        b = core // 2
        role = core % 2
        ob = np.asarray(res.results[core]["out_own"], np.float32)
        out[b].reshape(32, 128, D)[role::2] = ob.reshape(NSLOT, 128, D)
    return out
